# version 03
# baseline (speedup 1.0000x reference)
import math
import os
from contextlib import ExitStack

import numpy as np
import ml_dtypes

import concourse.bass as bass
import concourse.mybir as mybir
from concourse.bass_utils import run_bass_kernel_spmd

F32 = mybir.dt.float32
BF16 = mybir.dt.bfloat16
F16 = mybir.dt.float16
I8 = mybir.dt.int8
I32 = mybir.dt.int32
AF = mybir.ActivationFunctionType
AL = mybir.AluOpType
AX = mybir.AxisListType

H = 2048
TOK = 2048
PREV = 2048
CH = 128
NCH = TOK // CH
NALL = (TOK + PREV) // CH
KT = H // 128
GAMMA = 0.96875
LN2 = math.log(2.0)

ENGINES = ("sync", "gpsimd", "scalar", "vector", "tensor")
_BLOCK_ATTR = {"sync": "sync", "gpsimd": "gpsimd", "scalar": "scalar",
               "vector": "vector", "tensor": "tensor"}



_TABLES = {}


def _host_tables():
    if _TABLES:
        return _TABLES
    import jax
    import jax.numpy as jnp
    cpu = jax.devices("cpu")[0]
    S = 8192
    with jax.default_device(cpu):
        half = H // 2
        scale_vec = (jnp.arange(0, H, 2, dtype=jnp.float32) + 0.4 * H) / (1.4 * H)
        power = jnp.arange(S, dtype=jnp.float32) / 512
        scale = scale_vec[None, :] ** power[:, None]
        inv_freq = 1.0 / 10000.0 ** (jnp.arange(half, dtype=jnp.float32) / half)
        sinusoid = jnp.arange(S, dtype=jnp.float32)[:, None] * inv_freq[None, :]
        sin, cos = jnp.sin(sinusoid), jnp.cos(sinusoid)
        ssq = np.asarray(sin * scale, dtype=np.float32)
        csq = np.asarray(cos * scale, dtype=np.float32)
        dscale = 1.0 / scale
        ssk = np.asarray(sin * dscale, dtype=np.float32)
        csk = np.asarray(cos * dscale, dtype=np.float32)
        n = jnp.arange(TOK)[:, None]
        m = jnp.arange(TOK)[None, :]
        dmat = jnp.where(n >= m, GAMMA ** jnp.maximum(n - m, 0).astype(jnp.float32), 0.0)
        d_own_T = np.asarray(dmat, dtype=np.float32).T.copy()
        n2 = jnp.arange(TOK)[:, None]
        m2 = jnp.arange(-PREV, 0)[None, :]
        dmat2 = jnp.where(n2 >= m2, GAMMA ** jnp.maximum(n2 - m2, 0).astype(jnp.float32), 0.0)
        d_prev_T = np.asarray(dmat2, dtype=np.float32).T.copy()
    _TABLES.update(dict(ssq=ssq, csq=csq, ssk=ssk, csk=csk,
                        d_own_T=np.ascontiguousarray(d_own_T),
                        d_prev_T=np.ascontiguousarray(d_prev_T),
                        d_zero_T=np.zeros_like(d_prev_T)))
    return _TABLES



class Ctx:

    N_DMA_SEMS = {"sync": 36, "gpsimd": 36}

    def __init__(self, nc, es):
        self.nc = nc
        self.sems = {e: es.enter_context(nc.semaphore("prog_" + e)) for e in ENGINES}
        self.semid = {e: ("e", e) for e in ENGINES}
        self.cnt = {e: 0 for e in ENGINES}
        self.hwm = {e: {} for e in ENGINES}
        self.state = {}
        self.prog = {e: [] for e in ENGINES}
        self.dpool = {}
        self.dcnt = {}
        self.dcons = {}
        self.dnext = {}
        for e, n in self.N_DMA_SEMS.items():
            self.dpool[e] = [es.enter_context(nc.semaphore(f"dma_{e}{i}"))
                             for i in range(n)]
            self.dnext[e] = 0
        self.semobj = {("e", e): self.sems[e] for e in ENGINES}
        for e, pool in self.dpool.items():
            for i, s in enumerate(pool):
                self.semobj[("d", e, i)] = s
                self.dcnt[("d", e, i)] = 0
                self.dcons[("d", e, i)] = True

    @staticmethod
    def _key(t):
        return t if isinstance(t, str) else t.name

    def _want(self, eng, reads, writes, skip_waw_engine=None):
        need = {}
        for t in reads:
            st = self.state.get(self._key(t))
            if st and st["w"]:
                sk, c = st["w"]
                need[sk] = max(need.get(sk, 0), c)
        for t in writes:
            st = self.state.get(self._key(t))
            if st:
                if st["w"] and not (skip_waw_engine and st["w"][0] == ("e", skip_waw_engine)):
                    sk, c = st["w"]
                    need[sk] = max(need.get(sk, 0), c)
                for sk, c in st["r"].items():
                    need[sk] = max(need.get(sk, 0), c)
        out = []
        for sk, c in need.items():
            if self.hwm[eng].get(sk, 0) < c:
                out.append((sk, c))
                self.hwm[eng][sk] = c
                if sk[0] == "d":
                    self.dcons[sk] = True
        return out

    def _emit(self, eng, fn, esems, sem, inc):
        def closure(engobj, fn=fn, esems=esems, sem=sem, inc=inc):
            for s, c in esems:
                engobj.wait_ge(s, c)
            ins = fn(engobj)
            if sem is not None:
                ins.then_inc(sem, inc)
        self.prog[eng].append(closure)

    def op(self, eng, fn, r=(), w=(), dma=False, accum=False):
        skip = eng if accum else None
        edges = self._want(eng, r, w, skip_waw_engine=skip)
        esems = [(self.semobj[sk], c) for sk, c in edges]
        if dma:
            pool = self.dpool[eng]
            i = self.dnext[eng]
            self.dnext[eng] = (i + 1) % len(pool)
            sk = ("d", eng, i)
            if not self.dcons[sk]:
                if self.hwm[eng].get(sk, 0) < self.dcnt[sk]:
                    esems.append((self.semobj[sk], self.dcnt[sk]))
                    self.hwm[eng][sk] = self.dcnt[sk]
            self.dcnt[sk] += 16
            self.dcons[sk] = False
            myc = self.dcnt[sk]
            self._emit(eng, fn, esems, self.semobj[sk], 16)
        else:
            sk = ("e", eng)
            self.cnt[eng] += 1
            myc = self.cnt[eng]
            self._emit(eng, fn, esems, self.sems[eng], 1)
        for t in r:
            st = self.state.setdefault(self._key(t), {"w": None, "r": {}})
            st["r"][sk] = myc
        for t in w:
            st = self.state.setdefault(self._key(t), {"w": None, "r": {}})
            st["w"] = (sk, myc)
            st["r"] = {}
        return sk, myc

    def wait_all(self, eng):
        esems = []
        for f in ENGINES:
            if f == eng:
                continue
            sk = ("e", f)
            c = self.cnt[f]
            if c > self.hwm[eng].get(sk, 0):
                self.hwm[eng][sk] = c
                esems.append((self.semobj[sk], c))
        for sk, c in self.dcnt.items():
            if c > self.hwm[eng].get(sk, 0):
                self.hwm[eng][sk] = c
                self.dcons[sk] = True
                esems.append((self.semobj[sk], c))
        if esems:
            def closure(engobj, esems=esems):
                for s, c in esems:
                    engobj.wait_ge(s, c)
            self.prog[eng].append(closure)

    def barrier(self):
        for e in ENGINES:
            self.wait_all(e)
        self.state = {}

    def flush(self, block):
        for e in ENGINES:
            lst = self.prog[e]
            self.prog[e] = []
            if lst:
                getattr(block, _BLOCK_ATTR[e])(lambda engobj, lst=lst: [f(engobj) for f in lst])


class Split:

    def __init__(self, nc, es, name, total_cols, cols_per_part, dtype):
        cols_per_part = min(cols_per_part, total_cols)
        assert total_cols % cols_per_part == 0
        self.cpp = cols_per_part
        self.t = [es.enter_context(nc.sbuf_tensor(f"{name}_{i}", [128, cols_per_part], dtype))
                  for i in range(total_cols // cols_per_part)]

    def ap(self, c0, w):
        i = c0 // self.cpp
        o = c0 % self.cpp
        assert o + w <= self.cpp, (c0, w, self.cpp)
        return self.t[i][:, o:o + w]

    def key(self, c0):
        return self.t[c0 // self.cpp].name



def _ln_quant(C, consts, x_ap, xkey, t, st, want_h2=False):
    V, S = "vector", "scalar"
    C.op(V, lambda e: e.reduce_sum(out=st["msum"][:, 0:1], in_=x_ap, axis=AX.X),
         r=[xkey], w=[st["msum"]])
    C.op(V, lambda e: e.tensor_scalar(out=st["negm"][:, 0:1], in0=st["msum"][:, 0:1],
                                      scalar1=-1.0 / H, scalar2=None, op0=AL.mult),
         r=[st["msum"]], w=[st["negm"]])
    C.op(V, lambda e: e.tensor_scalar_add(out=t["d"][:], in0=x_ap,
                                          scalar1=st["negm"][:, 0:1]),
         r=[xkey, st["negm"]], w=[t["d"]])
    C.op(S, lambda e: e.activation(out=t["sq"][:], in_=t["d"][:], func=AF.Square),
         r=[t["d"]], w=[t["sq"]])
    C.op(V, lambda e: e.reduce_sum(out=st["vsum"][:, 0:1], in_=t["sq"][:], axis=AX.X),
         r=[t["sq"]], w=[st["vsum"]])
    C.op(S, lambda e: e.activation(out=st["lnv"][:, 0:1], in_=st["vsum"][:, 0:1],
                                   func=AF.Ln, scale=consts["invH"],
                                   bias=consts["epsln"]),
         r=[st["vsum"], "consts"], w=[st["lnv"]])
    C.op(S, lambda e: e.activation(out=st["rstd"][:, 0:1], in_=st["lnv"][:, 0:1],
                                   func=AF.Exp, scale=consts["neghalf"]),
         r=[st["lnv"], "consts"], w=[st["rstd"]])
    C.op(V, lambda e: e.reduce_max(out=st["dmax"][:, 0:1], in_=t["d"][:], axis=AX.X,
                                   apply_absolute_value=True),
         r=[t["d"]], w=[st["dmax"]])
    C.op(V, lambda e: e.tensor_mul(out=st["amax"][:, 0:1], in0=st["dmax"][:, 0:1],
                                   in1=st["rstd"][:, 0:1]),
         r=[st["dmax"], st["rstd"]], w=[st["amax"]])
    C.op(V, lambda e: e.tensor_scalar_max(out=st["am2"][:, 0:1], in0=st["amax"][:, 0:1],
                                          scalar1=1e-7),
         r=[st["amax"]], w=[st["am2"]])
    C.op(V, lambda e: e.reciprocal(out=st["rec"][:, 0:1], in_=st["am2"][:, 0:1]),
         r=[st["am2"]], w=[st["rec"]])
    C.op(V, lambda e: e.tensor_scalar(out=st["iscale"][:, 0:1], in0=st["am2"][:, 0:1],
                                      scalar1=1.0 / 127.0, scalar2=None, op0=AL.mult),
         r=[st["am2"]], w=[st["iscale"]])
    C.op(V, lambda e: e.tensor_scalar(out=st["rec127"][:, 0:1], in0=st["rec"][:, 0:1],
                                      scalar1=127.0, scalar2=None, op0=AL.mult),
         r=[st["rec"]], w=[st["rec127"]])
    C.op(V, lambda e: e.tensor_mul(out=st["fac"][:, 0:1], in0=st["rec127"][:, 0:1],
                                   in1=st["rstd"][:, 0:1]),
         r=[st["rec127"], st["rstd"]], w=[st["fac"]])
    C.op(V, lambda e: e.tensor_scalar_mul(out=t["y"][:], in0=t["d"][:],
                                          scalar1=st["fac"][:, 0:1]),
         r=[t["d"], st["fac"]], w=[t["y"]])
    if want_h2:
        C.op(V, lambda e: e.tensor_scalar_mul(out=t["h2"][:], in0=t["d"][:],
                                              scalar1=st["rstd"][:, 0:1]),
             r=[t["d"], st["rstd"]], w=[t["h2"]])
    C.op(V, lambda e: e.tensor_copy(out=t["ri"][:], in_=t["y"][:]),
         r=[t["y"]], w=[t["ri"]])
    C.op(V, lambda e: e.tensor_copy(out=t["rf"][:], in_=t["ri"][:]),
         r=[t["ri"]], w=[t["rf"]])
    C.op(V, lambda e: e.tensor_sub(out=t["dd"][:], in0=t["y"][:], in1=t["rf"][:]),
         r=[t["y"], t["rf"]], w=[t["dd"]])
    C.op(V, lambda e: e.tensor_scalar(out=t["u1"][:], in0=t["dd"][:], scalar1=0.5,
                                      scalar2=None, op0=AL.is_gt),
         r=[t["dd"]], w=[t["u1"]])
    C.op(V, lambda e: e.tensor_scalar(out=t["u2"][:], in0=t["dd"][:], scalar1=-0.5,
                                      scalar2=None, op0=AL.is_lt),
         r=[t["dd"]], w=[t["u2"]])
    C.op(V, lambda e: e.scalar_tensor_tensor(out=t["tq"][:], in0=t["u1"][:], scalar=1.0,
                                             in1=t["rf"][:], op0=AL.mult, op1=AL.add),
         r=[t["u1"], t["rf"]], w=[t["tq"]])
    C.op(V, lambda e: e.tensor_sub(out=t["q"][:], in0=t["tq"][:], in1=t["u2"][:]),
         r=[t["tq"], t["u2"]], w=[t["q"]])


def _f16_round_add(C, consts, t, vkey, xin_ap, xin_key, xout_ap, xout_key):
    V, S = "vector", "scalar"
    v = t[vkey]
    i32a, i32b = t["i32a"], t["i32b"]
    bca = i32a[:].bitcast(F32)
    bcb = i32b[:].bitcast(F32)
    C.op(S, lambda e: e.activation(out=t["fa"][:], in_=v[:], func=AF.Abs),
         r=[v], w=[t["fa"]])
    C.op(V, lambda e: e.tensor_scalar_max(out=t["fa2"][:], in0=t["fa"][:], scalar1=1e-30),
         r=[t["fa"]], w=[t["fa2"]])
    C.op(S, lambda e: e.activation(out=t["flog"][:], in_=t["fa2"][:], func=AF.Ln),
         r=[t["fa2"]], w=[t["flog"]])
    C.op(V, lambda e: e.tensor_scalar(out=t["fu"][:], in0=t["flog"][:],
                                      scalar1=1.0 / LN2, scalar2=14.0,
                                      op0=AL.mult, op1=AL.add),
         r=[t["flog"]], w=[t["fu"]])
    C.op(V, lambda e: e.tensor_scalar(out=t["fuc"][:], in0=t["fu"][:],
                                      scalar1=0.0, scalar2=22.9,
                                      op0=AL.max, op1=AL.min),
         r=[t["fu"]], w=[t["fuc"]])
    C.op(V, lambda e: e.tensor_copy(out=i32a[:], in_=t["fuc"][:]),
         r=[t["fuc"]], w=[i32a])
    C.op(V, lambda e: e.tensor_scalar(out=i32b[:], in0=i32a[:],
                                      scalar1=-8388608, scalar2=151 * 8388608,
                                      op0=AL.mult, op1=AL.add),
         r=[i32a], w=[i32b])
    C.op(V, lambda e: e.tensor_scalar(out=i32a[:], in0=i32a[:],
                                      scalar1=8388608, scalar2=102 * 8388608,
                                      op0=AL.mult, op1=AL.add),
         r=[i32a], w=[i32a])
    C.op(V, lambda e: e.tensor_mul(out=t["fum"][:], in0=v[:], in1=bcb),
         r=[v, i32b], w=[t["fum"]])
    C.op(V, lambda e: e.tensor_copy(out=i32b[:], in_=t["fum"][:]),
         r=[t["fum"]], w=[i32b])
    C.op(V, lambda e: e.tensor_copy(out=t["frf"][:], in_=i32b[:]),
         r=[i32b], w=[t["frf"]])
    C.op(V, lambda e: e.tensor_sub(out=t["fdd"][:], in0=t["fum"][:], in1=t["frf"][:]),
         r=[t["fum"], t["frf"]], w=[t["fdd"]])
    C.op(V, lambda e: e.tensor_scalar(out=t["fu1"][:], in0=t["fdd"][:], scalar1=0.5,
                                      scalar2=None, op0=AL.is_gt),
         r=[t["fdd"]], w=[t["fu1"]])
    C.op(V, lambda e: e.tensor_scalar(out=t["fu2"][:], in0=t["fdd"][:], scalar1=-0.5,
                                      scalar2=None, op0=AL.is_lt),
         r=[t["fdd"]], w=[t["fu2"]])
    C.op(V, lambda e: e.scalar_tensor_tensor(out=t["fr1"][:], in0=t["fu1"][:], scalar=1.0,
                                             in1=t["frf"][:], op0=AL.mult, op1=AL.add),
         r=[t["fu1"], t["frf"]], w=[t["fr1"]])
    C.op(V, lambda e: e.tensor_sub(out=t["fr2"][:], in0=t["fr1"][:], in1=t["fu2"][:]),
         r=[t["fr1"], t["fu2"]], w=[t["fr2"]])
    C.op(V, lambda e: e.tensor_mul(out=t["f0"][:], in0=t["fr2"][:], in1=bca),
         r=[t["fr2"], i32a], w=[t["f0"]])
    C.op(V, lambda e: e.tensor_add(out=xout_ap, in0=xin_ap, in1=t["f0"][:]),
         r=[xin_key, t["f0"]], w=[xout_key])


def _xpos(C, t, src_key, src_ap, cs_ap, ss_ap, tab_keys, out_ap, out_key):
    V = "vector"
    csk, ssk = tab_keys
    xe = src_ap[:, 0:H:2]
    xo = src_ap[:, 1:H:2]
    C.op(V, lambda e: e.tensor_mul(out=t["xp1"][:], in0=xe, in1=cs_ap),
         r=[src_key, csk], w=[t["xp1"]])
    C.op(V, lambda e: e.tensor_mul(out=t["xp2"][:], in0=xo, in1=ss_ap),
         r=[src_key, ssk], w=[t["xp2"]])
    C.op(V, lambda e: e.tensor_sub(out=out_ap[:, 0:H:2], in0=t["xp1"][:], in1=t["xp2"][:]),
         r=[t["xp1"], t["xp2"]], w=[out_key])
    C.op(V, lambda e: e.tensor_mul(out=t["xp1"][:], in0=xo, in1=cs_ap),
         r=[src_key, csk, out_key], w=[t["xp1"]])
    C.op(V, lambda e: e.tensor_mul(out=t["xp2"][:], in0=xe, in1=ss_ap),
         r=[src_key, ssk], w=[t["xp2"]])
    C.op(V, lambda e: e.tensor_add(out=out_ap[:, 1:H:2], in0=t["xp1"][:], in1=t["xp2"][:]),
         r=[t["xp1"], t["xp2"]], w=[out_key])


def _xpos_psum(C, t, ps_key, ps_ap, cs_ap, ss_ap, out_ap, out_key):
    V = "vector"
    xe = ps_ap[:, 0:H:2]
    xo = ps_ap[:, 1:H:2]
    s1 = t["xp2a"]
    s2 = t["xp2b"]
    sk = t["xp2key"]
    C.op(V, lambda e: e.tensor_mul(out=s1, in0=xo, in1=ss_ap),
         r=[ps_key, t["ss"]], w=[sk])
    C.op(V, lambda e: e.tensor_mul(out=s2, in0=xe, in1=ss_ap),
         r=[ps_key, t["ss"], sk], w=[sk])
    C.op(V, lambda e: e.tensor_mul(out=xe, in0=xe, in1=cs_ap),
         r=[ps_key, t["cs"], sk], w=[out_key, ps_key])
    C.op(V, lambda e: e.tensor_sub(out=xe, in0=xe, in1=s1),
         r=[out_key, sk], w=[out_key, ps_key])
    C.op(V, lambda e: e.tensor_mul(out=xo, in0=xo, in1=cs_ap),
         r=[out_key, t["cs"]], w=[out_key, ps_key])
    C.op(V, lambda e: e.tensor_add(out=xo, in0=xo, in1=s2),
         r=[out_key, sk], w=[out_key, ps_key])


def _split_hilo(C, t, src_key, src_ap, hi_ap, hi_key, lo_ap=None, lo_key=None):
    C.op("scalar", lambda e: e.activation(out=hi_ap, in_=src_ap, func=AF.Copy),
         r=[src_key], w=[hi_key])
    if lo_ap is not None:
        C.op("vector", lambda e: e.tensor_sub(out=lo_ap, in0=src_ap, in1=hi_ap),
             r=[src_key, hi_key], w=[lo_key])


def _transpose16(C, t, ident, src_ap_fn, src_key, psum, dst_ap_fn, dst_key,
                 n_tiles=KT):
    T, S = "tensor", "scalar"
    for i in range(n_tiles):
        half = i % 2
        pap = psum[:, half * 1024:half * 1024 + 128]
        C.op(T, lambda e, i=i, pap=pap: e.transpose(pap, src_ap_fn(i), ident[:]),
             r=[src_key, "identity"], w=[psum.name + f"_h{half}"])
        C.op(S, lambda e, i=i, pap=pap: e.activation(out=dst_ap_fn(i), in_=pap,
                                                     func=AF.Copy),
             r=[psum.name + f"_h{half}"], w=[dst_key])



def build_program(debug=(), phases=(1, 2, 3, 4)):
    nc = bass.Bass("TRN2", target_bir_lowering=False)

    x0both = nc.dram_tensor("x0both", [PREV + TOK, H], F32, kind="ExternalInput").ap()
    wq = nc.dram_tensor("wq", [H, H], F32, kind="ExternalInput").ap()
    wk = nc.dram_tensor("wk", [H, H], F32, kind="ExternalInput").ap()
    wv = nc.dram_tensor("wv", [H, H], F32, kind="ExternalInput").ap()
    wf8 = nc.dram_tensor("wf8", [H, H], I8, kind="ExternalInput").ap()
    wsrow = nc.dram_tensor("wsrow", [128, H], F32, kind="ExternalInput").ap()
    biasrow = nc.dram_tensor("biasrow", [128, H], F32, kind="ExternalInput").ap()
    ident_in = nc.dram_tensor("ident", [128, 128], BF16, kind="ExternalInput").ap()
    csq_in = nc.dram_tensor("csq", [TOK, H // 2], F32, kind="ExternalInput").ap()
    ssq_in = nc.dram_tensor("ssq", [TOK, H // 2], F32, kind="ExternalInput").ap()
    csk_in = nc.dram_tensor("csk", [PREV + TOK, H // 2], F32, kind="ExternalInput").ap()
    ssk_in = nc.dram_tensor("ssk", [PREV + TOK, H // 2], F32, kind="ExternalInput").ap()
    down_in = nc.dram_tensor("d_own_T", [TOK, TOK], F32, kind="ExternalInput").ap()
    dprev_in = nc.dram_tensor("d_prev_T", [PREV, TOK], F32, kind="ExternalInput").ap()

    out = nc.dram_tensor("out", [TOK, H], F32, kind="ExternalOutput").ap()

    x1_hbm = nc.dram_tensor("x1_hbm", [PREV + TOK, H], F32, kind="Internal").ap()
    h2t_hi = nc.dram_tensor("h2t_hi", [NALL, 128, H], BF16, kind="Internal").ap()
    h2t_lo = nc.dram_tensor("h2t_lo", [NCH, 128, H], BF16, kind="Internal").ap()
    vhi_hbm = nc.dram_tensor("vhi_hbm", [NALL, 128, H], BF16, kind="Internal").ap()
    vlo_hbm = nc.dram_tensor("vlo_hbm", [NCH, 128, H], BF16, kind="Internal").ap()
    kfm_hi = nc.dram_tensor("kfm_hi", [KT, 128, PREV + TOK], BF16, kind="Internal").ap()
    kfm_lo = nc.dram_tensor("kfm_lo", [KT, 128, TOK], BF16, kind="Internal").ap()
    qfm_hi = nc.dram_tensor("qfm_hi", [KT, 128, TOK], BF16, kind="Internal").ap()
    qfm_lo = nc.dram_tensor("qfm_lo", [KT, 128, TOK], BF16, kind="Internal").ap()
    oprev_hbm = nc.dram_tensor("oprev_hbm", [TOK, H], F32, kind="Internal").ap()
    oown_hbm = nc.dram_tensor("oown_hbm", [TOK, H], F32, kind="Internal").ap()

    dbg = {}
    for name in debug:
        shapes = {"x1": [PREV + TOK, H], "h2t_hi": [NALL, 128, H],
                  "vhi": [NALL, 128, H], "kfm_hi": [KT, 128, PREV + TOK],
                  "qfm_hi": [KT, 128, TOK], "oprev": [TOK, H], "oown": [TOK, H]}
        dbg[name] = nc.dram_tensor("dbg_" + name, shapes[name],
                                   BF16 if name in ("h2t_hi", "vhi", "kfm_hi", "qfm_hi") else F32,
                                   kind="ExternalOutput").ap()

    es_outer = ExitStack()
    with es_outer:
        C = Ctx(nc, es_outer)
        V, S, T, DM, G = "vector", "scalar", "tensor", "sync", "gpsimd"

        consts = {}
        const_vals = dict(invH=1.0 / H, epsln=1e-5, neghalf=-0.5,
                          negln2=-LN2, b24ln2=24.0 * LN2, posln2=LN2,
                          bm24h=-24.0 * LN2 + math.log(0.5))
        const_tile = es_outer.enter_context(nc.sbuf_tensor("consts", [128, len(const_vals)], F32))
        for i, (k, val) in enumerate(const_vals.items()):
            consts[k] = const_tile[:, i:i + 1]

        ident = es_outer.enter_context(nc.sbuf_tensor("identity", [128, 128], BF16))
        wsr = es_outer.enter_context(nc.sbuf_tensor("wsr", [128, H], F32))
        bir = es_outer.enter_context(nc.sbuf_tensor("bir", [128, H], F32))

        with nc.Block() as block:
            for i, (k, val) in enumerate(const_vals.items()):
                C.op(V, lambda e, i=i, val=val: e.memset(const_tile[:, i:i + 1], float(val)),
                     w=["consts"])
            C.op(DM, lambda e: e.dma_start(out=ident[:], in_=ident_in), w=["identity"], dma=True)
            C.op(DM, lambda e: e.dma_start(out=wsr[:], in_=wsrow), w=["wsr"], dma=True)
            C.op(DM, lambda e: e.dma_start(out=bir[:], in_=biasrow), w=["bir"], dma=True)

            with ExitStack() as es:
                sb = lambda name, shape, dt: es.enter_context(nc.sbuf_tensor(name, shape, dt))
                wft = Split(nc, es, "wft", KT * H, 8 * H, BF16)
                wstage = [sb(f"wstage{s}", [128, H], I8) for s in range(2)]
                wfbf = sb("wfbf", [128, H], BF16)
                ps_tr = es.enter_context(nc.psum_tensor("ps_tr", [128, 2048], BF16))
                ps_ffn = es.enter_context(nc.psum_tensor("ps_ffn", [128, H], F32))

                for ntile in range(KT):
                    slot = ntile % 2
                    C.op(DM, lambda e, ntile=ntile, slot=slot: e.dma_start(
                        out=wstage[slot][:], in_=wf8[ntile * 128:(ntile + 1) * 128, :]),
                        w=[wstage[slot]], dma=True)
                    C.op(V, lambda e, slot=slot: e.tensor_copy(out=wfbf[:], in_=wstage[slot][:]),
                         r=[wstage[slot]], w=[wfbf])
                    _transpose16(C, None, ident,
                                 lambda kt, ntile=ntile: wfbf[:, kt * 128:(kt + 1) * 128],
                                 wfbf, ps_tr,
                                 lambda kt, ntile=ntile: wft.ap(kt * H + ntile * 128, 128),
                                 "wft")
                scr = [sb(f"p1_s{i}", [128, H], F32) for i in range(5)]
                i32a = sb("p1_i32a", [128, H], I32)
                i32b = sb("p1_i32b", [128, H], I32)
                qbf = wfbf
                u1b = sb("p1_u1", [128, H], BF16)
                u2b = sb("p1_u2", [128, H], BF16)
                h2hi_t = sb("p1_h2hi", [128, H], BF16)
                h2lo_t = sb("p1_h2lo", [128, H], BF16)
                h2thi_t = sb("p1_h2thi", [128, H], BF16)
                h2tlo_t = sb("p1_h2tlo", [128, H], BF16)
                xts = [sb(f"p1_xt{s}", [128, H], F32) for s in range(2)]
                x1cs = [sb(f"p1_x1c{s}", [128, H], F32) for s in range(2)]
                qts = [sb("p1_qt0", [128, H], BF16)] * 2
                stt = {}
                for nm in ("msum", "negm", "vsum", "lnv", "rstd", "dmax",
                           "amax", "am2", "rec", "rec127", "fac", "iscale"):
                    stt[nm] = sb(f"p1_st_{nm}", [128, 1], F32)
                tiles = []
                for s in range(2):
                    t = dict(d=scr[0], sq=scr[1], y=scr[2], dd=scr[1], rf=scr[3],
                             tq=scr[2], v0=scr[0], v1=scr[1], vv=scr[2],
                             fa=scr[0], fa2=scr[1], flog=scr[0], fu=scr[1],
                             fuc=scr[0],
                             fum=scr[4], frf=scr[0], fdd=scr[1], fr1=scr[3],
                             fr2=scr[0], f0=scr[1], h2=scr[2],
                             ri=i32a, i32a=i32a, i32b=i32b,
                             q=qbf, u1=u1b, u2=u2b, fu1=u1b, fu2=u2b,
                             h2hi=h2hi_t, h2lo=h2lo_t, h2thi=h2thi_t,
                             h2tlo=h2tlo_t,
                             xt=xts[s], x1c=x1cs[s], qt=qts[s])
                    tiles.append((t, stt))

                for c in range(NALL):
                    t, stt = tiles[c % 2]
                    own = c >= NCH
                    C.op(DM, lambda e, c=c, t=t: e.dma_start(
                        out=t["xt"][:], in_=x0both[c * CH:(c + 1) * CH, :]),
                        w=[t["xt"]], dma=True)
                    _ln_quant(C, consts, t["xt"][:], t["xt"], t, stt, want_h2=False)
                    _transpose16(C, t, ident,
                                 lambda i, t=t: t["q"][:, i * 128:(i + 1) * 128],
                                 t["q"], ps_tr,
                                 lambda i, t=t: t["qt"][:, i * 128:(i + 1) * 128],
                                 t["qt"])
                    for ns in range(4):
                        for kt in range(KT):
                            C.op(T, lambda e, t=t, kt=kt, ns=ns: e.matmul(
                                ps_ffn[:, ns * 512:(ns + 1) * 512],
                                t["qt"][:, kt * 128:(kt + 1) * 128],
                                wft.ap(kt * H + ns * 512, 512),
                                start=(kt == 0), stop=(kt == KT - 1)),
                                r=[t["qt"], "wft"], w=[f"ps_ffn{ns}"], accum=(kt > 0))
                        C.op(S, lambda e, t=t, stt=stt, ns=ns: e.activation(
                            out=t["v0"][:, ns * 512:(ns + 1) * 512],
                            in_=ps_ffn[:, ns * 512:(ns + 1) * 512], func=AF.Copy,
                            scale=stt["iscale"][:, 0:1]),
                            r=[f"ps_ffn{ns}", stt["iscale"]], w=[t["v0"]])
                    C.op(V, lambda e, t=t: e.tensor_mul(out=t["v1"][:], in0=t["v0"][:],
                                                        in1=wsr[:]),
                         r=[t["v0"], "wsr"], w=[t["v1"]])
                    C.op(V, lambda e, t=t: e.tensor_add(out=t["vv"][:], in0=t["v1"][:],
                                                        in1=bir[:]),
                         r=[t["v1"], "bir"], w=[t["vv"]])
                    _f16_round_add(C, consts, t, "vv", t["xt"][:], t["xt"],
                                   t["x1c"][:], t["x1c"])
                    C.op(DM, lambda e, c=c, t=t: e.dma_start(
                        out=x1_hbm[c * CH:(c + 1) * CH, :], in_=t["x1c"][:]),
                        r=[t["x1c"]], w=["x1_hbm"], dma=True)
                    _ln2 = _ln_simple(C, consts, t, stt, t["x1c"][:], t["x1c"])
                    _split_hilo(C, t, t["h2"], t["h2"][:], t["h2hi"][:], t["h2hi"],
                                t["h2lo"][:] if own else None,
                                t["h2lo"] if own else None)
                    _transpose16(C, t, ident,
                                 lambda i, t=t: t["h2hi"][:, i * 128:(i + 1) * 128],
                                 t["h2hi"], ps_tr,
                                 lambda i, t=t: t["h2thi"][:, i * 128:(i + 1) * 128],
                                 t["h2thi"])
                    C.op(G, lambda e, c=c, t=t: e.dma_start(
                        out=h2t_hi[c], in_=t["h2thi"][:]),
                        r=[t["h2thi"]], w=["h2t_hi"], dma=True)
                    if own:
                        _transpose16(C, t, ident,
                                     lambda i, t=t: t["h2lo"][:, i * 128:(i + 1) * 128],
                                     t["h2lo"], ps_tr,
                                     lambda i, t=t: t["h2tlo"][:, i * 128:(i + 1) * 128],
                                     t["h2tlo"])
                        C.op(G, lambda e, c=c, t=t: e.dma_start(
                            out=h2t_lo[c - NCH], in_=t["h2tlo"][:]),
                            r=[t["h2tlo"]], w=["h2t_lo"], dma=True)
                C.barrier()
                C.flush(block)

            if 2 in phases:
                _proj_phase(C, nc, block, dict(wq=wq, wk=wk, wv=wv), h2t_hi, h2t_lo,
                            vhi_hbm, vlo_hbm, kfm_hi, kfm_lo, qfm_hi, qfm_lo,
                            csq_in, ssq_in, csk_in, ssk_in, ident, consts)

            if 3 in phases:
                _attn_phase(C, nc, block, vhi_hbm, vlo_hbm, kfm_hi, kfm_lo,
                            qfm_hi, qfm_lo, down_in, dprev_in, oprev_hbm, oown_hbm)

            if 4 in phases:
                _ffn_post_phase(C, nc, block, wf8, x1_hbm, oprev_hbm, oown_hbm,
                                out, ident, consts, wsr, bir)
            else:
                with ExitStack() as es:
                    stage = es.enter_context(nc.sbuf_tensor("outstage", [128, H], F32))
                    for rr in range(0, NCH * CH, 128):
                        C.op("sync", lambda e, rr=rr: e.dma_start(
                            out=stage[:], in_=x1_hbm[NCH * CH + rr:NCH * CH + rr + 128, :]),
                            r=["x1_hbm"], w=["outstage"], dma=True)
                        C.op("sync", lambda e, rr=rr: e.dma_start(
                            out=out[rr:rr + 128, :], in_=stage[:]),
                            r=["outstage"], w=["out"], dma=True)
                    C.barrier()
                    C.flush(block)

            if dbg:
                with ExitStack() as es:
                    stage = es.enter_context(nc.sbuf_tensor("dbgstage", [128, H], F32))
                    srcs = {"x1": (x1_hbm, "x1_hbm", NALL * CH),
                            "oprev": (oprev_hbm, "oprev_hbm", NCH * CH),
                            "oown": (oown_hbm, "oown_hbm", NCH * CH)}
                    for name, dst in dbg.items():
                        srcap, srckey, n_rows = srcs[name]
                        for rr in range(0, n_rows, 128):
                            C.op(DM, lambda e, srcap=srcap, rr=rr: e.dma_start(
                                out=stage[:], in_=srcap[rr:rr + 128, :]),
                                r=[srckey], w=["dbgstage"], dma=True)
                            C.op(DM, lambda e, dst=dst, rr=rr: e.dma_start(
                                out=dst[rr:rr + 128, :], in_=stage[:]),
                                r=["dbgstage"], w=["dbg"], dma=True)
                    C.barrier()
                C.flush(block)

            C.barrier()
            C.flush(block)
    return nc


def _ln_simple(C, consts, t, stt, x_ap, xkey):
    V, S = "vector", "scalar"
    C.op(V, lambda e: e.reduce_sum(out=stt["msum"][:, 0:1], in_=x_ap, axis=AX.X),
         r=[xkey], w=[stt["msum"]])
    C.op(V, lambda e: e.tensor_scalar(out=stt["negm"][:, 0:1], in0=stt["msum"][:, 0:1],
                                      scalar1=-1.0 / H, scalar2=None, op0=AL.mult),
         r=[stt["msum"]], w=[stt["negm"]])
    C.op(V, lambda e: e.tensor_scalar_add(out=t["d"][:], in0=x_ap,
                                          scalar1=stt["negm"][:, 0:1]),
         r=[xkey, stt["negm"]], w=[t["d"]])
    C.op(S, lambda e: e.activation(out=t["sq"][:], in_=t["d"][:], func=AF.Square),
         r=[t["d"]], w=[t["sq"]])
    C.op(V, lambda e: e.reduce_sum(out=stt["vsum"][:, 0:1], in_=t["sq"][:], axis=AX.X),
         r=[t["sq"]], w=[stt["vsum"]])
    C.op(S, lambda e: e.activation(out=stt["lnv"][:, 0:1], in_=stt["vsum"][:, 0:1],
                                   func=AF.Ln, scale=consts["invH"],
                                   bias=consts["epsln"]),
         r=[stt["vsum"], "consts"], w=[stt["lnv"]])
    C.op(S, lambda e: e.activation(out=stt["rstd"][:, 0:1], in_=stt["lnv"][:, 0:1],
                                   func=AF.Exp, scale=consts["neghalf"]),
         r=[stt["lnv"], "consts"], w=[stt["rstd"]])
    C.op(V, lambda e: e.tensor_scalar_mul(out=t["h2"][:], in0=t["d"][:],
                                           scalar1=stt["rstd"][:, 0:1]),
         r=[t["d"], stt["rstd"]], w=[t["h2"]])


def _proj_phase(C, nc, block, ws, h2t_hi, h2t_lo, vhi_hbm, vlo_hbm,
                kfm_hi, kfm_lo, qfm_hi, qfm_lo, csq_in, ssq_in, csk_in, ssk_in,
                ident, consts):
    V, S, T, DM, G = "vector", "scalar", "tensor", "sync", "gpsimd"
    for wname in ("wv", "wk", "wq"):
        wrap = ws[wname]
        own_only = wname == "wq"
        nch = NCH if own_only else NALL
        with ExitStack() as es:
            sb = lambda name, shape, dt: es.enter_context(
                nc.sbuf_tensor(f"{wname}_{name}", shape, dt))
            whi = Split(nc, es, wname + "_whi", KT * H, 8 * H, BF16)
            wlo = Split(nc, es, wname + "_wlo", KT * H, 8 * H, BF16)
            wstage = [sb("wstage0", [128, H], F32)] * 2
            ps_tr = es.enter_context(nc.psum_tensor(f"{wname}_ps_tr", [128, 2048], BF16))
            ps_mm = es.enter_context(nc.psum_tensor(f"{wname}_ps_mm", [128, H], F32))
            for kt in range(KT):
                slot = kt % 2
                C.op(DM, lambda e, kt=kt, slot=slot: e.dma_start(
                    out=wstage[slot][:], in_=wrap[kt * 128:(kt + 1) * 128, :]),
                    w=[wstage[slot]], dma=True)
                C.op(S, lambda e, kt=kt, slot=slot: e.activation(
                    out=whi.ap(kt * H, H), in_=wstage[slot][:], func=AF.Copy),
                    r=[wstage[slot]], w=["whi" + wname])
                C.op(V, lambda e, kt=kt, slot=slot: e.tensor_sub(
                    out=wlo.ap(kt * H, H), in0=wstage[slot][:],
                    in1=whi.ap(kt * H, H)),
                    r=[wstage[slot], "whi" + wname], w=["wlo" + wname])
            t0 = {}
            t0["hthi"] = sb("hthi0", [128, H], BF16)
            t0["htlo"] = sb("htlo0", [128, H], BF16)
            t0["pf"] = sb("pf0", [128, H], F32)
            t0["xp2a"] = wstage[0][:, 0:H // 2]
            t0["xp2b"] = wstage[0][:, H // 2:H]
            t0["xp2key"] = wstage[0]
            t0["cs"] = sb("cs0", [128, H // 2], F32)
            t0["ss"] = sb("ss0", [128, H // 2], F32)
            t0["phi"] = [sb("phi0", [128, H], BF16), sb("phi1", [128, H], BF16)]
            t0["plo"] = [sb("plo0", [128, H], BF16), sb("plo1", [128, H], BF16)]
            tiles = [t0, t0]
            if wname in ("wk", "wq"):
                fmst_hi = sb("fmst_hi", [128, KT * 128], BF16)
                fmst_lo = sb("fmst_lo", [128, KT * 128], BF16)

            pending = []
            for ci in range(nch):
                c = ci + (NCH if own_only else 0)
                own = c >= NCH
                t = tiles[ci % 2]
                C.op(DM, lambda e, c=c, t=t: e.dma_start(out=t["hthi"][:], in_=h2t_hi[c]),
                     w=[t["hthi"]], dma=True)
                if own:
                    C.op(DM, lambda e, c=c, t=t: e.dma_start(out=t["htlo"][:],
                                                             in_=h2t_lo[c - NCH]),
                         w=[t["htlo"]], dma=True)
                for kt in range(KT):
                    first = kt == 0
                    last = kt == KT - 1
                    hi_stop = last and not own
                    for ns in range(4):
                        C.op(T, lambda e, t=t, kt=kt, ns=ns, first=first, hi_stop=hi_stop: e.matmul(
                            ps_mm[:, ns * 512:(ns + 1) * 512],
                            t["hthi"][:, kt * 128:(kt + 1) * 128],
                            whi.ap(kt * H + ns * 512, 512),
                            start=first, stop=hi_stop),
                            r=[t["hthi"], "whi" + wname], w=["psmm" + wname],
                            accum=not first)
                    if own:
                        for ns in range(4):
                            C.op(T, lambda e, t=t, kt=kt, ns=ns: e.matmul(
                                ps_mm[:, ns * 512:(ns + 1) * 512],
                                t["hthi"][:, kt * 128:(kt + 1) * 128],
                                wlo.ap(kt * H + ns * 512, 512),
                                start=False, stop=False),
                                r=[t["hthi"], "wlo" + wname], w=["psmm" + wname],
                                accum=True)
                        for ns in range(4):
                            C.op(T, lambda e, t=t, kt=kt, ns=ns, last=last: e.matmul(
                                ps_mm[:, ns * 512:(ns + 1) * 512],
                                t["htlo"][:, kt * 128:(kt + 1) * 128],
                                whi.ap(kt * H + ns * 512, 512),
                                start=False, stop=last),
                                r=[t["htlo"], "whi" + wname], w=["psmm" + wname],
                                accum=True)
                C.op(S, lambda e, t=t: e.activation(out=t["pf"][:], in_=ps_mm[:],
                                                    func=AF.Copy),
                     r=["psmm" + wname], w=[t["pf"]])
                if pending:
                    pending.pop(0)()
                phi = t["phi"][ci % 2]
                plo = t["plo"][ci % 2]
                if wname == "wv":
                    _split_hilo(C, t, t["pf"], t["pf"][:], phi[:], phi,
                                plo[:] if own else None,
                                plo if own else None)
                    C.op(G, lambda e, c=c, phi=phi: e.dma_start(out=vhi_hbm[c], in_=phi[:]),
                         r=[phi], w=["vhi_hbm"], dma=True)
                    if own:
                        C.op(G, lambda e, c=c, plo=plo: e.dma_start(out=vlo_hbm[c - NCH],
                                                                    in_=plo[:]),
                             r=[plo], w=["vlo_hbm"], dma=True)
                else:
                    if wname == "wk":
                        cs_src, ss_src = csk_in, ssk_in
                        row0 = c * CH
                    else:
                        cs_src, ss_src = csq_in, ssq_in
                        row0 = (c - NCH) * CH
                    C.op(DM, lambda e, t=t, cs_src=cs_src, row0=row0: e.dma_start(
                        out=t["cs"][:], in_=cs_src[row0:row0 + CH, :]),
                        w=[t["cs"]], dma=True)
                    C.op(DM, lambda e, t=t, ss_src=ss_src, row0=row0: e.dma_start(
                        out=t["ss"][:], in_=ss_src[row0:row0 + CH, :]),
                        w=[t["ss"]], dma=True)
                    _xpos_psum(C, t, t["pf"], t["pf"][:], t["cs"][:], t["ss"][:],
                               t["pf"][:], t["pf"])
                    _split_hilo(C, t, t["pf"], t["pf"][:], phi[:], phi,
                                plo[:] if own else None,
                                plo if own else None)

                    def _deferred(t=t, phi=phi, plo=plo, own=own, ci=ci):
                        _transpose16(C, t, ident,
                                     lambda i, phi=phi: phi[:, i * 128:(i + 1) * 128],
                                     phi, ps_tr,
                                     lambda i: fmst_hi[:, i * 128:(i + 1) * 128],
                                     "fmst_hi")
                        if own:
                            _transpose16(C, t, ident,
                                         lambda i, plo=plo: plo[:, i * 128:(i + 1) * 128],
                                         plo, ps_tr,
                                         lambda i: fmst_lo[:, i * 128:(i + 1) * 128],
                                         "fmst_lo")
                        g0 = ci * 128
                        dsts = (kfm_hi, kfm_lo) if wname == "wk" else (qfm_hi, qfm_lo)
                        for ft in range(KT):
                            C.op(G, lambda e, ft=ft, g0=g0, dsts=dsts: e.dma_start(
                                out=dsts[0][ft, :, g0:g0 + 128],
                                in_=fmst_hi[:, ft * 128:(ft + 1) * 128]),
                                r=["fmst_hi"], w=["kfmq_hi"], dma=True)
                            if own:
                                lo_g0 = g0 - (0 if own_only else NCH * CH)
                                C.op(G, lambda e, ft=ft, lo_g0=lo_g0, dsts=dsts: e.dma_start(
                                    out=dsts[1][ft, :, lo_g0:lo_g0 + 128],
                                    in_=fmst_lo[:, ft * 128:(ft + 1) * 128]),
                                    r=["fmst_lo"], w=["kfmq_lo"], dma=True)
                    pending.append(_deferred)
            for fn in pending:
                fn()
            pending = []
            C.barrier()
            C.flush(block)


def _attn_phase(C, nc, block, vhi_hbm, vlo_hbm, kfm_hi, kfm_lo, qfm_hi, qfm_lo,
                down_in, dprev_in, oprev_hbm, oown_hbm):
    V, S, T, DM, G = "vector", "scalar", "tensor", "sync", "gpsimd"
    with ExitStack() as es:
        sb = lambda name, shape, dt: es.enter_context(nc.sbuf_tensor("p3a_" + name, shape, dt))
        vprev = Split(nc, es, "p3a_vprev", NCH * H, 8 * H, BF16)
        qhi = sb("qhi", [128, KT * 512], BF16)
        ktile = [sb(f"ktile{s}", [128, KT * 128], BF16) for s in range(3)]
        atile = sb("atile", [128, 16 * 512], BF16)
        dtile = [sb(f"dtile{s}", [128, 512], F32) for s in range(2)]
        oev = [sb(f"oev{s}", [128, H], F32) for s in range(2)]
        ps_sc = es.enter_context(nc.psum_tensor("p3a_ps_sc", [128, 1024], F32))
        ps_out = es.enter_context(nc.psum_tensor("p3a_ps_out", [128, H], F32))
        for c in range(NCH):
            C.op(DM, lambda e, c=c: e.dma_start(out=vprev.ap(c * H, H),
                                                in_=vhi_hbm[c]),
                 w=[vprev.key(c * H)], dma=True)
        for qi in range(NCH * CH // 512):
            i0 = qi * 512
            for ft in range(KT):
                C.op(DM, lambda e, ft=ft, i0=i0: e.dma_start(
                    out=qhi[:, ft * 512:(ft + 1) * 512],
                    in_=qfm_hi[ft, :, i0:i0 + 512]),
                    w=["qhi"], dma=True)
            for jt in range(NCH):
                slot = jt % 3
                C.op(DM, lambda e, jt=jt, slot=slot: e.dma_start(
                    out=ktile[slot][:],
                    in_=kfm_hi[0:KT, :, jt * 128:(jt + 1) * 128].transpose([1, 0, 2])),
                    w=[ktile[slot]], dma=True)
                C.op(G, lambda e, jt=jt, i0=i0, slot=slot: e.dma_start(
                    out=dtile[slot % 2][:], in_=dprev_in[jt * 128:(jt + 1) * 128, i0:i0 + 512]),
                    w=[dtile[slot % 2]], dma=True)
                sc = ps_sc[:, (jt % 2) * 512:(jt % 2) * 512 + 512]
                for ft in range(KT):
                    C.op(T, lambda e, ft=ft, jt=jt, slot=slot, sc=sc: e.matmul(
                        sc, ktile[slot][:, ft * 128:(ft + 1) * 128],
                        qhi[:, ft * 512:(ft + 1) * 512],
                        start=(ft == 0), stop=(ft == KT - 1)),
                        r=[ktile[slot], "qhi"], w=[f"p3a_sc{jt % 2}"], accum=(ft > 0))
                C.op(V, lambda e, jt=jt, slot=slot, sc=sc: e.tensor_mul(
                    out=atile[:, jt * 512:(jt + 1) * 512], in0=sc, in1=dtile[slot % 2][:]),
                    r=[f"p3a_sc{jt % 2}", dtile[slot % 2]], w=["atile"])
            for it in range(4):
                for jt in range(NCH):
                    for vs in range(4):
                        C.op(T, lambda e, it=it, jt=jt, vs=vs: e.matmul(
                            ps_out[:, vs * 512:(vs + 1) * 512],
                            atile[:, jt * 512 + it * 128:jt * 512 + (it + 1) * 128],
                            vprev.ap(jt * H + vs * 512, 512),
                            start=(jt == 0), stop=(jt == NCH - 1)),
                            r=["atile", vprev.key(jt * H)], w=["p3a_out"], accum=(jt > 0))
                oslot = it % 2
                C.op(S, lambda e, oslot=oslot: e.activation(out=oev[oslot][:],
                                                            in_=ps_out[:], func=AF.Copy),
                     r=["p3a_out"], w=[oev[oslot]])
                C.op(DM, lambda e, qi=qi, it=it, oslot=oslot: e.dma_start(
                    out=oprev_hbm[qi * 512 + it * 128:qi * 512 + (it + 1) * 128, :],
                    in_=oev[oslot][:]),
                    r=[oev[oslot]], w=["oprev_hbm"], dma=True)
        C.barrier()
        C.flush(block)

    for vfh in range(2):
        with ExitStack() as es:
            sb = lambda name, shape, dt: es.enter_context(
                nc.sbuf_tensor(f"p3b{vfh}_" + name, shape, dt))
            vh_hi = Split(nc, es, f"p3b{vfh}_vh_hi", NCH * 1024, 8 * 1024, BF16)
            vh_lo = Split(nc, es, f"p3b{vfh}_vh_lo", NCH * 1024, 8 * 1024, BF16)
            qhi = sb("qhi", [128, KT * 256], BF16)
            qlo = sb("qlo", [128, KT * 256], BF16)
            k_hi = [sb(f"k_hi{s}", [128, KT * 128], BF16) for s in range(3)]
            k_lo = [sb(f"k_lo{s}", [128, KT * 128], BF16) for s in range(3)]
            a_hi = sb("a_hi", [128, NCH * 256], BF16)
            a_lo = sb("a_lo", [128, NCH * 256], BF16)
            sf32 = [sb(f"sf32{s}", [128, 256], F32) for s in range(2)]
            dtile = [sb(f"dtile{s}", [128, 256], F32) for s in range(2)]
            oev = [sb(f"oev{s}", [128, 1024], F32) for s in range(2)]
            ps_sc = es.enter_context(nc.psum_tensor(f"p3b{vfh}_ps_sc", [128, 1024], F32))
            ps_out = es.enter_context(nc.psum_tensor(f"p3b{vfh}_ps_out", [128, 2048], F32))
            for c in range(NCH):
                C.op(DM, lambda e, c=c, vfh=vfh: e.dma_start(
                    out=vh_hi.ap(c * 1024, 1024),
                    in_=vhi_hbm[NCH + c][:, vfh * 1024:(vfh + 1) * 1024]),
                    w=[vh_hi.key(c * 1024)], dma=True)
                C.op(DM, lambda e, c=c, vfh=vfh: e.dma_start(
                    out=vh_lo.ap(c * 1024, 1024),
                    in_=vlo_hbm[c][:, vfh * 1024:(vfh + 1) * 1024]),
                    w=[vh_lo.key(c * 1024)], dma=True)
            for qi in range(NCH * CH // 256):
                i0 = qi * 256
                for ft in range(KT):
                    C.op(DM, lambda e, ft=ft, i0=i0: e.dma_start(
                        out=qhi[:, ft * 256:(ft + 1) * 256],
                        in_=qfm_hi[ft, :, i0:i0 + 256]),
                        w=["qhi"], dma=True)
                    C.op(DM, lambda e, ft=ft, i0=i0: e.dma_start(
                        out=qlo[:, ft * 256:(ft + 1) * 256],
                        in_=qfm_lo[ft, :, i0:i0 + 256]),
                        w=["qlo"], dma=True)
                njt = 2 * qi + 2
                for jt in range(njt):
                    slot = jt % 3
                    C.op(DM, lambda e, jt=jt, slot=slot: e.dma_start(
                        out=k_hi[slot][:],
                        in_=kfm_hi[0:KT, :, NCH * CH + jt * 128:
                                   NCH * CH + (jt + 1) * 128].transpose([1, 0, 2])),
                        w=[k_hi[slot]], dma=True)
                    C.op(G, lambda e, jt=jt, slot=slot: e.dma_start(
                        out=k_lo[slot][:],
                        in_=kfm_lo[0:KT, :, jt * 128:(jt + 1) * 128].transpose([1, 0, 2])),
                        w=[k_lo[slot]], dma=True)
                    C.op(G, lambda e, jt=jt, i0=i0, slot=slot: e.dma_start(
                        out=dtile[slot % 2][:], in_=down_in[jt * 128:(jt + 1) * 128, i0:i0 + 256]),
                        w=[dtile[slot % 2]], dma=True)
                    sc = ps_sc[:, (jt % 2) * 512:(jt % 2) * 512 + 256]
                    for ft in range(KT):
                        C.op(T, lambda e, ft=ft, slot=slot, sc=sc: e.matmul(
                            sc, k_hi[slot][:, ft * 128:(ft + 1) * 128],
                            qhi[:, ft * 256:(ft + 1) * 256],
                            start=(ft == 0), stop=False),
                            r=[k_hi[slot], "qhi"], w=[f"p3b_sc{jt % 2}"], accum=(ft > 0))
                        C.op(T, lambda e, ft=ft, slot=slot, sc=sc: e.matmul(
                            sc, k_hi[slot][:, ft * 128:(ft + 1) * 128],
                            qlo[:, ft * 256:(ft + 1) * 256],
                            start=False, stop=False),
                            r=[k_hi[slot], "qlo"], w=[f"p3b_sc{jt % 2}"], accum=True)
                        C.op(T, lambda e, ft=ft, slot=slot, sc=sc: e.matmul(
                            sc, k_lo[slot][:, ft * 128:(ft + 1) * 128],
                            qhi[:, ft * 256:(ft + 1) * 256],
                            start=False, stop=(ft == KT - 1)),
                            r=[k_lo[slot], "qhi"], w=[f"p3b_sc{jt % 2}"], accum=True)
                    C.op(V, lambda e, slot=slot, sc=sc: e.tensor_mul(
                        out=sf32[slot % 2][:], in0=sc, in1=dtile[slot % 2][:]),
                        r=[f"p3b_sc{jt % 2}", dtile[slot % 2]], w=[sf32[slot % 2]])
                    C.op(S, lambda e, jt=jt, slot=slot: e.activation(
                        out=a_hi[:, jt * 256:(jt + 1) * 256], in_=sf32[slot % 2][:],
                        func=AF.Copy),
                        r=[sf32[slot % 2]], w=["a_hi"])
                    C.op(V, lambda e, jt=jt, slot=slot: e.tensor_sub(
                        out=a_lo[:, jt * 256:(jt + 1) * 256], in0=sf32[slot % 2][:],
                        in1=a_hi[:, jt * 256:(jt + 1) * 256]),
                        r=[sf32[slot % 2], "a_hi"], w=["a_lo"])
                for it in range(2):
                    for jt in range(njt):
                        a_h = a_hi[:, jt * 256 + it * 128:jt * 256 + (it + 1) * 128]
                        a_l = a_lo[:, jt * 256 + it * 128:jt * 256 + (it + 1) * 128]
                        for vs in range(2):
                            vh = vh_hi.ap(jt * 1024 + vs * 512, 512)
                            vl = vh_lo.ap(jt * 1024 + vs * 512, 512)
                            vhk = vh_hi.key(jt * 1024)
                            vlk = vh_lo.key(jt * 1024)
                            po = ps_out[:, (it * 1024) + vs * 512:(it * 1024) + (vs + 1) * 512]
                            C.op(T, lambda e, a_h=a_h, vh=vh, po=po, jt=jt: e.matmul(
                                po, a_h, vh, start=(jt == 0), stop=False),
                                r=["a_hi", vhk], w=[f"p3b_out{it}"], accum=(jt > 0))
                            C.op(T, lambda e, a_h=a_h, vl=vl, po=po: e.matmul(
                                po, a_h, vl, start=False, stop=False),
                                r=["a_hi", vlk], w=[f"p3b_out{it}"], accum=True)
                            C.op(T, lambda e, a_l=a_l, vh=vh, po=po, jt=jt, njt=njt: e.matmul(
                                po, a_l, vh, start=False, stop=(jt == njt - 1)),
                                r=["a_lo", vhk], w=[f"p3b_out{it}"], accum=True)
                    oslot = it % 2
                    C.op(S, lambda e, it=it, oslot=oslot: e.activation(
                        out=oev[oslot][:], in_=ps_out[:, it * 1024:(it + 1) * 1024],
                        func=AF.Copy),
                        r=[f"p3b_out{it}"], w=[oev[oslot]])
                    C.op(DM, lambda e, qi=qi, it=it, vfh=vfh, oslot=oslot: e.dma_start(
                        out=oown_hbm[qi * 256 + it * 128:qi * 256 + (it + 1) * 128,
                                     vfh * 1024:(vfh + 1) * 1024],
                        in_=oev[oslot][:]),
                        r=[oev[oslot]], w=["oown_hbm"], dma=True)
            C.barrier()
            C.flush(block)


def _ffn_post_phase(C, nc, block, wf8, x1_hbm, oprev_hbm, oown_hbm, out,
                    ident, consts, wsr, bir):
    V, S, T, DM, G = "vector", "scalar", "tensor", "sync", "gpsimd"
    with ExitStack() as es:
        sb = lambda name, shape, dt: es.enter_context(nc.sbuf_tensor("p4_" + name, shape, dt))
        wft = Split(nc, es, "p4_wft", KT * H, 8 * H, BF16)
        wstage = [sb(f"wstage{s}", [128, H], I8) for s in range(2)]
        wfbf = sb("wfbf", [128, H], BF16)
        ps_tr = es.enter_context(nc.psum_tensor("p4_ps_tr", [128, 2048], BF16))
        ps_ffn = es.enter_context(nc.psum_tensor("p4_ps_ffn", [128, H], F32))
        for ntile in range(KT):
            slot = ntile % 2
            C.op(DM, lambda e, ntile=ntile, slot=slot: e.dma_start(
                out=wstage[slot][:], in_=wf8[ntile * 128:(ntile + 1) * 128, :]),
                w=[wstage[slot]], dma=True)
            C.op(V, lambda e, slot=slot: e.tensor_copy(out=wfbf[:], in_=wstage[slot][:]),
                 r=[wstage[slot]], w=["wfbf4"])
            _transpose16(C, None, ident,
                         lambda kt, ntile=ntile: wfbf[:, kt * 128:(kt + 1) * 128],
                         "wfbf4", ps_tr,
                         lambda kt, ntile=ntile: wft.ap(kt * H + ntile * 128, 128),
                         "wft4")
        scr = [sb(f"s{i}", [128, H], F32) for i in range(5)]
        i32a = sb("i32a", [128, H], I32)
        i32b = sb("i32b", [128, H], I32)
        qbf = wfbf
        u1b = sb("u1", [128, H], BF16)
        u2b = sb("u2", [128, H], BF16)
        x2c_t = sb("x2c", [128, H], F32)
        x3c_t = sb("x3c", [128, H], F32)
        x1cs = [sb(f"x1c{s}", [128, H], F32) for s in range(2)]
        ops = [sb("op0", [128, H], F32)] * 2
        oos = [sb("oo0", [128, H], F32)] * 2
        qts = [sb("qt0", [128, H], BF16)] * 2
        stt = {}
        for nm in ("msum", "negm", "vsum", "lnv", "rstd", "dmax",
                   "amax", "am2", "rec", "rec127", "fac", "iscale"):
            stt[nm] = sb(f"st_{nm}", [128, 1], F32)
        tiles = []
        for s in range(2):
            t = dict(d=scr[0], sq=scr[1], y=scr[2], dd=scr[1], rf=scr[3],
                     tq=scr[2], v0=scr[0], v1=scr[1], vv=scr[2],
                     fa=scr[0], fa2=scr[1], flog=scr[0], fu=scr[1],
                     fuc=scr[0],
                     fum=scr[4], frf=scr[0], fdd=scr[1], fr1=scr[3],
                     fr2=scr[0], f0=scr[1],
                     ri=i32a, i32a=i32a, i32b=i32b,
                     q=qbf, u1=u1b, u2=u2b, fu1=u1b, fu2=u2b,
                     x2c=x2c_t, x3c=x3c_t,
                     x1c=x1cs[s], op_=ops[s], oo_=oos[s], qt=qts[s])
            tiles.append((t, stt))
        for c in range(NCH):
            t, stt = tiles[c % 2]
            r0 = c * CH
            C.op(DM, lambda e, t=t, c=c, r0=r0: e.dma_start(
                out=t["x1c"][:], in_=x1_hbm[NCH * CH + r0:NCH * CH + r0 + CH, :]),
                w=[t["x1c"]], dma=True)
            C.op(DM, lambda e, t=t, r0=r0: e.dma_start(
                out=t["op_"][:], in_=oprev_hbm[r0:r0 + CH, :]),
                w=[t["op_"]], dma=True)
            C.op(DM, lambda e, t=t, r0=r0: e.dma_start(
                out=t["oo_"][:], in_=oown_hbm[r0:r0 + CH, :]),
                w=[t["oo_"]], dma=True)
            C.op(V, lambda e, t=t: e.tensor_add(out=t["x2c"][:], in0=t["x1c"][:],
                                                in1=t["op_"][:]),
                 r=[t["x1c"], t["op_"]], w=[t["x2c"]])
            C.op(V, lambda e, t=t: e.tensor_add(out=t["x2c"][:], in0=t["x2c"][:],
                                                in1=t["oo_"][:]),
                 r=[t["x2c"], t["oo_"]], w=[t["x2c"]])
            _ln_quant(C, consts, t["x2c"][:], t["x2c"], t, stt, want_h2=False)
            _transpose16(C, t, ident,
                         lambda i, t=t: t["q"][:, i * 128:(i + 1) * 128],
                         t["q"], ps_tr,
                         lambda i, t=t: t["qt"][:, i * 128:(i + 1) * 128],
                         t["qt"])
            for ns in range(4):
                for kt in range(KT):
                    C.op(T, lambda e, t=t, kt=kt, ns=ns: e.matmul(
                        ps_ffn[:, ns * 512:(ns + 1) * 512],
                        t["qt"][:, kt * 128:(kt + 1) * 128],
                        wft.ap(kt * H + ns * 512, 512),
                        start=(kt == 0), stop=(kt == KT - 1)),
                        r=[t["qt"], "wft4"], w=[f"ps_ffn4{ns}"], accum=(kt > 0))
                C.op(S, lambda e, t=t, stt=stt, ns=ns: e.activation(
                    out=t["v0"][:, ns * 512:(ns + 1) * 512],
                    in_=ps_ffn[:, ns * 512:(ns + 1) * 512], func=AF.Copy,
                    scale=stt["iscale"][:, 0:1]),
                    r=[f"ps_ffn4{ns}", stt["iscale"]], w=[t["v0"]])
            C.op(V, lambda e, t=t: e.tensor_mul(out=t["v1"][:], in0=t["v0"][:], in1=wsr[:]),
                 r=[t["v0"], "wsr"], w=[t["v1"]])
            C.op(V, lambda e, t=t: e.tensor_add(out=t["vv"][:], in0=t["v1"][:], in1=bir[:]),
                 r=[t["v1"], "bir"], w=[t["vv"]])
            _f16_round_add(C, consts, t, "vv", t["x2c"][:], t["x2c"],
                           t["x3c"][:], t["x3c"])
            C.op(DM, lambda e, t=t, r0=r0: e.dma_start(
                out=out[r0:r0 + CH, :], in_=t["x3c"][:]),
                r=[t["x3c"]], w=["out"], dma=True)
        C.barrier()
        C.flush(block)



def _prep_inputs(L0_attn_X, L0_attn_W_Q, L0_attn_W_K, L0_attn_W_V,
                 L0_ffn_weight, L0_ffn_w_scale, L0_ffn_bias):
    tb = _host_tables()
    X = np.ascontiguousarray(np.asarray(L0_attn_X, dtype=np.float32))
    wq = np.ascontiguousarray(np.asarray(L0_attn_W_Q, dtype=np.float32))
    wk = np.ascontiguousarray(np.asarray(L0_attn_W_K, dtype=np.float32))
    wv = np.ascontiguousarray(np.asarray(L0_attn_W_V, dtype=np.float32))
    wf8 = np.ascontiguousarray(np.asarray(L0_ffn_weight, dtype=np.int8))
    ws = np.asarray(L0_ffn_w_scale, dtype=np.float32).reshape(-1)
    bias = np.asarray(L0_ffn_bias, dtype=np.float32).reshape(-1)
    wsrow = np.ascontiguousarray(np.broadcast_to(ws[None, :], (128, H)))
    biasrow = np.ascontiguousarray(np.broadcast_to(bias[None, :], (128, H)))
    ident = np.eye(128, dtype=ml_dtypes.bfloat16)
    zeros_slice = np.zeros((PREV, H), dtype=np.float32)
    zeros_tab = np.zeros((PREV, H // 2), dtype=np.float32)
    in_maps = []
    for core in range(8):
        b, q = divmod(core, 4)
        own = X[b, q * TOK:(q + 1) * TOK]
        prev = X[b, (q - 1) * TOK:q * TOK] if q > 0 else zeros_slice
        x0both = np.ascontiguousarray(np.concatenate([prev, own], axis=0))
        r0 = q * TOK
        csq = np.ascontiguousarray(tb["csq"][r0:r0 + TOK])
        ssq = np.ascontiguousarray(tb["ssq"][r0:r0 + TOK])
        if q > 0:
            csk = np.ascontiguousarray(tb["csk"][r0 - PREV:r0 + TOK])
            ssk = np.ascontiguousarray(tb["ssk"][r0 - PREV:r0 + TOK])
            dprev = tb["d_prev_T"]
        else:
            csk = np.ascontiguousarray(
                np.concatenate([zeros_tab, tb["csk"][r0:r0 + TOK]], axis=0))
            ssk = np.ascontiguousarray(
                np.concatenate([zeros_tab, tb["ssk"][r0:r0 + TOK]], axis=0))
            dprev = tb["d_zero_T"]
        in_maps.append({
            "x0both": x0both, "wq": wq, "wk": wk, "wv": wv, "wf8": wf8,
            "wsrow": wsrow, "biasrow": biasrow, "ident": ident,
            "csq": csq, "ssq": ssq, "csk": csk, "ssk": ssk,
            "d_own_T": tb["d_own_T"], "d_prev_T": dprev,
        })
    return in_maps


def kernel(L0_attn_X, L0_attn_W_Q, L0_attn_W_K, L0_attn_W_V,
           L0_ffn_weight, L0_ffn_w_scale, L0_ffn_bias):
    in_maps = _prep_inputs(L0_attn_X, L0_attn_W_Q, L0_attn_W_K, L0_attn_W_V,
                           L0_ffn_weight, L0_ffn_w_scale, L0_ffn_bias)
    nc = build_program()
    res = run_bass_kernel_spmd(nc, in_maps, core_ids=list(range(8)))
    out = np.empty((2, 8192, H), dtype=np.float32)
    for core in range(8):
        b, q = divmod(core, 4)
        out[b, q * TOK:(q + 1) * TOK] = np.asarray(res.results[core]["out"],
                                                   dtype=np.float32)
    return out


# version 04
# speedup vs baseline: 1.2172x; 1.2172x over previous
import math
import os
from contextlib import ExitStack

import numpy as np
import ml_dtypes

import concourse.bass as bass
import concourse.mybir as mybir
from concourse.bass_utils import run_bass_kernel_spmd

F32 = mybir.dt.float32
BF16 = mybir.dt.bfloat16
F16 = mybir.dt.float16
I8 = mybir.dt.int8
I32 = mybir.dt.int32
AF = mybir.ActivationFunctionType
AL = mybir.AluOpType
AX = mybir.AxisListType

H = 2048
TOK = 2048
PREV = 2048
CH = 128
NCH = TOK // CH
NALL = (TOK + PREV) // CH
KT = H // 128
GAMMA = 0.96875
LN2 = math.log(2.0)

ENGINES = ("sync", "gpsimd", "scalar", "vector", "tensor")
_BLOCK_ATTR = {"sync": "sync", "gpsimd": "gpsimd", "scalar": "scalar",
               "vector": "vector", "tensor": "tensor"}



_TABLES = {}


def _host_tables():
    if _TABLES:
        return _TABLES
    import jax
    import jax.numpy as jnp
    cpu = jax.devices("cpu")[0]
    S = 8192
    with jax.default_device(cpu):
        half = H // 2
        scale_vec = (jnp.arange(0, H, 2, dtype=jnp.float32) + 0.4 * H) / (1.4 * H)
        power = jnp.arange(S, dtype=jnp.float32) / 512
        scale = scale_vec[None, :] ** power[:, None]
        inv_freq = 1.0 / 10000.0 ** (jnp.arange(half, dtype=jnp.float32) / half)
        sinusoid = jnp.arange(S, dtype=jnp.float32)[:, None] * inv_freq[None, :]
        sin, cos = jnp.sin(sinusoid), jnp.cos(sinusoid)
        ssq = np.asarray(sin * scale, dtype=np.float32)
        csq = np.asarray(cos * scale, dtype=np.float32)
        dscale = 1.0 / scale
        ssk = np.asarray(sin * dscale, dtype=np.float32)
        csk = np.asarray(cos * dscale, dtype=np.float32)
        n = jnp.arange(TOK)[:, None]
        m = jnp.arange(TOK)[None, :]
        dmat = jnp.where(n >= m, GAMMA ** jnp.maximum(n - m, 0).astype(jnp.float32), 0.0)
        d_own_T = np.asarray(dmat, dtype=np.float32).T.copy()
        n2 = jnp.arange(TOK)[:, None]
        m2 = jnp.arange(-PREV, 0)[None, :]
        dmat2 = jnp.where(n2 >= m2, GAMMA ** jnp.maximum(n2 - m2, 0).astype(jnp.float32), 0.0)
        d_prev_T = np.asarray(dmat2, dtype=np.float32).T.copy()
    _TABLES.update(dict(ssq=ssq, csq=csq, ssk=ssk, csk=csk,
                        d_own_T=np.ascontiguousarray(d_own_T),
                        d_prev_T=np.ascontiguousarray(d_prev_T),
                        d_zero_T=np.zeros_like(d_prev_T)))
    return _TABLES



class Ctx:

    N_DMA_SEMS = {"sync": 36, "gpsimd": 36}

    def __init__(self, nc, es):
        self.nc = nc
        self.sems = {e: es.enter_context(nc.semaphore("prog_" + e)) for e in ENGINES}
        self.semid = {e: ("e", e) for e in ENGINES}
        self.cnt = {e: 0 for e in ENGINES}
        self.hwm = {e: {} for e in ENGINES}
        self.state = {}
        self.prog = {e: [] for e in ENGINES}
        self.dpool = {}
        self.dcnt = {}
        self.dcons = {}
        self.dnext = {}
        for e, n in self.N_DMA_SEMS.items():
            self.dpool[e] = [es.enter_context(nc.semaphore(f"dma_{e}{i}"))
                             for i in range(n)]
            self.dnext[e] = 0
        self.semobj = {("e", e): self.sems[e] for e in ENGINES}
        for e, pool in self.dpool.items():
            for i, s in enumerate(pool):
                self.semobj[("d", e, i)] = s
                self.dcnt[("d", e, i)] = 0
                self.dcons[("d", e, i)] = True

    @staticmethod
    def _key(t):
        return t if isinstance(t, str) else t.name

    def _want(self, eng, reads, writes, skip_waw_engine=None):
        need = {}
        for t in reads:
            st = self.state.get(self._key(t))
            if st and st["w"]:
                sk, c = st["w"]
                need[sk] = max(need.get(sk, 0), c)
        for t in writes:
            st = self.state.get(self._key(t))
            if st:
                if st["w"] and not (skip_waw_engine and st["w"][0] == ("e", skip_waw_engine)):
                    sk, c = st["w"]
                    need[sk] = max(need.get(sk, 0), c)
                for sk, c in st["r"].items():
                    need[sk] = max(need.get(sk, 0), c)
        out = []
        for sk, c in need.items():
            if self.hwm[eng].get(sk, 0) < c:
                out.append((sk, c))
                self.hwm[eng][sk] = c
                if sk[0] == "d":
                    self.dcons[sk] = True
        return out

    def _emit(self, eng, fn, esems, sem, inc):
        def closure(engobj, fn=fn, esems=esems, sem=sem, inc=inc):
            for s, c in esems:
                engobj.wait_ge(s, c)
            ins = fn(engobj)
            if sem is not None:
                ins.then_inc(sem, inc)
        self.prog[eng].append(closure)

    def op(self, eng, fn, r=(), w=(), dma=False, accum=False):
        skip = eng if accum else None
        edges = self._want(eng, r, w, skip_waw_engine=skip)
        esems = [(self.semobj[sk], c) for sk, c in edges]
        if dma:
            pool = self.dpool[eng]
            i = self.dnext[eng]
            self.dnext[eng] = (i + 1) % len(pool)
            sk = ("d", eng, i)
            if not self.dcons[sk]:
                if self.hwm[eng].get(sk, 0) < self.dcnt[sk]:
                    esems.append((self.semobj[sk], self.dcnt[sk]))
                    self.hwm[eng][sk] = self.dcnt[sk]
            self.dcnt[sk] += 16
            self.dcons[sk] = False
            myc = self.dcnt[sk]
            self._emit(eng, fn, esems, self.semobj[sk], 16)
        else:
            sk = ("e", eng)
            self.cnt[eng] += 1
            myc = self.cnt[eng]
            self._emit(eng, fn, esems, self.sems[eng], 1)
        for t in r:
            st = self.state.setdefault(self._key(t), {"w": None, "r": {}})
            st["r"][sk] = myc
        for t in w:
            st = self.state.setdefault(self._key(t), {"w": None, "r": {}})
            st["w"] = (sk, myc)
            st["r"] = {}
        return sk, myc

    def wait_all(self, eng):
        esems = []
        for f in ENGINES:
            if f == eng:
                continue
            sk = ("e", f)
            c = self.cnt[f]
            if c > self.hwm[eng].get(sk, 0):
                self.hwm[eng][sk] = c
                esems.append((self.semobj[sk], c))
        for sk, c in self.dcnt.items():
            if c > self.hwm[eng].get(sk, 0):
                self.hwm[eng][sk] = c
                self.dcons[sk] = True
                esems.append((self.semobj[sk], c))
        if esems:
            def closure(engobj, esems=esems):
                for s, c in esems:
                    engobj.wait_ge(s, c)
            self.prog[eng].append(closure)

    def barrier(self):
        for e in ENGINES:
            self.wait_all(e)
        self.state = {}

    def flush(self, block):
        for e in ENGINES:
            lst = self.prog[e]
            self.prog[e] = []
            if lst:
                getattr(block, _BLOCK_ATTR[e])(lambda engobj, lst=lst: [f(engobj) for f in lst])


class Split:

    def __init__(self, nc, es, name, total_cols, cols_per_part, dtype):
        cols_per_part = min(cols_per_part, total_cols)
        assert total_cols % cols_per_part == 0
        self.cpp = cols_per_part
        self.t = [es.enter_context(nc.sbuf_tensor(f"{name}_{i}", [128, cols_per_part], dtype))
                  for i in range(total_cols // cols_per_part)]

    def ap(self, c0, w):
        i = c0 // self.cpp
        o = c0 % self.cpp
        assert o + w <= self.cpp, (c0, w, self.cpp)
        return self.t[i][:, o:o + w]

    def key(self, c0):
        return self.t[c0 // self.cpp].name



def _ln_quant(C, consts, x_ap, xkey, t, st, want_h2=False):
    V, S = "vector", "scalar"
    C.op(V, lambda e: e.reduce_sum(out=st["msum"][:, 0:1], in_=x_ap, axis=AX.X),
         r=[xkey], w=[st["msum"]])
    C.op(V, lambda e: e.tensor_scalar(out=st["negm"][:, 0:1], in0=st["msum"][:, 0:1],
                                      scalar1=-1.0 / H, scalar2=None, op0=AL.mult),
         r=[st["msum"]], w=[st["negm"]])
    C.op(V, lambda e: e.tensor_scalar_add(out=t["d"][:], in0=x_ap,
                                          scalar1=st["negm"][:, 0:1]),
         r=[xkey, st["negm"]], w=[t["d"]])
    C.op(S, lambda e: e.activation(out=t["sq"][:], in_=t["d"][:], func=AF.Square),
         r=[t["d"]], w=[t["sq"]])
    C.op(V, lambda e: e.reduce_sum(out=st["vsum"][:, 0:1], in_=t["sq"][:], axis=AX.X),
         r=[t["sq"]], w=[st["vsum"]])
    C.op(S, lambda e: e.activation(out=st["lnv"][:, 0:1], in_=st["vsum"][:, 0:1],
                                   func=AF.Ln, scale=consts["invH"],
                                   bias=consts["epsln"]),
         r=[st["vsum"], "consts"], w=[st["lnv"]])
    C.op(S, lambda e: e.activation(out=st["rstd"][:, 0:1], in_=st["lnv"][:, 0:1],
                                   func=AF.Exp, scale=consts["neghalf"]),
         r=[st["lnv"], "consts"], w=[st["rstd"]])
    C.op(V, lambda e: e.reduce_max(out=st["dmax"][:, 0:1], in_=t["d"][:], axis=AX.X,
                                   apply_absolute_value=True),
         r=[t["d"]], w=[st["dmax"]])
    C.op(V, lambda e: e.tensor_mul(out=st["amax"][:, 0:1], in0=st["dmax"][:, 0:1],
                                   in1=st["rstd"][:, 0:1]),
         r=[st["dmax"], st["rstd"]], w=[st["amax"]])
    C.op(V, lambda e: e.tensor_scalar_max(out=st["am2"][:, 0:1], in0=st["amax"][:, 0:1],
                                          scalar1=1e-7),
         r=[st["amax"]], w=[st["am2"]])
    C.op(V, lambda e: e.reciprocal(out=st["rec"][:, 0:1], in_=st["am2"][:, 0:1]),
         r=[st["am2"]], w=[st["rec"]])
    C.op(V, lambda e: e.tensor_scalar(out=st["iscale"][:, 0:1], in0=st["am2"][:, 0:1],
                                      scalar1=1.0 / 127.0, scalar2=None, op0=AL.mult),
         r=[st["am2"]], w=[st["iscale"]])
    C.op(V, lambda e: e.tensor_scalar(out=st["rec127"][:, 0:1], in0=st["rec"][:, 0:1],
                                      scalar1=127.0, scalar2=None, op0=AL.mult),
         r=[st["rec"]], w=[st["rec127"]])
    C.op(V, lambda e: e.tensor_mul(out=st["fac"][:, 0:1], in0=st["rec127"][:, 0:1],
                                   in1=st["rstd"][:, 0:1]),
         r=[st["rec127"], st["rstd"]], w=[st["fac"]])
    C.op(V, lambda e: e.tensor_scalar_mul(out=t["y"][:], in0=t["d"][:],
                                          scalar1=st["fac"][:, 0:1]),
         r=[t["d"], st["fac"]], w=[t["y"]])
    if want_h2:
        C.op(V, lambda e: e.tensor_scalar_mul(out=t["h2"][:], in0=t["d"][:],
                                              scalar1=st["rstd"][:, 0:1]),
             r=[t["d"], st["rstd"]], w=[t["h2"]])
    C.op(V, lambda e: e.tensor_copy(out=t["ri"][:], in_=t["y"][:]),
         r=[t["y"]], w=[t["ri"]])
    C.op(V, lambda e: e.tensor_copy(out=t["rf"][:], in_=t["ri"][:]),
         r=[t["ri"]], w=[t["rf"]])
    C.op(V, lambda e: e.tensor_sub(out=t["dd"][:], in0=t["y"][:], in1=t["rf"][:]),
         r=[t["y"], t["rf"]], w=[t["dd"]])
    C.op(V, lambda e: e.tensor_scalar(out=t["u1"][:], in0=t["dd"][:], scalar1=0.5,
                                      scalar2=None, op0=AL.is_gt),
         r=[t["dd"]], w=[t["u1"]])
    C.op(V, lambda e: e.tensor_scalar(out=t["u2"][:], in0=t["dd"][:], scalar1=-0.5,
                                      scalar2=None, op0=AL.is_lt),
         r=[t["dd"]], w=[t["u2"]])
    C.op(V, lambda e: e.scalar_tensor_tensor(out=t["tq"][:], in0=t["u1"][:], scalar=1.0,
                                             in1=t["rf"][:], op0=AL.mult, op1=AL.add),
         r=[t["u1"], t["rf"]], w=[t["tq"]])
    C.op(V, lambda e: e.tensor_sub(out=t["q"][:], in0=t["tq"][:], in1=t["u2"][:]),
         r=[t["tq"], t["u2"]], w=[t["q"]])


def _f16_round_add(C, consts, t, vkey, xin_ap, xin_key, xout_ap, xout_key):
    V, S = "vector", "scalar"
    v = t[vkey]
    i32a, i32b = t["i32a"], t["i32b"]
    bca = i32a[:].bitcast(F32)
    bcb = i32b[:].bitcast(F32)
    C.op(S, lambda e: e.activation(out=t["fa"][:], in_=v[:], func=AF.Abs),
         r=[v], w=[t["fa"]])
    C.op(V, lambda e: e.tensor_scalar_max(out=t["fa2"][:], in0=t["fa"][:], scalar1=1e-30),
         r=[t["fa"]], w=[t["fa2"]])
    C.op(S, lambda e: e.activation(out=t["flog"][:], in_=t["fa2"][:], func=AF.Ln),
         r=[t["fa2"]], w=[t["flog"]])
    C.op(V, lambda e: e.tensor_scalar(out=t["fu"][:], in0=t["flog"][:],
                                      scalar1=1.0 / LN2, scalar2=14.0,
                                      op0=AL.mult, op1=AL.add),
         r=[t["flog"]], w=[t["fu"]])
    C.op(V, lambda e: e.tensor_scalar(out=t["fuc"][:], in0=t["fu"][:],
                                      scalar1=0.0, scalar2=22.9,
                                      op0=AL.max, op1=AL.min),
         r=[t["fu"]], w=[t["fuc"]])
    C.op(V, lambda e: e.tensor_copy(out=i32a[:], in_=t["fuc"][:]),
         r=[t["fuc"]], w=[i32a])
    C.op(V, lambda e: e.tensor_scalar(out=i32b[:], in0=i32a[:],
                                      scalar1=-8388608, scalar2=151 * 8388608,
                                      op0=AL.mult, op1=AL.add),
         r=[i32a], w=[i32b])
    C.op(V, lambda e: e.tensor_scalar(out=i32a[:], in0=i32a[:],
                                      scalar1=8388608, scalar2=102 * 8388608,
                                      op0=AL.mult, op1=AL.add),
         r=[i32a], w=[i32a])
    C.op(V, lambda e: e.tensor_mul(out=t["fum"][:], in0=v[:], in1=bcb),
         r=[v, i32b], w=[t["fum"]])
    C.op(V, lambda e: e.tensor_copy(out=i32b[:], in_=t["fum"][:]),
         r=[t["fum"]], w=[i32b])
    C.op(V, lambda e: e.tensor_copy(out=t["frf"][:], in_=i32b[:]),
         r=[i32b], w=[t["frf"]])
    C.op(V, lambda e: e.tensor_sub(out=t["fdd"][:], in0=t["fum"][:], in1=t["frf"][:]),
         r=[t["fum"], t["frf"]], w=[t["fdd"]])
    C.op(V, lambda e: e.tensor_scalar(out=t["fu1"][:], in0=t["fdd"][:], scalar1=0.5,
                                      scalar2=None, op0=AL.is_gt),
         r=[t["fdd"]], w=[t["fu1"]])
    C.op(V, lambda e: e.tensor_scalar(out=t["fu2"][:], in0=t["fdd"][:], scalar1=-0.5,
                                      scalar2=None, op0=AL.is_lt),
         r=[t["fdd"]], w=[t["fu2"]])
    C.op(V, lambda e: e.scalar_tensor_tensor(out=t["fr1"][:], in0=t["fu1"][:], scalar=1.0,
                                             in1=t["frf"][:], op0=AL.mult, op1=AL.add),
         r=[t["fu1"], t["frf"]], w=[t["fr1"]])
    C.op(V, lambda e: e.tensor_sub(out=t["fr2"][:], in0=t["fr1"][:], in1=t["fu2"][:]),
         r=[t["fr1"], t["fu2"]], w=[t["fr2"]])
    C.op(V, lambda e: e.tensor_mul(out=t["f0"][:], in0=t["fr2"][:], in1=bca),
         r=[t["fr2"], i32a], w=[t["f0"]])
    C.op(V, lambda e: e.tensor_add(out=xout_ap, in0=xin_ap, in1=t["f0"][:]),
         r=[xin_key, t["f0"]], w=[xout_key])


def _xpos(C, t, src_key, src_ap, cs_ap, ss_ap, tab_keys, out_ap, out_key):
    V = "vector"
    csk, ssk = tab_keys
    xe = src_ap[:, 0:H:2]
    xo = src_ap[:, 1:H:2]
    C.op(V, lambda e: e.tensor_mul(out=t["xp1"][:], in0=xe, in1=cs_ap),
         r=[src_key, csk], w=[t["xp1"]])
    C.op(V, lambda e: e.tensor_mul(out=t["xp2"][:], in0=xo, in1=ss_ap),
         r=[src_key, ssk], w=[t["xp2"]])
    C.op(V, lambda e: e.tensor_sub(out=out_ap[:, 0:H:2], in0=t["xp1"][:], in1=t["xp2"][:]),
         r=[t["xp1"], t["xp2"]], w=[out_key])
    C.op(V, lambda e: e.tensor_mul(out=t["xp1"][:], in0=xo, in1=cs_ap),
         r=[src_key, csk, out_key], w=[t["xp1"]])
    C.op(V, lambda e: e.tensor_mul(out=t["xp2"][:], in0=xe, in1=ss_ap),
         r=[src_key, ssk], w=[t["xp2"]])
    C.op(V, lambda e: e.tensor_add(out=out_ap[:, 1:H:2], in0=t["xp1"][:], in1=t["xp2"][:]),
         r=[t["xp1"], t["xp2"]], w=[out_key])


def _xpos_psum(C, t, ps_key, ps_ap, cs_ap, ss_ap, out_ap, out_key):
    V = "vector"
    xe = ps_ap[:, 0:H:2]
    xo = ps_ap[:, 1:H:2]
    s1 = t["xp2a"]
    s2 = t["xp2b"]
    sk = t["xp2key"]
    C.op(V, lambda e: e.tensor_mul(out=s1, in0=xo, in1=ss_ap),
         r=[ps_key, t["ss"]], w=[sk])
    C.op(V, lambda e: e.tensor_mul(out=s2, in0=xe, in1=ss_ap),
         r=[ps_key, t["ss"], sk], w=[sk])
    C.op(V, lambda e: e.tensor_mul(out=xe, in0=xe, in1=cs_ap),
         r=[ps_key, t["cs"], sk], w=[out_key, ps_key])
    C.op(V, lambda e: e.tensor_sub(out=xe, in0=xe, in1=s1),
         r=[out_key, sk], w=[out_key, ps_key])
    C.op(V, lambda e: e.tensor_mul(out=xo, in0=xo, in1=cs_ap),
         r=[out_key, t["cs"]], w=[out_key, ps_key])
    C.op(V, lambda e: e.tensor_add(out=xo, in0=xo, in1=s2),
         r=[out_key, sk], w=[out_key, ps_key])


def _split_hilo(C, t, src_key, src_ap, hi_ap, hi_key, lo_ap=None, lo_key=None):
    C.op("scalar", lambda e: e.activation(out=hi_ap, in_=src_ap, func=AF.Copy),
         r=[src_key], w=[hi_key])
    if lo_ap is not None:
        C.op("vector", lambda e: e.tensor_sub(out=lo_ap, in0=src_ap, in1=hi_ap),
             r=[src_key, hi_key], w=[lo_key])


def _transpose16(C, t, ident, src_ap_fn, src_key, psum, dst_ap_fn, dst_key,
                 n_tiles=KT):
    T, S = "tensor", "scalar"
    for i in range(n_tiles):
        half = i % 2
        pap = psum[:, half * 1024:half * 1024 + 128]
        C.op(T, lambda e, i=i, pap=pap: e.transpose(pap, src_ap_fn(i), ident[:]),
             r=[src_key, "identity"], w=[psum.name + f"_h{half}"])
        C.op(S, lambda e, i=i, pap=pap: e.activation(out=dst_ap_fn(i), in_=pap,
                                                     func=AF.Copy),
             r=[psum.name + f"_h{half}"], w=[dst_key])



def build_program(debug=(), phases=(1, 2, 3, 4)):
    nc = bass.Bass("TRN2", target_bir_lowering=False)

    x0both = nc.dram_tensor("x0both", [PREV + TOK, H], F32, kind="ExternalInput").ap()
    wq = nc.dram_tensor("wq", [H, H], F32, kind="ExternalInput").ap()
    wk = nc.dram_tensor("wk", [H, H], F32, kind="ExternalInput").ap()
    wv = nc.dram_tensor("wv", [H, H], F32, kind="ExternalInput").ap()
    wf8 = nc.dram_tensor("wf8", [H, H], I8, kind="ExternalInput").ap()
    wsrow = nc.dram_tensor("wsrow", [128, H], F32, kind="ExternalInput").ap()
    biasrow = nc.dram_tensor("biasrow", [128, H], F32, kind="ExternalInput").ap()
    ident_in = nc.dram_tensor("ident", [128, 128], BF16, kind="ExternalInput").ap()
    csq_in = nc.dram_tensor("csq", [TOK, H // 2], F32, kind="ExternalInput").ap()
    ssq_in = nc.dram_tensor("ssq", [TOK, H // 2], F32, kind="ExternalInput").ap()
    csk_in = nc.dram_tensor("csk", [PREV + TOK, H // 2], F32, kind="ExternalInput").ap()
    ssk_in = nc.dram_tensor("ssk", [PREV + TOK, H // 2], F32, kind="ExternalInput").ap()
    down_in = nc.dram_tensor("d_own_T", [TOK, TOK], F32, kind="ExternalInput").ap()
    dprev_in = nc.dram_tensor("d_prev_T", [PREV, TOK], F32, kind="ExternalInput").ap()

    out = nc.dram_tensor("out", [TOK, H], F32, kind="ExternalOutput").ap()

    x1_hbm = nc.dram_tensor("x1_hbm", [PREV + TOK, H], F32, kind="Internal").ap()
    h2t_hi = nc.dram_tensor("h2t_hi", [NALL, 128, H], BF16, kind="Internal").ap()
    h2t_lo = nc.dram_tensor("h2t_lo", [NCH, 128, H], BF16, kind="Internal").ap()
    vhi_hbm = nc.dram_tensor("vhi_hbm", [NALL, 128, H], BF16, kind="Internal").ap()
    vlo_hbm = nc.dram_tensor("vlo_hbm", [NCH, 128, H], BF16, kind="Internal").ap()
    kfm_hi = nc.dram_tensor("kfm_hi", [KT, 128, PREV + TOK], BF16, kind="Internal").ap()
    kfm_lo = nc.dram_tensor("kfm_lo", [KT, 128, TOK], BF16, kind="Internal").ap()
    qfm_hi = nc.dram_tensor("qfm_hi", [KT, 128, TOK], BF16, kind="Internal").ap()
    qfm_lo = nc.dram_tensor("qfm_lo", [KT, 128, TOK], BF16, kind="Internal").ap()
    oprev_hbm = nc.dram_tensor("oprev_hbm", [TOK, H], F32, kind="Internal").ap()
    oown_hbm = nc.dram_tensor("oown_hbm", [TOK, H], F32, kind="Internal").ap()

    dbg = {}
    for name in debug:
        shapes = {"x1": [PREV + TOK, H], "h2t_hi": [NALL, 128, H],
                  "vhi": [NALL, 128, H], "kfm_hi": [KT, 128, PREV + TOK],
                  "qfm_hi": [KT, 128, TOK], "oprev": [TOK, H], "oown": [TOK, H]}
        dbg[name] = nc.dram_tensor("dbg_" + name, shapes[name],
                                   BF16 if name in ("h2t_hi", "vhi", "kfm_hi", "qfm_hi") else F32,
                                   kind="ExternalOutput").ap()

    es_outer = ExitStack()
    with es_outer:
        C = Ctx(nc, es_outer)
        V, S, T, DM, G = "vector", "scalar", "tensor", "sync", "gpsimd"

        consts = {}
        const_vals = dict(invH=1.0 / H, epsln=1e-5, neghalf=-0.5,
                          negln2=-LN2, b24ln2=24.0 * LN2, posln2=LN2,
                          bm24h=-24.0 * LN2 + math.log(0.5))
        const_tile = es_outer.enter_context(nc.sbuf_tensor("consts", [128, len(const_vals)], F32))
        for i, (k, val) in enumerate(const_vals.items()):
            consts[k] = const_tile[:, i:i + 1]

        ident = es_outer.enter_context(nc.sbuf_tensor("identity", [128, 128], BF16))
        wsr = es_outer.enter_context(nc.sbuf_tensor("wsr", [128, H], F32))
        bir = es_outer.enter_context(nc.sbuf_tensor("bir", [128, H], F32))

        with nc.Block() as block:
            for i, (k, val) in enumerate(const_vals.items()):
                C.op(V, lambda e, i=i, val=val: e.memset(const_tile[:, i:i + 1], float(val)),
                     w=["consts"])
            C.op(DM, lambda e: e.dma_start(out=ident[:], in_=ident_in), w=["identity"], dma=True)
            C.op(DM, lambda e: e.dma_start(out=wsr[:], in_=wsrow), w=["wsr"], dma=True)
            C.op(DM, lambda e: e.dma_start(out=bir[:], in_=biasrow), w=["bir"], dma=True)

            with ExitStack() as es:
                sb = lambda name, shape, dt: es.enter_context(nc.sbuf_tensor(name, shape, dt))
                wft = Split(nc, es, "wft", KT * H, 8 * H, BF16)
                wstage = [sb(f"wstage{s}", [128, H], I8) for s in range(2)]
                wfbf = sb("wfbf", [128, H], BF16)
                ps_tr = es.enter_context(nc.psum_tensor("ps_tr", [128, 2048], BF16))
                ps_ffn = es.enter_context(nc.psum_tensor("ps_ffn", [128, H], F32))

                for ntile in range(KT):
                    slot = ntile % 2
                    C.op(DM, lambda e, ntile=ntile, slot=slot: e.dma_start(
                        out=wstage[slot][:], in_=wf8[ntile * 128:(ntile + 1) * 128, :]),
                        w=[wstage[slot]], dma=True)
                    C.op(V, lambda e, slot=slot: e.tensor_copy(out=wfbf[:], in_=wstage[slot][:]),
                         r=[wstage[slot]], w=[wfbf])
                    _transpose16(C, None, ident,
                                 lambda kt, ntile=ntile: wfbf[:, kt * 128:(kt + 1) * 128],
                                 wfbf, ps_tr,
                                 lambda kt, ntile=ntile: wft.ap(kt * H + ntile * 128, 128),
                                 "wft")
                scr = [sb(f"p1_s{i}", [128, H], F32) for i in range(5)]
                i32a = sb("p1_i32a", [128, H], I32)
                i32b = sb("p1_i32b", [128, H], I32)
                qbf = wfbf
                u1b = sb("p1_u1", [128, H], BF16)
                u2b = sb("p1_u2", [128, H], BF16)
                h2hi_t = sb("p1_h2hi", [128, H], BF16)
                h2lo_t = sb("p1_h2lo", [128, H], BF16)
                h2thi_t = sb("p1_h2thi", [128, H], BF16)
                h2tlo_t = sb("p1_h2tlo", [128, H], BF16)
                xts = [sb(f"p1_xt{s}", [128, H], F32) for s in range(2)]
                x1cs = [sb(f"p1_x1c{s}", [128, H], F32) for s in range(2)]
                qts = [sb("p1_qt0", [128, H], BF16)] * 2
                stts = []
                for s in range(2):
                    stt = {}
                    for nm in ("msum", "negm", "vsum", "lnv", "rstd", "dmax",
                               "amax", "am2", "rec", "rec127", "fac", "iscale"):
                        stt[nm] = sb(f"p1_st_{nm}{s}", [128, 1], F32)
                    stts.append(stt)
                tiles = []
                for s in range(2):
                    t = dict(d=scr[0], sq=scr[1], y=scr[2], dd=scr[1], rf=scr[3],
                             tq=scr[2],
                             fa=scr[0], fa2=scr[1], flog=scr[0], fu=scr[1],
                             fuc=scr[0],
                             fum=scr[4], frf=scr[0], fdd=scr[1], fr1=scr[3],
                             fr2=scr[0], f0=scr[1], h2=scr[2],
                             ri=i32a, i32a=i32a, i32b=i32b,
                             q=qbf, u1=u1b, u2=u2b, fu1=u1b, fu2=u2b,
                             h2hi=h2hi_t, h2lo=h2lo_t, h2thi=h2thi_t,
                             h2tlo=h2tlo_t,
                             xt=xts[s], x1c=x1cs[s], qt=qts[s])
                    tiles.append((t, stts[s]))

                def p1_load(c):
                    t, _ = tiles[c % 2]
                    C.op(DM, lambda e, c=c, t=t: e.dma_start(
                        out=t["xt"][:], in_=x0both[c * CH:(c + 1) * CH, :]),
                        w=[t["xt"]], dma=True)

                def p1_seg1(c):
                    t, stt = tiles[c % 2]
                    _ln_quant(C, consts, t["xt"][:], t["xt"], t, stt, want_h2=False)
                    _transpose16(C, t, ident,
                                 lambda i, t=t: t["q"][:, i * 128:(i + 1) * 128],
                                 t["q"], ps_tr,
                                 lambda i, t=t: t["qt"][:, i * 128:(i + 1) * 128],
                                 t["qt"])

                def p1_mm(c):
                    t, stt = tiles[c % 2]
                    vt = x1cs[(c + 1) % 2]
                    for ns in range(4):
                        for kt in range(KT):
                            C.op(T, lambda e, t=t, kt=kt, ns=ns: e.matmul(
                                ps_ffn[:, ns * 512:(ns + 1) * 512],
                                t["qt"][:, kt * 128:(kt + 1) * 128],
                                wft.ap(kt * H + ns * 512, 512),
                                start=(kt == 0), stop=(kt == KT - 1)),
                                r=[t["qt"], "wft"], w=[f"ps_ffn{ns}"], accum=(kt > 0))
                        C.op(S, lambda e, vt=vt, stt=stt, ns=ns: e.activation(
                            out=vt[:, ns * 512:(ns + 1) * 512],
                            in_=ps_ffn[:, ns * 512:(ns + 1) * 512], func=AF.Copy,
                            scale=stt["iscale"][:, 0:1]),
                            r=[f"ps_ffn{ns}", stt["iscale"]], w=[vt])

                def p1_seg3(c):
                    t, stt = tiles[c % 2]
                    own = c >= NCH
                    vt = x1cs[(c + 1) % 2]
                    C.op(V, lambda e, vt=vt: e.tensor_mul(out=vt[:], in0=vt[:],
                                                          in1=wsr[:]),
                         r=[vt, "wsr"], w=[vt])
                    C.op(V, lambda e, vt=vt: e.tensor_add(out=vt[:], in0=vt[:],
                                                          in1=bir[:]),
                         r=[vt, "bir"], w=[vt])
                    tt = dict(t)
                    tt["vv"] = vt
                    _f16_round_add(C, consts, tt, "vv", t["xt"][:], t["xt"],
                                   t["x1c"][:], t["x1c"])
                    C.op(DM, lambda e, c=c, t=t: e.dma_start(
                        out=x1_hbm[c * CH:(c + 1) * CH, :], in_=t["x1c"][:]),
                        r=[t["x1c"]], w=["x1_hbm"], dma=True)
                    if c + 2 < NALL:
                        p1_load(c + 2)
                    _ln_simple(C, consts, t, stt, t["x1c"][:], t["x1c"])
                    _split_hilo(C, t, t["h2"], t["h2"][:], t["h2hi"][:], t["h2hi"],
                                t["h2lo"][:] if own else None,
                                t["h2lo"] if own else None)
                    _transpose16(C, t, ident,
                                 lambda i, t=t: t["h2hi"][:, i * 128:(i + 1) * 128],
                                 t["h2hi"], ps_tr,
                                 lambda i, t=t: t["h2thi"][:, i * 128:(i + 1) * 128],
                                 t["h2thi"])
                    C.op(G, lambda e, c=c, t=t: e.dma_start(
                        out=h2t_hi[c], in_=t["h2thi"][:]),
                        r=[t["h2thi"]], w=["h2t_hi"], dma=True)
                    if own:
                        _transpose16(C, t, ident,
                                     lambda i, t=t: t["h2lo"][:, i * 128:(i + 1) * 128],
                                     t["h2lo"], ps_tr,
                                     lambda i, t=t: t["h2tlo"][:, i * 128:(i + 1) * 128],
                                     t["h2tlo"])
                        C.op(G, lambda e, c=c, t=t: e.dma_start(
                            out=h2t_lo[c - NCH], in_=t["h2tlo"][:]),
                            r=[t["h2tlo"]], w=["h2t_lo"], dma=True)

                p1_load(0)
                if NALL > 1:
                    p1_load(1)
                p1_seg1(0)
                for c in range(NALL):
                    p1_mm(c)
                    if c + 1 < NALL:
                        p1_seg1(c + 1)
                    p1_seg3(c)
                C.barrier()
                C.flush(block)

            if 2 in phases:
                _proj_phase(C, nc, block, dict(wq=wq, wk=wk, wv=wv), h2t_hi, h2t_lo,
                            vhi_hbm, vlo_hbm, kfm_hi, kfm_lo, qfm_hi, qfm_lo,
                            csq_in, ssq_in, csk_in, ssk_in, ident, consts)

            if 3 in phases:
                _attn_phase(C, nc, block, vhi_hbm, vlo_hbm, kfm_hi, kfm_lo,
                            qfm_hi, qfm_lo, down_in, dprev_in, oprev_hbm, oown_hbm)

            if 4 in phases:
                _ffn_post_phase(C, nc, block, wf8, x1_hbm, oprev_hbm, oown_hbm,
                                out, ident, consts, wsr, bir)
            else:
                with ExitStack() as es:
                    stage = es.enter_context(nc.sbuf_tensor("outstage", [128, H], F32))
                    for rr in range(0, NCH * CH, 128):
                        C.op("sync", lambda e, rr=rr: e.dma_start(
                            out=stage[:], in_=x1_hbm[NCH * CH + rr:NCH * CH + rr + 128, :]),
                            r=["x1_hbm"], w=["outstage"], dma=True)
                        C.op("sync", lambda e, rr=rr: e.dma_start(
                            out=out[rr:rr + 128, :], in_=stage[:]),
                            r=["outstage"], w=["out"], dma=True)
                    C.barrier()
                    C.flush(block)

            if dbg:
                with ExitStack() as es:
                    stage = es.enter_context(nc.sbuf_tensor("dbgstage", [128, H], F32))
                    srcs = {"x1": (x1_hbm, "x1_hbm", NALL * CH),
                            "oprev": (oprev_hbm, "oprev_hbm", NCH * CH),
                            "oown": (oown_hbm, "oown_hbm", NCH * CH)}
                    for name, dst in dbg.items():
                        srcap, srckey, n_rows = srcs[name]
                        for rr in range(0, n_rows, 128):
                            C.op(DM, lambda e, srcap=srcap, rr=rr: e.dma_start(
                                out=stage[:], in_=srcap[rr:rr + 128, :]),
                                r=[srckey], w=["dbgstage"], dma=True)
                            C.op(DM, lambda e, dst=dst, rr=rr: e.dma_start(
                                out=dst[rr:rr + 128, :], in_=stage[:]),
                                r=["dbgstage"], w=["dbg"], dma=True)
                    C.barrier()
                C.flush(block)

            C.barrier()
            C.flush(block)
    return nc


def _ln_simple(C, consts, t, stt, x_ap, xkey):
    V, S = "vector", "scalar"
    C.op(V, lambda e: e.reduce_sum(out=stt["msum"][:, 0:1], in_=x_ap, axis=AX.X),
         r=[xkey], w=[stt["msum"]])
    C.op(V, lambda e: e.tensor_scalar(out=stt["negm"][:, 0:1], in0=stt["msum"][:, 0:1],
                                      scalar1=-1.0 / H, scalar2=None, op0=AL.mult),
         r=[stt["msum"]], w=[stt["negm"]])
    C.op(V, lambda e: e.tensor_scalar_add(out=t["d"][:], in0=x_ap,
                                          scalar1=stt["negm"][:, 0:1]),
         r=[xkey, stt["negm"]], w=[t["d"]])
    C.op(S, lambda e: e.activation(out=t["sq"][:], in_=t["d"][:], func=AF.Square),
         r=[t["d"]], w=[t["sq"]])
    C.op(V, lambda e: e.reduce_sum(out=stt["vsum"][:, 0:1], in_=t["sq"][:], axis=AX.X),
         r=[t["sq"]], w=[stt["vsum"]])
    C.op(S, lambda e: e.activation(out=stt["lnv"][:, 0:1], in_=stt["vsum"][:, 0:1],
                                   func=AF.Ln, scale=consts["invH"],
                                   bias=consts["epsln"]),
         r=[stt["vsum"], "consts"], w=[stt["lnv"]])
    C.op(S, lambda e: e.activation(out=stt["rstd"][:, 0:1], in_=stt["lnv"][:, 0:1],
                                   func=AF.Exp, scale=consts["neghalf"]),
         r=[stt["lnv"], "consts"], w=[stt["rstd"]])
    C.op(V, lambda e: e.tensor_scalar_mul(out=t["h2"][:], in0=t["d"][:],
                                           scalar1=stt["rstd"][:, 0:1]),
         r=[t["d"], stt["rstd"]], w=[t["h2"]])


def _proj_phase(C, nc, block, ws, h2t_hi, h2t_lo, vhi_hbm, vlo_hbm,
                kfm_hi, kfm_lo, qfm_hi, qfm_lo, csq_in, ssq_in, csk_in, ssk_in,
                ident, consts):
    V, S, T, DM, G = "vector", "scalar", "tensor", "sync", "gpsimd"
    for wname in ("wv", "wk", "wq"):
        wrap = ws[wname]
        own_only = wname == "wq"
        nch = NCH if own_only else NALL
        with ExitStack() as es:
            sb = lambda name, shape, dt: es.enter_context(
                nc.sbuf_tensor(f"{wname}_{name}", shape, dt))
            whi = Split(nc, es, wname + "_whi", KT * H, 8 * H, BF16)
            wlo = Split(nc, es, wname + "_wlo", KT * H, 8 * H, BF16)
            wstage = [sb("wstage0", [128, H], F32)] * 2
            ps_tr = es.enter_context(nc.psum_tensor(f"{wname}_ps_tr", [128, 2048], BF16))
            ps_mm = es.enter_context(nc.psum_tensor(f"{wname}_ps_mm", [128, H], F32))
            for kt in range(KT):
                slot = kt % 2
                C.op(DM, lambda e, kt=kt, slot=slot: e.dma_start(
                    out=wstage[slot][:], in_=wrap[kt * 128:(kt + 1) * 128, :]),
                    w=[wstage[slot]], dma=True)
                C.op(S, lambda e, kt=kt, slot=slot: e.activation(
                    out=whi.ap(kt * H, H), in_=wstage[slot][:], func=AF.Copy),
                    r=[wstage[slot]], w=["whi" + wname])
                C.op(V, lambda e, kt=kt, slot=slot: e.tensor_sub(
                    out=wlo.ap(kt * H, H), in0=wstage[slot][:],
                    in1=whi.ap(kt * H, H)),
                    r=[wstage[slot], "whi" + wname], w=["wlo" + wname])
            t0 = {}
            t0["hthi"] = sb("hthi0", [128, H], BF16)
            t0["htlo"] = sb("htlo0", [128, H], BF16)
            t0["pf"] = sb("pf0", [128, H], F32)
            t0["xp2a"] = wstage[0][:, 0:H // 2]
            t0["xp2b"] = wstage[0][:, H // 2:H]
            t0["xp2key"] = wstage[0]
            t0["cs"] = sb("cs0", [128, H // 2], F32)
            t0["ss"] = sb("ss0", [128, H // 2], F32)
            t0["phi"] = [sb("phi0", [128, H], BF16), sb("phi1", [128, H], BF16)]
            t0["plo"] = [sb("plo0", [128, H], BF16), sb("plo1", [128, H], BF16)]
            tiles = [t0, t0]
            if wname in ("wk", "wq"):
                fmst_hi = sb("fmst_hi", [128, KT * 128], BF16)
                fmst_lo = sb("fmst_lo", [128, KT * 128], BF16)

            pending = []
            for ci in range(nch):
                c = ci + (NCH if own_only else 0)
                own = c >= NCH
                t = tiles[ci % 2]
                C.op(DM, lambda e, c=c, t=t: e.dma_start(out=t["hthi"][:], in_=h2t_hi[c]),
                     w=[t["hthi"]], dma=True)
                if own:
                    C.op(DM, lambda e, c=c, t=t: e.dma_start(out=t["htlo"][:],
                                                             in_=h2t_lo[c - NCH]),
                         w=[t["htlo"]], dma=True)
                for kt in range(KT):
                    first = kt == 0
                    last = kt == KT - 1
                    hi_stop = last and not own
                    for ns in range(4):
                        C.op(T, lambda e, t=t, kt=kt, ns=ns, first=first, hi_stop=hi_stop: e.matmul(
                            ps_mm[:, ns * 512:(ns + 1) * 512],
                            t["hthi"][:, kt * 128:(kt + 1) * 128],
                            whi.ap(kt * H + ns * 512, 512),
                            start=first, stop=hi_stop),
                            r=[t["hthi"], "whi" + wname], w=["psmm" + wname],
                            accum=not first)
                    if own:
                        for ns in range(4):
                            C.op(T, lambda e, t=t, kt=kt, ns=ns: e.matmul(
                                ps_mm[:, ns * 512:(ns + 1) * 512],
                                t["hthi"][:, kt * 128:(kt + 1) * 128],
                                wlo.ap(kt * H + ns * 512, 512),
                                start=False, stop=False),
                                r=[t["hthi"], "wlo" + wname], w=["psmm" + wname],
                                accum=True)
                        for ns in range(4):
                            C.op(T, lambda e, t=t, kt=kt, ns=ns, last=last: e.matmul(
                                ps_mm[:, ns * 512:(ns + 1) * 512],
                                t["htlo"][:, kt * 128:(kt + 1) * 128],
                                whi.ap(kt * H + ns * 512, 512),
                                start=False, stop=last),
                                r=[t["htlo"], "whi" + wname], w=["psmm" + wname],
                                accum=True)
                C.op(S, lambda e, t=t: e.activation(out=t["pf"][:], in_=ps_mm[:],
                                                    func=AF.Copy),
                     r=["psmm" + wname], w=[t["pf"]])
                if pending:
                    pending.pop(0)()
                phi = t["phi"][ci % 2]
                plo = t["plo"][ci % 2]
                if wname == "wv":
                    _split_hilo(C, t, t["pf"], t["pf"][:], phi[:], phi,
                                plo[:] if own else None,
                                plo if own else None)
                    C.op(G, lambda e, c=c, phi=phi: e.dma_start(out=vhi_hbm[c], in_=phi[:]),
                         r=[phi], w=["vhi_hbm"], dma=True)
                    if own:
                        C.op(G, lambda e, c=c, plo=plo: e.dma_start(out=vlo_hbm[c - NCH],
                                                                    in_=plo[:]),
                             r=[plo], w=["vlo_hbm"], dma=True)
                else:
                    if wname == "wk":
                        cs_src, ss_src = csk_in, ssk_in
                        row0 = c * CH
                    else:
                        cs_src, ss_src = csq_in, ssq_in
                        row0 = (c - NCH) * CH
                    C.op(DM, lambda e, t=t, cs_src=cs_src, row0=row0: e.dma_start(
                        out=t["cs"][:], in_=cs_src[row0:row0 + CH, :]),
                        w=[t["cs"]], dma=True)
                    C.op(DM, lambda e, t=t, ss_src=ss_src, row0=row0: e.dma_start(
                        out=t["ss"][:], in_=ss_src[row0:row0 + CH, :]),
                        w=[t["ss"]], dma=True)
                    _xpos_psum(C, t, t["pf"], t["pf"][:], t["cs"][:], t["ss"][:],
                               t["pf"][:], t["pf"])
                    _split_hilo(C, t, t["pf"], t["pf"][:], phi[:], phi,
                                plo[:] if own else None,
                                plo if own else None)

                    def _deferred(t=t, phi=phi, plo=plo, own=own, ci=ci):
                        _transpose16(C, t, ident,
                                     lambda i, phi=phi: phi[:, i * 128:(i + 1) * 128],
                                     phi, ps_tr,
                                     lambda i: fmst_hi[:, i * 128:(i + 1) * 128],
                                     "fmst_hi")
                        if own:
                            _transpose16(C, t, ident,
                                         lambda i, plo=plo: plo[:, i * 128:(i + 1) * 128],
                                         plo, ps_tr,
                                         lambda i: fmst_lo[:, i * 128:(i + 1) * 128],
                                         "fmst_lo")
                        g0 = ci * 128
                        dsts = (kfm_hi, kfm_lo) if wname == "wk" else (qfm_hi, qfm_lo)
                        for ft in range(KT):
                            C.op(G, lambda e, ft=ft, g0=g0, dsts=dsts: e.dma_start(
                                out=dsts[0][ft, :, g0:g0 + 128],
                                in_=fmst_hi[:, ft * 128:(ft + 1) * 128]),
                                r=["fmst_hi"], w=["kfmq_hi"], dma=True)
                            if own:
                                lo_g0 = g0 - (0 if own_only else NCH * CH)
                                C.op(G, lambda e, ft=ft, lo_g0=lo_g0, dsts=dsts: e.dma_start(
                                    out=dsts[1][ft, :, lo_g0:lo_g0 + 128],
                                    in_=fmst_lo[:, ft * 128:(ft + 1) * 128]),
                                    r=["fmst_lo"], w=["kfmq_lo"], dma=True)
                    pending.append(_deferred)
            for fn in pending:
                fn()
            pending = []
            C.barrier()
            C.flush(block)


def _attn_phase(C, nc, block, vhi_hbm, vlo_hbm, kfm_hi, kfm_lo, qfm_hi, qfm_lo,
                down_in, dprev_in, oprev_hbm, oown_hbm):
    V, S, T, DM, G = "vector", "scalar", "tensor", "sync", "gpsimd"
    with ExitStack() as es:
        sb = lambda name, shape, dt: es.enter_context(nc.sbuf_tensor("p3a_" + name, shape, dt))
        vprev = Split(nc, es, "p3a_vprev", NCH * H, 8 * H, BF16)
        qhi = sb("qhi", [128, KT * 512], BF16)
        ktile = [sb(f"ktile{s}", [128, KT * 128], BF16) for s in range(3)]
        atile = sb("atile", [128, 16 * 512], BF16)
        dtile = [sb(f"dtile{s}", [128, 512], F32) for s in range(2)]
        oev = [sb(f"oev{s}", [128, H], F32) for s in range(2)]
        ps_sc = es.enter_context(nc.psum_tensor("p3a_ps_sc", [128, 1024], F32))
        ps_out = es.enter_context(nc.psum_tensor("p3a_ps_out", [128, H], F32))
        for c in range(NCH):
            C.op(DM, lambda e, c=c: e.dma_start(out=vprev.ap(c * H, H),
                                                in_=vhi_hbm[c]),
                 w=[vprev.key(c * H)], dma=True)
        for qi in range(NCH * CH // 512):
            i0 = qi * 512
            for ft in range(KT):
                C.op(DM, lambda e, ft=ft, i0=i0: e.dma_start(
                    out=qhi[:, ft * 512:(ft + 1) * 512],
                    in_=qfm_hi[ft, :, i0:i0 + 512]),
                    w=["qhi"], dma=True)
            for jt in range(NCH):
                slot = jt % 3
                C.op(DM, lambda e, jt=jt, slot=slot: e.dma_start(
                    out=ktile[slot][:],
                    in_=kfm_hi[0:KT, :, jt * 128:(jt + 1) * 128].transpose([1, 0, 2])),
                    w=[ktile[slot]], dma=True)
                C.op(G, lambda e, jt=jt, i0=i0, slot=slot: e.dma_start(
                    out=dtile[slot % 2][:], in_=dprev_in[jt * 128:(jt + 1) * 128, i0:i0 + 512]),
                    w=[dtile[slot % 2]], dma=True)
                sc = ps_sc[:, (jt % 2) * 512:(jt % 2) * 512 + 512]
                for ft in range(KT):
                    C.op(T, lambda e, ft=ft, jt=jt, slot=slot, sc=sc: e.matmul(
                        sc, ktile[slot][:, ft * 128:(ft + 1) * 128],
                        qhi[:, ft * 512:(ft + 1) * 512],
                        start=(ft == 0), stop=(ft == KT - 1)),
                        r=[ktile[slot], "qhi"], w=[f"p3a_sc{jt % 2}"], accum=(ft > 0))
                C.op(V, lambda e, jt=jt, slot=slot, sc=sc: e.tensor_mul(
                    out=atile[:, jt * 512:(jt + 1) * 512], in0=sc, in1=dtile[slot % 2][:]),
                    r=[f"p3a_sc{jt % 2}", dtile[slot % 2]], w=["atile"])
            for it in range(4):
                for jt in range(NCH):
                    for vs in range(4):
                        C.op(T, lambda e, it=it, jt=jt, vs=vs: e.matmul(
                            ps_out[:, vs * 512:(vs + 1) * 512],
                            atile[:, jt * 512 + it * 128:jt * 512 + (it + 1) * 128],
                            vprev.ap(jt * H + vs * 512, 512),
                            start=(jt == 0), stop=(jt == NCH - 1)),
                            r=["atile", vprev.key(jt * H)], w=["p3a_out"], accum=(jt > 0))
                oslot = it % 2
                C.op(S, lambda e, oslot=oslot: e.activation(out=oev[oslot][:],
                                                            in_=ps_out[:], func=AF.Copy),
                     r=["p3a_out"], w=[oev[oslot]])
                C.op(DM, lambda e, qi=qi, it=it, oslot=oslot: e.dma_start(
                    out=oprev_hbm[qi * 512 + it * 128:qi * 512 + (it + 1) * 128, :],
                    in_=oev[oslot][:]),
                    r=[oev[oslot]], w=["oprev_hbm"], dma=True)
        C.barrier()
        C.flush(block)

    for vfh in range(2):
        with ExitStack() as es:
            sb = lambda name, shape, dt: es.enter_context(
                nc.sbuf_tensor(f"p3b{vfh}_" + name, shape, dt))
            vh_hi = Split(nc, es, f"p3b{vfh}_vh_hi", NCH * 1024, 8 * 1024, BF16)
            vh_lo = Split(nc, es, f"p3b{vfh}_vh_lo", NCH * 1024, 8 * 1024, BF16)
            qhi = sb("qhi", [128, KT * 256], BF16)
            qlo = sb("qlo", [128, KT * 256], BF16)
            k_hi = [sb(f"k_hi{s}", [128, KT * 128], BF16) for s in range(3)]
            k_lo = [sb(f"k_lo{s}", [128, KT * 128], BF16) for s in range(3)]
            a_hi = sb("a_hi", [128, NCH * 256], BF16)
            a_lo = sb("a_lo", [128, NCH * 256], BF16)
            sf32 = [sb(f"sf32{s}", [128, 256], F32) for s in range(2)]
            dtile = [sb(f"dtile{s}", [128, 256], F32) for s in range(2)]
            oev = [sb(f"oev{s}", [128, 1024], F32) for s in range(2)]
            ps_sc = es.enter_context(nc.psum_tensor(f"p3b{vfh}_ps_sc", [128, 1024], F32))
            ps_out = es.enter_context(nc.psum_tensor(f"p3b{vfh}_ps_out", [128, 2048], F32))
            for c in range(NCH):
                C.op(DM, lambda e, c=c, vfh=vfh: e.dma_start(
                    out=vh_hi.ap(c * 1024, 1024),
                    in_=vhi_hbm[NCH + c][:, vfh * 1024:(vfh + 1) * 1024]),
                    w=[vh_hi.key(c * 1024)], dma=True)
                C.op(DM, lambda e, c=c, vfh=vfh: e.dma_start(
                    out=vh_lo.ap(c * 1024, 1024),
                    in_=vlo_hbm[c][:, vfh * 1024:(vfh + 1) * 1024]),
                    w=[vh_lo.key(c * 1024)], dma=True)
            for qi in range(NCH * CH // 256):
                i0 = qi * 256
                for ft in range(KT):
                    C.op(DM, lambda e, ft=ft, i0=i0: e.dma_start(
                        out=qhi[:, ft * 256:(ft + 1) * 256],
                        in_=qfm_hi[ft, :, i0:i0 + 256]),
                        w=["qhi"], dma=True)
                    C.op(DM, lambda e, ft=ft, i0=i0: e.dma_start(
                        out=qlo[:, ft * 256:(ft + 1) * 256],
                        in_=qfm_lo[ft, :, i0:i0 + 256]),
                        w=["qlo"], dma=True)
                njt = 2 * qi + 2
                for jt in range(njt):
                    slot = jt % 3
                    C.op(DM, lambda e, jt=jt, slot=slot: e.dma_start(
                        out=k_hi[slot][:],
                        in_=kfm_hi[0:KT, :, NCH * CH + jt * 128:
                                   NCH * CH + (jt + 1) * 128].transpose([1, 0, 2])),
                        w=[k_hi[slot]], dma=True)
                    C.op(G, lambda e, jt=jt, slot=slot: e.dma_start(
                        out=k_lo[slot][:],
                        in_=kfm_lo[0:KT, :, jt * 128:(jt + 1) * 128].transpose([1, 0, 2])),
                        w=[k_lo[slot]], dma=True)
                    C.op(G, lambda e, jt=jt, i0=i0, slot=slot: e.dma_start(
                        out=dtile[slot % 2][:], in_=down_in[jt * 128:(jt + 1) * 128, i0:i0 + 256]),
                        w=[dtile[slot % 2]], dma=True)
                    sc = ps_sc[:, (jt % 2) * 512:(jt % 2) * 512 + 256]
                    for ft in range(KT):
                        C.op(T, lambda e, ft=ft, slot=slot, sc=sc: e.matmul(
                            sc, k_hi[slot][:, ft * 128:(ft + 1) * 128],
                            qhi[:, ft * 256:(ft + 1) * 256],
                            start=(ft == 0), stop=False),
                            r=[k_hi[slot], "qhi"], w=[f"p3b_sc{jt % 2}"], accum=(ft > 0))
                        C.op(T, lambda e, ft=ft, slot=slot, sc=sc: e.matmul(
                            sc, k_hi[slot][:, ft * 128:(ft + 1) * 128],
                            qlo[:, ft * 256:(ft + 1) * 256],
                            start=False, stop=False),
                            r=[k_hi[slot], "qlo"], w=[f"p3b_sc{jt % 2}"], accum=True)
                        C.op(T, lambda e, ft=ft, slot=slot, sc=sc: e.matmul(
                            sc, k_lo[slot][:, ft * 128:(ft + 1) * 128],
                            qhi[:, ft * 256:(ft + 1) * 256],
                            start=False, stop=(ft == KT - 1)),
                            r=[k_lo[slot], "qhi"], w=[f"p3b_sc{jt % 2}"], accum=True)
                    C.op(V, lambda e, slot=slot, sc=sc: e.tensor_mul(
                        out=sf32[slot % 2][:], in0=sc, in1=dtile[slot % 2][:]),
                        r=[f"p3b_sc{jt % 2}", dtile[slot % 2]], w=[sf32[slot % 2]])
                    C.op(S, lambda e, jt=jt, slot=slot: e.activation(
                        out=a_hi[:, jt * 256:(jt + 1) * 256], in_=sf32[slot % 2][:],
                        func=AF.Copy),
                        r=[sf32[slot % 2]], w=["a_hi"])
                    C.op(V, lambda e, jt=jt, slot=slot: e.tensor_sub(
                        out=a_lo[:, jt * 256:(jt + 1) * 256], in0=sf32[slot % 2][:],
                        in1=a_hi[:, jt * 256:(jt + 1) * 256]),
                        r=[sf32[slot % 2], "a_hi"], w=["a_lo"])
                for it in range(2):
                    for jt in range(njt):
                        a_h = a_hi[:, jt * 256 + it * 128:jt * 256 + (it + 1) * 128]
                        a_l = a_lo[:, jt * 256 + it * 128:jt * 256 + (it + 1) * 128]
                        for vs in range(2):
                            vh = vh_hi.ap(jt * 1024 + vs * 512, 512)
                            vl = vh_lo.ap(jt * 1024 + vs * 512, 512)
                            vhk = vh_hi.key(jt * 1024)
                            vlk = vh_lo.key(jt * 1024)
                            po = ps_out[:, (it * 1024) + vs * 512:(it * 1024) + (vs + 1) * 512]
                            C.op(T, lambda e, a_h=a_h, vh=vh, po=po, jt=jt: e.matmul(
                                po, a_h, vh, start=(jt == 0), stop=False),
                                r=["a_hi", vhk], w=[f"p3b_out{it}"], accum=(jt > 0))
                            C.op(T, lambda e, a_h=a_h, vl=vl, po=po: e.matmul(
                                po, a_h, vl, start=False, stop=False),
                                r=["a_hi", vlk], w=[f"p3b_out{it}"], accum=True)
                            C.op(T, lambda e, a_l=a_l, vh=vh, po=po, jt=jt, njt=njt: e.matmul(
                                po, a_l, vh, start=False, stop=(jt == njt - 1)),
                                r=["a_lo", vhk], w=[f"p3b_out{it}"], accum=True)
                    oslot = it % 2
                    C.op(S, lambda e, it=it, oslot=oslot: e.activation(
                        out=oev[oslot][:], in_=ps_out[:, it * 1024:(it + 1) * 1024],
                        func=AF.Copy),
                        r=[f"p3b_out{it}"], w=[oev[oslot]])
                    C.op(DM, lambda e, qi=qi, it=it, vfh=vfh, oslot=oslot: e.dma_start(
                        out=oown_hbm[qi * 256 + it * 128:qi * 256 + (it + 1) * 128,
                                     vfh * 1024:(vfh + 1) * 1024],
                        in_=oev[oslot][:]),
                        r=[oev[oslot]], w=["oown_hbm"], dma=True)
            C.barrier()
            C.flush(block)


def _ffn_post_phase(C, nc, block, wf8, x1_hbm, oprev_hbm, oown_hbm, out,
                    ident, consts, wsr, bir):
    V, S, T, DM, G = "vector", "scalar", "tensor", "sync", "gpsimd"
    with ExitStack() as es:
        sb = lambda name, shape, dt: es.enter_context(nc.sbuf_tensor("p4_" + name, shape, dt))
        wft = Split(nc, es, "p4_wft", KT * H, 8 * H, BF16)
        wstage = [sb(f"wstage{s}", [128, H], I8) for s in range(2)]
        wfbf = sb("wfbf", [128, H], BF16)
        ps_tr = es.enter_context(nc.psum_tensor("p4_ps_tr", [128, 2048], BF16))
        ps_ffn = es.enter_context(nc.psum_tensor("p4_ps_ffn", [128, H], F32))
        for ntile in range(KT):
            slot = ntile % 2
            C.op(DM, lambda e, ntile=ntile, slot=slot: e.dma_start(
                out=wstage[slot][:], in_=wf8[ntile * 128:(ntile + 1) * 128, :]),
                w=[wstage[slot]], dma=True)
            C.op(V, lambda e, slot=slot: e.tensor_copy(out=wfbf[:], in_=wstage[slot][:]),
                 r=[wstage[slot]], w=["wfbf4"])
            _transpose16(C, None, ident,
                         lambda kt, ntile=ntile: wfbf[:, kt * 128:(kt + 1) * 128],
                         "wfbf4", ps_tr,
                         lambda kt, ntile=ntile: wft.ap(kt * H + ntile * 128, 128),
                         "wft4")
        scr = [sb(f"s{i}", [128, H], F32) for i in range(5)]
        i32a = sb("i32a", [128, H], I32)
        i32b = sb("i32b", [128, H], I32)
        qbf = wfbf
        u1b = sb("u1", [128, H], BF16)
        u2b = sb("u2", [128, H], BF16)
        x3c_t = sb("x3c", [128, H], F32)
        x1cs = [sb(f"x1c{s}", [128, H], F32) for s in range(2)]
        ops = [sb("op0", [128, H], F32)] * 2
        oos = [sb("oo0", [128, H], F32)] * 2
        qts = [sb("qt0", [128, H], BF16)] * 2
        stts = []
        for s in range(2):
            stt = {}
            for nm in ("msum", "negm", "vsum", "lnv", "rstd", "dmax",
                       "amax", "am2", "rec", "rec127", "fac", "iscale"):
                stt[nm] = sb(f"st_{nm}{s}", [128, 1], F32)
            stts.append(stt)
        tiles = []
        for s in range(2):
            t = dict(d=scr[0], sq=scr[1], y=scr[2], dd=scr[1], rf=scr[3],
                     tq=scr[2],
                     fa=scr[0], fa2=scr[1], flog=scr[0], fu=scr[1],
                     fuc=scr[0],
                     fum=scr[4], frf=scr[0], fdd=scr[1], fr1=scr[3],
                     fr2=scr[0], f0=scr[1],
                     ri=i32a, i32a=i32a, i32b=i32b,
                     q=qbf, u1=u1b, u2=u2b, fu1=u1b, fu2=u2b,
                     x3c=x3c_t,
                     x1c=x1cs[s], op_=ops[s], oo_=oos[s], qt=qts[s])
            tiles.append((t, stts[s]))

        def p4_load_x(c):
            t, _ = tiles[c % 2]
            C.op(DM, lambda e, t=t, c=c: e.dma_start(
                out=t["x1c"][:],
                in_=x1_hbm[NCH * CH + c * CH:NCH * CH + (c + 1) * CH, :]),
                w=[t["x1c"]], dma=True)

        def p4_load_o(c):
            t, _ = tiles[c % 2]
            C.op(DM, lambda e, t=t, c=c: e.dma_start(
                out=t["op_"][:], in_=oprev_hbm[c * CH:(c + 1) * CH, :]),
                w=[t["op_"]], dma=True)
            C.op(G, lambda e, t=t, c=c: e.dma_start(
                out=t["oo_"][:], in_=oown_hbm[c * CH:(c + 1) * CH, :]),
                w=[t["oo_"]], dma=True)

        def p4_seg1(c):
            t, stt = tiles[c % 2]
            C.op(V, lambda e, t=t: e.tensor_add(out=t["x1c"][:], in0=t["x1c"][:],
                                                in1=t["op_"][:]),
                 r=[t["x1c"], t["op_"]], w=[t["x1c"]])
            C.op(V, lambda e, t=t: e.tensor_add(out=t["x1c"][:], in0=t["x1c"][:],
                                                in1=t["oo_"][:]),
                 r=[t["x1c"], t["oo_"]], w=[t["x1c"]])
            if c + 1 < NCH:
                p4_load_o(c + 1)
            _ln_quant(C, consts, t["x1c"][:], t["x1c"], t, stt, want_h2=False)
            _transpose16(C, t, ident,
                         lambda i, t=t: t["q"][:, i * 128:(i + 1) * 128],
                         t["q"], ps_tr,
                         lambda i, t=t: t["qt"][:, i * 128:(i + 1) * 128],
                         t["qt"])

        def p4_mm(c):
            t, stt = tiles[c % 2]
            for ns in range(4):
                for kt in range(KT):
                    C.op(T, lambda e, t=t, kt=kt, ns=ns: e.matmul(
                        ps_ffn[:, ns * 512:(ns + 1) * 512],
                        t["qt"][:, kt * 128:(kt + 1) * 128],
                        wft.ap(kt * H + ns * 512, 512),
                        start=(kt == 0), stop=(kt == KT - 1)),
                        r=[t["qt"], "wft4"], w=[f"ps_ffn4{ns}"], accum=(kt > 0))
                C.op(S, lambda e, stt=stt, ns=ns: e.activation(
                    out=x3c_t[:, ns * 512:(ns + 1) * 512],
                    in_=ps_ffn[:, ns * 512:(ns + 1) * 512], func=AF.Copy,
                    scale=stt["iscale"][:, 0:1]),
                    r=[f"ps_ffn4{ns}", stt["iscale"]], w=[x3c_t])

        def p4_seg3(c):
            t, stt = tiles[c % 2]
            r0 = c * CH
            C.op(V, lambda e: e.tensor_mul(out=x3c_t[:], in0=x3c_t[:], in1=wsr[:]),
                 r=[x3c_t, "wsr"], w=[x3c_t])
            C.op(V, lambda e: e.tensor_add(out=x3c_t[:], in0=x3c_t[:], in1=bir[:]),
                 r=[x3c_t, "bir"], w=[x3c_t])
            tt = dict(t)
            tt["vv"] = x3c_t
            _f16_round_add(C, consts, tt, "vv", t["x1c"][:], t["x1c"],
                           x3c_t[:], x3c_t)
            C.op(DM, lambda e, r0=r0: e.dma_start(
                out=out[r0:r0 + CH, :], in_=x3c_t[:]),
                r=[x3c_t], w=["out"], dma=True)
            if c + 2 < NCH:
                p4_load_x(c + 2)

        p4_load_x(0)
        p4_load_o(0)
        if NCH > 1:
            p4_load_x(1)
        p4_seg1(0)
        for c in range(NCH):
            p4_mm(c)
            if c + 1 < NCH:
                p4_seg1(c + 1)
            p4_seg3(c)
        C.barrier()
        C.flush(block)



def _prep_inputs(L0_attn_X, L0_attn_W_Q, L0_attn_W_K, L0_attn_W_V,
                 L0_ffn_weight, L0_ffn_w_scale, L0_ffn_bias):
    tb = _host_tables()
    X = np.ascontiguousarray(np.asarray(L0_attn_X, dtype=np.float32))
    wq = np.ascontiguousarray(np.asarray(L0_attn_W_Q, dtype=np.float32))
    wk = np.ascontiguousarray(np.asarray(L0_attn_W_K, dtype=np.float32))
    wv = np.ascontiguousarray(np.asarray(L0_attn_W_V, dtype=np.float32))
    wf8 = np.ascontiguousarray(np.asarray(L0_ffn_weight, dtype=np.int8))
    ws = np.asarray(L0_ffn_w_scale, dtype=np.float32).reshape(-1)
    bias = np.asarray(L0_ffn_bias, dtype=np.float32).reshape(-1)
    wsrow = np.ascontiguousarray(np.broadcast_to(ws[None, :], (128, H)))
    biasrow = np.ascontiguousarray(np.broadcast_to(bias[None, :], (128, H)))
    ident = np.eye(128, dtype=ml_dtypes.bfloat16)
    zeros_slice = np.zeros((PREV, H), dtype=np.float32)
    zeros_tab = np.zeros((PREV, H // 2), dtype=np.float32)
    in_maps = []
    for core in range(8):
        b, q = divmod(core, 4)
        own = X[b, q * TOK:(q + 1) * TOK]
        prev = X[b, (q - 1) * TOK:q * TOK] if q > 0 else zeros_slice
        x0both = np.ascontiguousarray(np.concatenate([prev, own], axis=0))
        r0 = q * TOK
        csq = np.ascontiguousarray(tb["csq"][r0:r0 + TOK])
        ssq = np.ascontiguousarray(tb["ssq"][r0:r0 + TOK])
        if q > 0:
            csk = np.ascontiguousarray(tb["csk"][r0 - PREV:r0 + TOK])
            ssk = np.ascontiguousarray(tb["ssk"][r0 - PREV:r0 + TOK])
            dprev = tb["d_prev_T"]
        else:
            csk = np.ascontiguousarray(
                np.concatenate([zeros_tab, tb["csk"][r0:r0 + TOK]], axis=0))
            ssk = np.ascontiguousarray(
                np.concatenate([zeros_tab, tb["ssk"][r0:r0 + TOK]], axis=0))
            dprev = tb["d_zero_T"]
        in_maps.append({
            "x0both": x0both, "wq": wq, "wk": wk, "wv": wv, "wf8": wf8,
            "wsrow": wsrow, "biasrow": biasrow, "ident": ident,
            "csq": csq, "ssq": ssq, "csk": csk, "ssk": ssk,
            "d_own_T": tb["d_own_T"], "d_prev_T": dprev,
        })
    return in_maps


def kernel(L0_attn_X, L0_attn_W_Q, L0_attn_W_K, L0_attn_W_V,
           L0_ffn_weight, L0_ffn_w_scale, L0_ffn_bias):
    in_maps = _prep_inputs(L0_attn_X, L0_attn_W_Q, L0_attn_W_K, L0_attn_W_V,
                           L0_ffn_weight, L0_ffn_w_scale, L0_ffn_bias)
    nc = build_program()
    res = run_bass_kernel_spmd(nc, in_maps, core_ids=list(range(8)))
    out = np.empty((2, 8192, H), dtype=np.float32)
    for core in range(8):
        b, q = divmod(core, 4)
        out[b, q * TOK:(q + 1) * TOK] = np.asarray(res.results[core]["out"],
                                                   dtype=np.float32)
    return out


# version 05 (final)
# speedup vs baseline: 1.2338x; 1.0137x over previous
import math
import os
from contextlib import ExitStack

import numpy as np
import ml_dtypes

import concourse.bass as bass
import concourse.mybir as mybir
from concourse.bass_utils import run_bass_kernel_spmd

F32 = mybir.dt.float32
BF16 = mybir.dt.bfloat16
F16 = mybir.dt.float16
I8 = mybir.dt.int8
I32 = mybir.dt.int32
AF = mybir.ActivationFunctionType
AL = mybir.AluOpType
AX = mybir.AxisListType

H = 2048
TOK = 2048
PREV = 2048
CH = 128
NCH = TOK // CH
NALL = (TOK + PREV) // CH
KT = H // 128
GAMMA = 0.96875
LN2 = math.log(2.0)

ENGINES = ("sync", "gpsimd", "scalar", "vector", "tensor")
_BLOCK_ATTR = {"sync": "sync", "gpsimd": "gpsimd", "scalar": "scalar",
               "vector": "vector", "tensor": "tensor"}



_TABLES = {}


def _host_tables():
    if _TABLES:
        return _TABLES
    import jax
    import jax.numpy as jnp
    cpu = jax.devices("cpu")[0]
    S = 8192
    with jax.default_device(cpu):
        half = H // 2
        scale_vec = (jnp.arange(0, H, 2, dtype=jnp.float32) + 0.4 * H) / (1.4 * H)
        power = jnp.arange(S, dtype=jnp.float32) / 512
        scale = scale_vec[None, :] ** power[:, None]
        inv_freq = 1.0 / 10000.0 ** (jnp.arange(half, dtype=jnp.float32) / half)
        sinusoid = jnp.arange(S, dtype=jnp.float32)[:, None] * inv_freq[None, :]
        sin, cos = jnp.sin(sinusoid), jnp.cos(sinusoid)
        ssq = np.asarray(sin * scale, dtype=np.float32)
        csq = np.asarray(cos * scale, dtype=np.float32)
        dscale = 1.0 / scale
        ssk = np.asarray(sin * dscale, dtype=np.float32)
        csk = np.asarray(cos * dscale, dtype=np.float32)
        n = jnp.arange(TOK)[:, None]
        m = jnp.arange(TOK)[None, :]
        dmat = jnp.where(n >= m, GAMMA ** jnp.maximum(n - m, 0).astype(jnp.float32), 0.0)
        d_own_T = np.asarray(dmat, dtype=np.float32).T.copy()
        n2 = jnp.arange(TOK)[:, None]
        m2 = jnp.arange(-PREV, 0)[None, :]
        dmat2 = jnp.where(n2 >= m2, GAMMA ** jnp.maximum(n2 - m2, 0).astype(jnp.float32), 0.0)
        d_prev_T = np.asarray(dmat2, dtype=np.float32).T.copy()
    _TABLES.update(dict(ssq=ssq, csq=csq, ssk=ssk, csk=csk,
                        d_own_T=np.ascontiguousarray(d_own_T),
                        d_prev_T=np.ascontiguousarray(d_prev_T),
                        d_zero_T=np.zeros_like(d_prev_T)))
    return _TABLES



class Ctx:

    N_DMA_SEMS = {"sync": 36, "gpsimd": 36}

    def __init__(self, nc, es):
        self.nc = nc
        self.sems = {e: es.enter_context(nc.semaphore("prog_" + e)) for e in ENGINES}
        self.semid = {e: ("e", e) for e in ENGINES}
        self.cnt = {e: 0 for e in ENGINES}
        self.hwm = {e: {} for e in ENGINES}
        self.state = {}
        self.prog = {e: [] for e in ENGINES}
        self.dpool = {}
        self.dcnt = {}
        self.dcons = {}
        self.dnext = {}
        for e, n in self.N_DMA_SEMS.items():
            self.dpool[e] = [es.enter_context(nc.semaphore(f"dma_{e}{i}"))
                             for i in range(n)]
            self.dnext[e] = 0
        self.semobj = {("e", e): self.sems[e] for e in ENGINES}
        for e, pool in self.dpool.items():
            for i, s in enumerate(pool):
                self.semobj[("d", e, i)] = s
                self.dcnt[("d", e, i)] = 0
                self.dcons[("d", e, i)] = True

    @staticmethod
    def _key(t):
        return t if isinstance(t, str) else t.name

    def _want(self, eng, reads, writes, skip_waw_engine=None):
        need = {}
        for t in reads:
            st = self.state.get(self._key(t))
            if st and st["w"]:
                sk, c = st["w"]
                need[sk] = max(need.get(sk, 0), c)
        for t in writes:
            st = self.state.get(self._key(t))
            if st:
                if st["w"] and not (skip_waw_engine and st["w"][0] == ("e", skip_waw_engine)):
                    sk, c = st["w"]
                    need[sk] = max(need.get(sk, 0), c)
                for sk, c in st["r"].items():
                    need[sk] = max(need.get(sk, 0), c)
        out = []
        for sk, c in need.items():
            if self.hwm[eng].get(sk, 0) < c:
                out.append((sk, c))
                self.hwm[eng][sk] = c
                if sk[0] == "d":
                    self.dcons[sk] = True
        return out

    def _emit(self, eng, fn, esems, sem, inc):
        def closure(engobj, fn=fn, esems=esems, sem=sem, inc=inc):
            for s, c in esems:
                engobj.wait_ge(s, c)
            ins = fn(engobj)
            if sem is not None:
                ins.then_inc(sem, inc)
        self.prog[eng].append(closure)

    def op(self, eng, fn, r=(), w=(), dma=False, accum=False):
        skip = eng if accum else None
        edges = self._want(eng, r, w, skip_waw_engine=skip)
        esems = [(self.semobj[sk], c) for sk, c in edges]
        if dma:
            pool = self.dpool[eng]
            i = self.dnext[eng]
            self.dnext[eng] = (i + 1) % len(pool)
            sk = ("d", eng, i)
            if not self.dcons[sk]:
                if self.hwm[eng].get(sk, 0) < self.dcnt[sk]:
                    esems.append((self.semobj[sk], self.dcnt[sk]))
                    self.hwm[eng][sk] = self.dcnt[sk]
            self.dcnt[sk] += 16
            self.dcons[sk] = False
            myc = self.dcnt[sk]
            self._emit(eng, fn, esems, self.semobj[sk], 16)
        else:
            sk = ("e", eng)
            self.cnt[eng] += 1
            myc = self.cnt[eng]
            self._emit(eng, fn, esems, self.sems[eng], 1)
        for t in r:
            st = self.state.setdefault(self._key(t), {"w": None, "r": {}})
            st["r"][sk] = myc
        for t in w:
            st = self.state.setdefault(self._key(t), {"w": None, "r": {}})
            st["w"] = (sk, myc)
            st["r"] = {}
        return sk, myc

    def wait_all(self, eng):
        esems = []
        for f in ENGINES:
            if f == eng:
                continue
            sk = ("e", f)
            c = self.cnt[f]
            if c > self.hwm[eng].get(sk, 0):
                self.hwm[eng][sk] = c
                esems.append((self.semobj[sk], c))
        for sk, c in self.dcnt.items():
            if c > self.hwm[eng].get(sk, 0):
                self.hwm[eng][sk] = c
                self.dcons[sk] = True
                esems.append((self.semobj[sk], c))
        if esems:
            def closure(engobj, esems=esems):
                for s, c in esems:
                    engobj.wait_ge(s, c)
            self.prog[eng].append(closure)

    def barrier(self):
        for e in ENGINES:
            self.wait_all(e)
        self.state = {}

    def flush(self, block):
        for e in ENGINES:
            lst = self.prog[e]
            self.prog[e] = []
            if lst:
                getattr(block, _BLOCK_ATTR[e])(lambda engobj, lst=lst: [f(engobj) for f in lst])


class Split:

    def __init__(self, nc, es, name, total_cols, cols_per_part, dtype):
        cols_per_part = min(cols_per_part, total_cols)
        assert total_cols % cols_per_part == 0
        self.cpp = cols_per_part
        self.t = [es.enter_context(nc.sbuf_tensor(f"{name}_{i}", [128, cols_per_part], dtype))
                  for i in range(total_cols // cols_per_part)]

    def ap(self, c0, w):
        i = c0 // self.cpp
        o = c0 % self.cpp
        assert o + w <= self.cpp, (c0, w, self.cpp)
        return self.t[i][:, o:o + w]

    def key(self, c0):
        return self.t[c0 // self.cpp].name



def _ln_quant(C, consts, x_ap, xkey, t, st, want_h2=False):
    V, S = "vector", "scalar"
    C.op(V, lambda e: e.reduce_sum(out=st["msum"][:, 0:1], in_=x_ap, axis=AX.X),
         r=[xkey], w=[st["msum"]])
    C.op(V, lambda e: e.tensor_scalar(out=st["negm"][:, 0:1], in0=st["msum"][:, 0:1],
                                      scalar1=-1.0 / H, scalar2=None, op0=AL.mult),
         r=[st["msum"]], w=[st["negm"]])
    C.op(V, lambda e: e.tensor_scalar_add(out=t["d"][:], in0=x_ap,
                                          scalar1=st["negm"][:, 0:1]),
         r=[xkey, st["negm"]], w=[t["d"]])
    C.op(S, lambda e: e.activation(out=t["sq"][:], in_=t["d"][:], func=AF.Square),
         r=[t["d"]], w=[t["sq"]])
    C.op(V, lambda e: e.reduce_sum(out=st["vsum"][:, 0:1], in_=t["sq"][:], axis=AX.X),
         r=[t["sq"]], w=[st["vsum"]])
    C.op(S, lambda e: e.activation(out=st["lnv"][:, 0:1], in_=st["vsum"][:, 0:1],
                                   func=AF.Ln, scale=consts["invH"],
                                   bias=consts["epsln"]),
         r=[st["vsum"], "consts"], w=[st["lnv"]])
    C.op(S, lambda e: e.activation(out=st["rstd"][:, 0:1], in_=st["lnv"][:, 0:1],
                                   func=AF.Exp, scale=consts["neghalf"]),
         r=[st["lnv"], "consts"], w=[st["rstd"]])
    C.op(V, lambda e: e.reduce_max(out=st["dmax"][:, 0:1], in_=t["d"][:], axis=AX.X,
                                   apply_absolute_value=True),
         r=[t["d"]], w=[st["dmax"]])
    C.op(V, lambda e: e.tensor_mul(out=st["amax"][:, 0:1], in0=st["dmax"][:, 0:1],
                                   in1=st["rstd"][:, 0:1]),
         r=[st["dmax"], st["rstd"]], w=[st["amax"]])
    C.op(V, lambda e: e.tensor_scalar_max(out=st["am2"][:, 0:1], in0=st["amax"][:, 0:1],
                                          scalar1=1e-7),
         r=[st["amax"]], w=[st["am2"]])
    C.op(V, lambda e: e.reciprocal(out=st["rec"][:, 0:1], in_=st["am2"][:, 0:1]),
         r=[st["am2"]], w=[st["rec"]])
    C.op(V, lambda e: e.tensor_scalar(out=st["iscale"][:, 0:1], in0=st["am2"][:, 0:1],
                                      scalar1=1.0 / 127.0, scalar2=None, op0=AL.mult),
         r=[st["am2"]], w=[st["iscale"]])
    C.op(V, lambda e: e.tensor_scalar(out=st["rec127"][:, 0:1], in0=st["rec"][:, 0:1],
                                      scalar1=127.0, scalar2=None, op0=AL.mult),
         r=[st["rec"]], w=[st["rec127"]])
    C.op(V, lambda e: e.tensor_mul(out=st["fac"][:, 0:1], in0=st["rec127"][:, 0:1],
                                   in1=st["rstd"][:, 0:1]),
         r=[st["rec127"], st["rstd"]], w=[st["fac"]])
    C.op(S, lambda e: e.activation(out=t["y"][:], in_=t["d"][:], func=AF.Copy,
                                   scale=st["fac"][:, 0:1]),
         r=[t["d"], st["fac"]], w=[t["y"]])
    if want_h2:
        C.op(V, lambda e: e.tensor_scalar_mul(out=t["h2"][:], in0=t["d"][:],
                                              scalar1=st["rstd"][:, 0:1]),
             r=[t["d"], st["rstd"]], w=[t["h2"]])
    C.op(V, lambda e: e.tensor_copy(out=t["ri"][:], in_=t["y"][:]),
         r=[t["y"]], w=[t["ri"]])
    C.op(V, lambda e: e.tensor_copy(out=t["rf"][:], in_=t["ri"][:]),
         r=[t["ri"]], w=[t["rf"]])
    C.op(V, lambda e: e.tensor_sub(out=t["dd"][:], in0=t["y"][:], in1=t["rf"][:]),
         r=[t["y"], t["rf"]], w=[t["dd"]])
    C.op(V, lambda e: e.tensor_scalar(out=t["u1"][:], in0=t["dd"][:], scalar1=0.5,
                                      scalar2=None, op0=AL.is_gt),
         r=[t["dd"]], w=[t["u1"]])
    C.op(V, lambda e: e.tensor_scalar(out=t["u2"][:], in0=t["dd"][:], scalar1=-0.5,
                                      scalar2=None, op0=AL.is_lt),
         r=[t["dd"]], w=[t["u2"]])
    C.op(V, lambda e: e.scalar_tensor_tensor(out=t["tq"][:], in0=t["u1"][:], scalar=1.0,
                                             in1=t["rf"][:], op0=AL.mult, op1=AL.add),
         r=[t["u1"], t["rf"]], w=[t["tq"]])
    C.op(V, lambda e: e.tensor_sub(out=t["q"][:], in0=t["tq"][:], in1=t["u2"][:]),
         r=[t["tq"], t["u2"]], w=[t["q"]])


def _f16_round_add(C, consts, t, vkey, xin_ap, xin_key, xout_ap, xout_key):
    V, S = "vector", "scalar"
    v = t[vkey]
    i32a, i32b = t["i32a"], t["i32b"]
    bca = i32a[:].bitcast(F32)
    bcb = i32b[:].bitcast(F32)
    C.op(S, lambda e: e.activation(out=t["fa"][:], in_=v[:], func=AF.Abs),
         r=[v], w=[t["fa"]])
    C.op(S, lambda e: e.activation(out=t["flog"][:], in_=t["fa"][:], func=AF.Ln,
                                   bias=consts["tiny"]),
         r=[t["fa"], "consts"], w=[t["flog"]])
    C.op(V, lambda e: e.tensor_scalar(out=t["fu"][:], in0=t["flog"][:],
                                      scalar1=1.0 / LN2, scalar2=14.0,
                                      op0=AL.mult, op1=AL.add),
         r=[t["flog"]], w=[t["fu"]])
    C.op(V, lambda e: e.tensor_scalar(out=t["fuc"][:], in0=t["fu"][:],
                                      scalar1=0.0, scalar2=22.9,
                                      op0=AL.max, op1=AL.min),
         r=[t["fu"]], w=[t["fuc"]])
    C.op(V, lambda e: e.tensor_copy(out=i32a[:], in_=t["fuc"][:]),
         r=[t["fuc"]], w=[i32a])
    C.op(V, lambda e: e.tensor_scalar(out=i32b[:], in0=i32a[:],
                                      scalar1=-8388608, scalar2=151 * 8388608,
                                      op0=AL.mult, op1=AL.add),
         r=[i32a], w=[i32b])
    C.op(V, lambda e: e.tensor_scalar(out=i32a[:], in0=i32a[:],
                                      scalar1=8388608, scalar2=102 * 8388608,
                                      op0=AL.mult, op1=AL.add),
         r=[i32a], w=[i32a])
    C.op(V, lambda e: e.tensor_mul(out=t["fum"][:], in0=v[:], in1=bcb),
         r=[v, i32b], w=[t["fum"]])
    C.op(V, lambda e: e.tensor_copy(out=i32b[:], in_=t["fum"][:]),
         r=[t["fum"]], w=[i32b])
    C.op(V, lambda e: e.tensor_copy(out=t["frf"][:], in_=i32b[:]),
         r=[i32b], w=[t["frf"]])
    C.op(V, lambda e: e.tensor_sub(out=t["fdd"][:], in0=t["fum"][:], in1=t["frf"][:]),
         r=[t["fum"], t["frf"]], w=[t["fdd"]])
    C.op(V, lambda e: e.tensor_scalar(out=t["fu1"][:], in0=t["fdd"][:], scalar1=0.5,
                                      scalar2=None, op0=AL.is_gt),
         r=[t["fdd"]], w=[t["fu1"]])
    C.op(V, lambda e: e.tensor_scalar(out=t["fu2"][:], in0=t["fdd"][:], scalar1=-0.5,
                                      scalar2=None, op0=AL.is_lt),
         r=[t["fdd"]], w=[t["fu2"]])
    C.op(V, lambda e: e.scalar_tensor_tensor(out=t["fr1"][:], in0=t["fu1"][:], scalar=1.0,
                                             in1=t["frf"][:], op0=AL.mult, op1=AL.add),
         r=[t["fu1"], t["frf"]], w=[t["fr1"]])
    C.op(V, lambda e: e.tensor_sub(out=t["fr2"][:], in0=t["fr1"][:], in1=t["fu2"][:]),
         r=[t["fr1"], t["fu2"]], w=[t["fr2"]])
    C.op(V, lambda e: e.tensor_mul(out=t["f0"][:], in0=t["fr2"][:], in1=bca),
         r=[t["fr2"], i32a], w=[t["f0"]])
    C.op(V, lambda e: e.tensor_add(out=xout_ap, in0=xin_ap, in1=t["f0"][:]),
         r=[xin_key, t["f0"]], w=[xout_key])


def _xpos(C, t, src_key, src_ap, cs_ap, ss_ap, tab_keys, out_ap, out_key):
    V = "vector"
    csk, ssk = tab_keys
    xe = src_ap[:, 0:H:2]
    xo = src_ap[:, 1:H:2]
    C.op(V, lambda e: e.tensor_mul(out=t["xp1"][:], in0=xe, in1=cs_ap),
         r=[src_key, csk], w=[t["xp1"]])
    C.op(V, lambda e: e.tensor_mul(out=t["xp2"][:], in0=xo, in1=ss_ap),
         r=[src_key, ssk], w=[t["xp2"]])
    C.op(V, lambda e: e.tensor_sub(out=out_ap[:, 0:H:2], in0=t["xp1"][:], in1=t["xp2"][:]),
         r=[t["xp1"], t["xp2"]], w=[out_key])
    C.op(V, lambda e: e.tensor_mul(out=t["xp1"][:], in0=xo, in1=cs_ap),
         r=[src_key, csk, out_key], w=[t["xp1"]])
    C.op(V, lambda e: e.tensor_mul(out=t["xp2"][:], in0=xe, in1=ss_ap),
         r=[src_key, ssk], w=[t["xp2"]])
    C.op(V, lambda e: e.tensor_add(out=out_ap[:, 1:H:2], in0=t["xp1"][:], in1=t["xp2"][:]),
         r=[t["xp1"], t["xp2"]], w=[out_key])


def _xpos_psum(C, t, ps_key, ps_ap, cs_ap, ss_ap, out_ap, out_key):
    V = "vector"
    xe = ps_ap[:, 0:H:2]
    xo = ps_ap[:, 1:H:2]
    s1 = t["xp2a"]
    s2 = t["xp2b"]
    sk = t["xp2key"]
    C.op(V, lambda e: e.tensor_mul(out=s1, in0=xo, in1=ss_ap),
         r=[ps_key, t["ss"]], w=[sk])
    C.op(V, lambda e: e.tensor_mul(out=s2, in0=xe, in1=ss_ap),
         r=[ps_key, t["ss"], sk], w=[sk])
    C.op(V, lambda e: e.tensor_mul(out=xe, in0=xe, in1=cs_ap),
         r=[ps_key, t["cs"], sk], w=[out_key, ps_key])
    C.op(V, lambda e: e.tensor_sub(out=xe, in0=xe, in1=s1),
         r=[out_key, sk], w=[out_key, ps_key])
    C.op(V, lambda e: e.tensor_mul(out=xo, in0=xo, in1=cs_ap),
         r=[out_key, t["cs"]], w=[out_key, ps_key])
    C.op(V, lambda e: e.tensor_add(out=xo, in0=xo, in1=s2),
         r=[out_key, sk], w=[out_key, ps_key])


def _split_hilo(C, t, src_key, src_ap, hi_ap, hi_key, lo_ap=None, lo_key=None):
    C.op("scalar", lambda e: e.activation(out=hi_ap, in_=src_ap, func=AF.Copy),
         r=[src_key], w=[hi_key])
    if lo_ap is not None:
        C.op("vector", lambda e: e.tensor_sub(out=lo_ap, in0=src_ap, in1=hi_ap),
             r=[src_key, hi_key], w=[lo_key])


def _transpose16(C, t, ident, src_ap_fn, src_key, psum, dst_ap_fn, dst_key,
                 n_tiles=KT):
    T, S = "tensor", "scalar"
    for i in range(n_tiles):
        half = i % 2
        pap = psum[:, half * 1024:half * 1024 + 128]
        C.op(T, lambda e, i=i, pap=pap: e.transpose(pap, src_ap_fn(i), ident[:]),
             r=[src_key, "identity"], w=[psum.name + f"_h{half}"])
        C.op(S, lambda e, i=i, pap=pap: e.activation(out=dst_ap_fn(i), in_=pap,
                                                     func=AF.Copy),
             r=[psum.name + f"_h{half}"], w=[dst_key])



def build_program(debug=(), phases=(1, 2, 3, 4)):
    nc = bass.Bass("TRN2", target_bir_lowering=False)

    x0both = nc.dram_tensor("x0both", [PREV + TOK, H], F32, kind="ExternalInput").ap()
    wq = nc.dram_tensor("wq", [H, H], F32, kind="ExternalInput").ap()
    wk = nc.dram_tensor("wk", [H, H], F32, kind="ExternalInput").ap()
    wv = nc.dram_tensor("wv", [H, H], F32, kind="ExternalInput").ap()
    wf8 = nc.dram_tensor("wf8", [H, H], I8, kind="ExternalInput").ap()
    wsrow = nc.dram_tensor("wsrow", [128, H], F32, kind="ExternalInput").ap()
    biasrow = nc.dram_tensor("biasrow", [128, H], F32, kind="ExternalInput").ap()
    ident_in = nc.dram_tensor("ident", [128, 128], BF16, kind="ExternalInput").ap()
    csq_in = nc.dram_tensor("csq", [TOK, H // 2], F32, kind="ExternalInput").ap()
    ssq_in = nc.dram_tensor("ssq", [TOK, H // 2], F32, kind="ExternalInput").ap()
    csk_in = nc.dram_tensor("csk", [PREV + TOK, H // 2], F32, kind="ExternalInput").ap()
    ssk_in = nc.dram_tensor("ssk", [PREV + TOK, H // 2], F32, kind="ExternalInput").ap()
    down_in = nc.dram_tensor("d_own_T", [TOK, TOK], F32, kind="ExternalInput").ap()
    dprev_in = nc.dram_tensor("d_prev_T", [PREV, TOK], F32, kind="ExternalInput").ap()

    out = nc.dram_tensor("out", [TOK, H], F32, kind="ExternalOutput").ap()

    x1_hbm = nc.dram_tensor("x1_hbm", [PREV + TOK, H], F32, kind="Internal").ap()
    h2t_hi = nc.dram_tensor("h2t_hi", [NALL, 128, H], BF16, kind="Internal").ap()
    h2t_lo = nc.dram_tensor("h2t_lo", [NCH, 128, H], BF16, kind="Internal").ap()
    vhi_hbm = nc.dram_tensor("vhi_hbm", [NALL, 128, H], BF16, kind="Internal").ap()
    vlo_hbm = nc.dram_tensor("vlo_hbm", [NCH, 128, H], BF16, kind="Internal").ap()
    kfm_hi = nc.dram_tensor("kfm_hi", [KT, 128, PREV + TOK], BF16, kind="Internal").ap()
    kfm_lo = nc.dram_tensor("kfm_lo", [KT, 128, TOK], BF16, kind="Internal").ap()
    qfm_hi = nc.dram_tensor("qfm_hi", [KT, 128, TOK], BF16, kind="Internal").ap()
    qfm_lo = nc.dram_tensor("qfm_lo", [KT, 128, TOK], BF16, kind="Internal").ap()
    oprev_hbm = nc.dram_tensor("oprev_hbm", [TOK, H], F32, kind="Internal").ap()
    oown_hbm = nc.dram_tensor("oown_hbm", [TOK, H], F32, kind="Internal").ap()

    dbg = {}
    for name in debug:
        shapes = {"x1": [PREV + TOK, H], "h2t_hi": [NALL, 128, H],
                  "vhi": [NALL, 128, H], "kfm_hi": [KT, 128, PREV + TOK],
                  "qfm_hi": [KT, 128, TOK], "oprev": [TOK, H], "oown": [TOK, H]}
        dbg[name] = nc.dram_tensor("dbg_" + name, shapes[name],
                                   BF16 if name in ("h2t_hi", "vhi", "kfm_hi", "qfm_hi") else F32,
                                   kind="ExternalOutput").ap()

    es_outer = ExitStack()
    with es_outer:
        C = Ctx(nc, es_outer)
        V, S, T, DM, G = "vector", "scalar", "tensor", "sync", "gpsimd"

        consts = {}
        const_vals = dict(invH=1.0 / H, epsln=1e-5, neghalf=-0.5,
                          negln2=-LN2, b24ln2=24.0 * LN2, posln2=LN2,
                          bm24h=-24.0 * LN2 + math.log(0.5), tiny=1e-30)
        const_tile = es_outer.enter_context(nc.sbuf_tensor("consts", [128, len(const_vals)], F32))
        for i, (k, val) in enumerate(const_vals.items()):
            consts[k] = const_tile[:, i:i + 1]

        ident = es_outer.enter_context(nc.sbuf_tensor("identity", [128, 128], BF16))
        wsr = es_outer.enter_context(nc.sbuf_tensor("wsr", [128, H], F32))
        bir = es_outer.enter_context(nc.sbuf_tensor("bir", [128, H], F32))

        with nc.Block() as block:
            for i, (k, val) in enumerate(const_vals.items()):
                C.op(V, lambda e, i=i, val=val: e.memset(const_tile[:, i:i + 1], float(val)),
                     w=["consts"])
            C.op(DM, lambda e: e.dma_start(out=ident[:], in_=ident_in), w=["identity"], dma=True)
            C.op(DM, lambda e: e.dma_start(out=wsr[:], in_=wsrow), w=["wsr"], dma=True)
            C.op(DM, lambda e: e.dma_start(out=bir[:], in_=biasrow), w=["bir"], dma=True)

            with ExitStack() as es:
                sb = lambda name, shape, dt: es.enter_context(nc.sbuf_tensor(name, shape, dt))
                wft = Split(nc, es, "wft", KT * H, 8 * H, BF16)
                wstage = [sb(f"wstage{s}", [128, H], I8) for s in range(2)]
                wfbf = sb("wfbf", [128, H], BF16)
                ps_tr = es.enter_context(nc.psum_tensor("ps_tr", [128, 2048], BF16))
                ps_ffn = es.enter_context(nc.psum_tensor("ps_ffn", [128, H], F32))

                for ntile in range(KT):
                    slot = ntile % 2
                    C.op(DM, lambda e, ntile=ntile, slot=slot: e.dma_start(
                        out=wstage[slot][:], in_=wf8[ntile * 128:(ntile + 1) * 128, :]),
                        w=[wstage[slot]], dma=True)
                    C.op(V, lambda e, slot=slot: e.tensor_copy(out=wfbf[:], in_=wstage[slot][:]),
                         r=[wstage[slot]], w=[wfbf])
                    _transpose16(C, None, ident,
                                 lambda kt, ntile=ntile: wfbf[:, kt * 128:(kt + 1) * 128],
                                 wfbf, ps_tr,
                                 lambda kt, ntile=ntile: wft.ap(kt * H + ntile * 128, 128),
                                 "wft")
                scr = [sb(f"p1_s{i}", [128, H], F32) for i in range(5)]
                i32a = sb("p1_i32a", [128, H], I32)
                i32b = sb("p1_i32b", [128, H], I32)
                qbf = wfbf
                u1b = sb("p1_u1", [128, H], BF16)
                u2b = sb("p1_u2", [128, H], BF16)
                h2hi_t = sb("p1_h2hi", [128, H], BF16)
                h2lo_t = sb("p1_h2lo", [128, H], BF16)
                h2thi_t = sb("p1_h2thi", [128, H], BF16)
                h2tlo_t = sb("p1_h2tlo", [128, H], BF16)
                xts = [sb(f"p1_xt{s}", [128, H], F32) for s in range(2)]
                x1cs = [sb(f"p1_x1c{s}", [128, H], F32) for s in range(2)]
                qts = [sb("p1_qt0", [128, H], BF16)] * 2
                stts = []
                for s in range(2):
                    stt = {}
                    for nm in ("msum", "negm", "vsum", "lnv", "rstd", "dmax",
                               "amax", "am2", "rec", "rec127", "fac", "iscale",
                               "nmr"):
                        stt[nm] = sb(f"p1_st_{nm}{s}", [128, 1], F32)
                    stts.append(stt)
                tiles = []
                for s in range(2):
                    t = dict(d=scr[0], sq=scr[1], y=scr[2], dd=scr[1], rf=scr[3],
                             tq=scr[2],
                             fa=scr[0], fa2=scr[1], flog=scr[0], fu=scr[1],
                             fuc=scr[0],
                             fum=scr[4], frf=scr[0], fdd=scr[1], fr1=scr[3],
                             fr2=scr[0], f0=scr[1], h2=scr[2],
                             ri=i32a, i32a=i32a, i32b=i32b,
                             q=qbf, u1=u1b, u2=u2b, fu1=u1b, fu2=u2b,
                             h2hi=h2hi_t, h2lo=h2lo_t, h2thi=h2thi_t,
                             h2tlo=h2tlo_t,
                             xt=xts[s], x1c=x1cs[s], qt=qts[s])
                    tiles.append((t, stts[s]))

                def p1_load(c):
                    t, _ = tiles[c % 2]
                    C.op(DM, lambda e, c=c, t=t: e.dma_start(
                        out=t["xt"][:], in_=x0both[c * CH:(c + 1) * CH, :]),
                        w=[t["xt"]], dma=True)

                def p1_seg1(c):
                    t, stt = tiles[c % 2]
                    _ln_quant(C, consts, t["xt"][:], t["xt"], t, stt, want_h2=False)
                    _transpose16(C, t, ident,
                                 lambda i, t=t: t["q"][:, i * 128:(i + 1) * 128],
                                 t["q"], ps_tr,
                                 lambda i, t=t: t["qt"][:, i * 128:(i + 1) * 128],
                                 t["qt"])

                def p1_mm(c):
                    t, stt = tiles[c % 2]
                    vt = x1cs[(c + 1) % 2]
                    for ns in range(4):
                        for kt in range(KT):
                            C.op(T, lambda e, t=t, kt=kt, ns=ns: e.matmul(
                                ps_ffn[:, ns * 512:(ns + 1) * 512],
                                t["qt"][:, kt * 128:(kt + 1) * 128],
                                wft.ap(kt * H + ns * 512, 512),
                                start=(kt == 0), stop=(kt == KT - 1)),
                                r=[t["qt"], "wft"], w=[f"ps_ffn{ns}"], accum=(kt > 0))
                        C.op(S, lambda e, vt=vt, stt=stt, ns=ns: e.activation(
                            out=vt[:, ns * 512:(ns + 1) * 512],
                            in_=ps_ffn[:, ns * 512:(ns + 1) * 512], func=AF.Copy,
                            scale=stt["iscale"][:, 0:1]),
                            r=[f"ps_ffn{ns}", stt["iscale"]], w=[vt])

                def p1_seg3(c):
                    t, stt = tiles[c % 2]
                    own = c >= NCH
                    vt = x1cs[(c + 1) % 2]
                    C.op(V, lambda e, vt=vt: e.tensor_mul(out=vt[:], in0=vt[:],
                                                          in1=wsr[:]),
                         r=[vt, "wsr"], w=[vt])
                    C.op(V, lambda e, vt=vt: e.tensor_add(out=vt[:], in0=vt[:],
                                                          in1=bir[:]),
                         r=[vt, "bir"], w=[vt])
                    tt = dict(t)
                    tt["vv"] = vt
                    _f16_round_add(C, consts, tt, "vv", t["xt"][:], t["xt"],
                                   t["x1c"][:], t["x1c"])
                    C.op(DM, lambda e, c=c, t=t: e.dma_start(
                        out=x1_hbm[c * CH:(c + 1) * CH, :], in_=t["x1c"][:]),
                        r=[t["x1c"]], w=["x1_hbm"], dma=True)
                    if c + 2 < NALL:
                        p1_load(c + 2)
                    _ln_simple(C, consts, t, stt, t["x1c"][:], t["x1c"])
                    _split_hilo(C, t, t["h2"], t["h2"][:], t["h2hi"][:], t["h2hi"],
                                t["h2lo"][:] if own else None,
                                t["h2lo"] if own else None)
                    _transpose16(C, t, ident,
                                 lambda i, t=t: t["h2hi"][:, i * 128:(i + 1) * 128],
                                 t["h2hi"], ps_tr,
                                 lambda i, t=t: t["h2thi"][:, i * 128:(i + 1) * 128],
                                 t["h2thi"])
                    C.op(G, lambda e, c=c, t=t: e.dma_start(
                        out=h2t_hi[c], in_=t["h2thi"][:]),
                        r=[t["h2thi"]], w=["h2t_hi"], dma=True)
                    if own:
                        _transpose16(C, t, ident,
                                     lambda i, t=t: t["h2lo"][:, i * 128:(i + 1) * 128],
                                     t["h2lo"], ps_tr,
                                     lambda i, t=t: t["h2tlo"][:, i * 128:(i + 1) * 128],
                                     t["h2tlo"])
                        C.op(G, lambda e, c=c, t=t: e.dma_start(
                            out=h2t_lo[c - NCH], in_=t["h2tlo"][:]),
                            r=[t["h2tlo"]], w=["h2t_lo"], dma=True)

                p1_load(0)
                if NALL > 1:
                    p1_load(1)
                p1_seg1(0)
                for c in range(NALL):
                    p1_mm(c)
                    if c + 1 < NALL:
                        p1_seg1(c + 1)
                    p1_seg3(c)
                C.barrier()
                C.flush(block)

            if 2 in phases:
                _proj_phase(C, nc, block, dict(wq=wq, wk=wk, wv=wv), h2t_hi, h2t_lo,
                            vhi_hbm, vlo_hbm, kfm_hi, kfm_lo, qfm_hi, qfm_lo,
                            csq_in, ssq_in, csk_in, ssk_in, ident, consts)

            if 3 in phases:
                _attn_phase(C, nc, block, vhi_hbm, vlo_hbm, kfm_hi, kfm_lo,
                            qfm_hi, qfm_lo, down_in, dprev_in, oprev_hbm, oown_hbm)

            if 4 in phases:
                _ffn_post_phase(C, nc, block, wf8, x1_hbm, oprev_hbm, oown_hbm,
                                out, ident, consts, wsr, bir)
            else:
                with ExitStack() as es:
                    stage = es.enter_context(nc.sbuf_tensor("outstage", [128, H], F32))
                    for rr in range(0, NCH * CH, 128):
                        C.op("sync", lambda e, rr=rr: e.dma_start(
                            out=stage[:], in_=x1_hbm[NCH * CH + rr:NCH * CH + rr + 128, :]),
                            r=["x1_hbm"], w=["outstage"], dma=True)
                        C.op("sync", lambda e, rr=rr: e.dma_start(
                            out=out[rr:rr + 128, :], in_=stage[:]),
                            r=["outstage"], w=["out"], dma=True)
                    C.barrier()
                    C.flush(block)

            if dbg:
                with ExitStack() as es:
                    stage = es.enter_context(nc.sbuf_tensor("dbgstage", [128, H], F32))
                    srcs = {"x1": (x1_hbm, "x1_hbm", NALL * CH),
                            "oprev": (oprev_hbm, "oprev_hbm", NCH * CH),
                            "oown": (oown_hbm, "oown_hbm", NCH * CH)}
                    for name, dst in dbg.items():
                        srcap, srckey, n_rows = srcs[name]
                        for rr in range(0, n_rows, 128):
                            C.op(DM, lambda e, srcap=srcap, rr=rr: e.dma_start(
                                out=stage[:], in_=srcap[rr:rr + 128, :]),
                                r=[srckey], w=["dbgstage"], dma=True)
                            C.op(DM, lambda e, dst=dst, rr=rr: e.dma_start(
                                out=dst[rr:rr + 128, :], in_=stage[:]),
                                r=["dbgstage"], w=["dbg"], dma=True)
                    C.barrier()
                C.flush(block)

            C.barrier()
            C.flush(block)
    return nc


def _ln_simple(C, consts, t, stt, x_ap, xkey):
    V, S = "vector", "scalar"
    C.op(V, lambda e: e.reduce_sum(out=stt["msum"][:, 0:1], in_=x_ap, axis=AX.X),
         r=[xkey], w=[stt["msum"]])
    C.op(V, lambda e: e.tensor_scalar(out=stt["negm"][:, 0:1], in0=stt["msum"][:, 0:1],
                                      scalar1=-1.0 / H, scalar2=None, op0=AL.mult),
         r=[stt["msum"]], w=[stt["negm"]])
    C.op(S, lambda e: e.activation(out=t["sq"][:], in_=x_ap, func=AF.Square,
                                   bias=stt["negm"][:, 0:1]),
         r=[xkey, stt["negm"]], w=[t["sq"]])
    C.op(V, lambda e: e.reduce_sum(out=stt["vsum"][:, 0:1], in_=t["sq"][:], axis=AX.X),
         r=[t["sq"]], w=[stt["vsum"]])
    C.op(S, lambda e: e.activation(out=stt["lnv"][:, 0:1], in_=stt["vsum"][:, 0:1],
                                   func=AF.Ln, scale=consts["invH"],
                                   bias=consts["epsln"]),
         r=[stt["vsum"], "consts"], w=[stt["lnv"]])
    C.op(S, lambda e: e.activation(out=stt["rstd"][:, 0:1], in_=stt["lnv"][:, 0:1],
                                   func=AF.Exp, scale=consts["neghalf"]),
         r=[stt["lnv"], "consts"], w=[stt["rstd"]])
    C.op(V, lambda e: e.tensor_mul(out=stt["nmr"][:, 0:1], in0=stt["negm"][:, 0:1],
                                   in1=stt["rstd"][:, 0:1]),
         r=[stt["negm"], stt["rstd"]], w=[stt["nmr"]])
    C.op(S, lambda e: e.activation(out=t["h2"][:], in_=x_ap, func=AF.Identity,
                                   scale=stt["rstd"][:, 0:1],
                                   bias=stt["nmr"][:, 0:1]),
         r=[xkey, stt["rstd"], stt["nmr"]], w=[t["h2"]])


def _proj_phase(C, nc, block, ws, h2t_hi, h2t_lo, vhi_hbm, vlo_hbm,
                kfm_hi, kfm_lo, qfm_hi, qfm_lo, csq_in, ssq_in, csk_in, ssk_in,
                ident, consts):
    V, S, T, DM, G = "vector", "scalar", "tensor", "sync", "gpsimd"
    for wname in ("wv", "wk", "wq"):
        wrap = ws[wname]
        own_only = wname == "wq"
        nch = NCH if own_only else NALL
        with ExitStack() as es:
            sb = lambda name, shape, dt: es.enter_context(
                nc.sbuf_tensor(f"{wname}_{name}", shape, dt))
            whi = Split(nc, es, wname + "_whi", KT * H, 8 * H, BF16)
            wlo = Split(nc, es, wname + "_wlo", KT * H, 8 * H, BF16)
            wstage = [sb("wstage0", [128, H], F32)] * 2
            ps_tr = es.enter_context(nc.psum_tensor(f"{wname}_ps_tr", [128, 2048], BF16))
            ps_mm = es.enter_context(nc.psum_tensor(f"{wname}_ps_mm", [128, H], F32))
            for kt in range(KT):
                slot = kt % 2
                C.op(DM, lambda e, kt=kt, slot=slot: e.dma_start(
                    out=wstage[slot][:], in_=wrap[kt * 128:(kt + 1) * 128, :]),
                    w=[wstage[slot]], dma=True)
                C.op(S, lambda e, kt=kt, slot=slot: e.activation(
                    out=whi.ap(kt * H, H), in_=wstage[slot][:], func=AF.Copy),
                    r=[wstage[slot]], w=["whi" + wname])
                C.op(V, lambda e, kt=kt, slot=slot: e.tensor_sub(
                    out=wlo.ap(kt * H, H), in0=wstage[slot][:],
                    in1=whi.ap(kt * H, H)),
                    r=[wstage[slot], "whi" + wname], w=["wlo" + wname])
            t0 = {}
            t0["hthi"] = sb("hthi0", [128, H], BF16)
            t0["htlo"] = sb("htlo0", [128, H], BF16)
            t0["pf"] = sb("pf0", [128, H], F32)
            t0["xp2a"] = wstage[0][:, 0:H // 2]
            t0["xp2b"] = wstage[0][:, H // 2:H]
            t0["xp2key"] = wstage[0]
            t0["cs"] = sb("cs0", [128, H // 2], F32)
            t0["ss"] = sb("ss0", [128, H // 2], F32)
            t0["phi"] = [sb("phi0", [128, H], BF16), sb("phi1", [128, H], BF16)]
            t0["plo"] = [sb("plo0", [128, H], BF16), sb("plo1", [128, H], BF16)]
            tiles = [t0, t0]
            if wname in ("wk", "wq"):
                fmst_hi = sb("fmst_hi", [128, KT * 128], BF16)
                fmst_lo = sb("fmst_lo", [128, KT * 128], BF16)

            pending = []
            for ci in range(nch):
                c = ci + (NCH if own_only else 0)
                own = c >= NCH
                t = tiles[ci % 2]
                C.op(DM, lambda e, c=c, t=t: e.dma_start(out=t["hthi"][:], in_=h2t_hi[c]),
                     w=[t["hthi"]], dma=True)
                if own:
                    C.op(DM, lambda e, c=c, t=t: e.dma_start(out=t["htlo"][:],
                                                             in_=h2t_lo[c - NCH]),
                         w=[t["htlo"]], dma=True)
                for kt in range(KT):
                    first = kt == 0
                    last = kt == KT - 1
                    hi_stop = last and not own
                    for ns in range(4):
                        C.op(T, lambda e, t=t, kt=kt, ns=ns, first=first, hi_stop=hi_stop: e.matmul(
                            ps_mm[:, ns * 512:(ns + 1) * 512],
                            t["hthi"][:, kt * 128:(kt + 1) * 128],
                            whi.ap(kt * H + ns * 512, 512),
                            start=first, stop=hi_stop),
                            r=[t["hthi"], "whi" + wname], w=["psmm" + wname],
                            accum=not first)
                    if own:
                        for ns in range(4):
                            C.op(T, lambda e, t=t, kt=kt, ns=ns: e.matmul(
                                ps_mm[:, ns * 512:(ns + 1) * 512],
                                t["hthi"][:, kt * 128:(kt + 1) * 128],
                                wlo.ap(kt * H + ns * 512, 512),
                                start=False, stop=False),
                                r=[t["hthi"], "wlo" + wname], w=["psmm" + wname],
                                accum=True)
                        for ns in range(4):
                            C.op(T, lambda e, t=t, kt=kt, ns=ns, last=last: e.matmul(
                                ps_mm[:, ns * 512:(ns + 1) * 512],
                                t["htlo"][:, kt * 128:(kt + 1) * 128],
                                whi.ap(kt * H + ns * 512, 512),
                                start=False, stop=last),
                                r=[t["htlo"], "whi" + wname], w=["psmm" + wname],
                                accum=True)
                C.op(S, lambda e, t=t: e.activation(out=t["pf"][:], in_=ps_mm[:],
                                                    func=AF.Copy),
                     r=["psmm" + wname], w=[t["pf"]])
                if pending:
                    pending.pop(0)()
                phi = t["phi"][ci % 2]
                plo = t["plo"][ci % 2]
                if wname == "wv":
                    _split_hilo(C, t, t["pf"], t["pf"][:], phi[:], phi,
                                plo[:] if own else None,
                                plo if own else None)
                    C.op(G, lambda e, c=c, phi=phi: e.dma_start(out=vhi_hbm[c], in_=phi[:]),
                         r=[phi], w=["vhi_hbm"], dma=True)
                    if own:
                        C.op(G, lambda e, c=c, plo=plo: e.dma_start(out=vlo_hbm[c - NCH],
                                                                    in_=plo[:]),
                             r=[plo], w=["vlo_hbm"], dma=True)
                else:
                    if wname == "wk":
                        cs_src, ss_src = csk_in, ssk_in
                        row0 = c * CH
                    else:
                        cs_src, ss_src = csq_in, ssq_in
                        row0 = (c - NCH) * CH
                    C.op(DM, lambda e, t=t, cs_src=cs_src, row0=row0: e.dma_start(
                        out=t["cs"][:], in_=cs_src[row0:row0 + CH, :]),
                        w=[t["cs"]], dma=True)
                    C.op(DM, lambda e, t=t, ss_src=ss_src, row0=row0: e.dma_start(
                        out=t["ss"][:], in_=ss_src[row0:row0 + CH, :]),
                        w=[t["ss"]], dma=True)
                    _xpos_psum(C, t, t["pf"], t["pf"][:], t["cs"][:], t["ss"][:],
                               t["pf"][:], t["pf"])
                    _split_hilo(C, t, t["pf"], t["pf"][:], phi[:], phi,
                                plo[:] if own else None,
                                plo if own else None)

                    def _deferred(t=t, phi=phi, plo=plo, own=own, ci=ci):
                        _transpose16(C, t, ident,
                                     lambda i, phi=phi: phi[:, i * 128:(i + 1) * 128],
                                     phi, ps_tr,
                                     lambda i: fmst_hi[:, i * 128:(i + 1) * 128],
                                     "fmst_hi")
                        if own:
                            _transpose16(C, t, ident,
                                         lambda i, plo=plo: plo[:, i * 128:(i + 1) * 128],
                                         plo, ps_tr,
                                         lambda i: fmst_lo[:, i * 128:(i + 1) * 128],
                                         "fmst_lo")
                        g0 = ci * 128
                        dsts = (kfm_hi, kfm_lo) if wname == "wk" else (qfm_hi, qfm_lo)
                        for ft in range(KT):
                            C.op(G, lambda e, ft=ft, g0=g0, dsts=dsts: e.dma_start(
                                out=dsts[0][ft, :, g0:g0 + 128],
                                in_=fmst_hi[:, ft * 128:(ft + 1) * 128]),
                                r=["fmst_hi"], w=["kfmq_hi"], dma=True)
                            if own:
                                lo_g0 = g0 - (0 if own_only else NCH * CH)
                                C.op(G, lambda e, ft=ft, lo_g0=lo_g0, dsts=dsts: e.dma_start(
                                    out=dsts[1][ft, :, lo_g0:lo_g0 + 128],
                                    in_=fmst_lo[:, ft * 128:(ft + 1) * 128]),
                                    r=["fmst_lo"], w=["kfmq_lo"], dma=True)
                    pending.append(_deferred)
            for fn in pending:
                fn()
            pending = []
            C.barrier()
            C.flush(block)


def _attn_phase(C, nc, block, vhi_hbm, vlo_hbm, kfm_hi, kfm_lo, qfm_hi, qfm_lo,
                down_in, dprev_in, oprev_hbm, oown_hbm):
    V, S, T, DM, G = "vector", "scalar", "tensor", "sync", "gpsimd"
    with ExitStack() as es:
        sb = lambda name, shape, dt: es.enter_context(nc.sbuf_tensor("p3a_" + name, shape, dt))
        vprev = Split(nc, es, "p3a_vprev", NCH * H, 8 * H, BF16)
        qhi = sb("qhi", [128, KT * 512], BF16)
        ktile = [sb(f"ktile{s}", [128, KT * 128], BF16) for s in range(3)]
        atile = sb("atile", [128, 16 * 512], BF16)
        dtile = [sb(f"dtile{s}", [128, 512], F32) for s in range(2)]
        oev = [sb(f"oev{s}", [128, H], F32) for s in range(2)]
        ps_sc = es.enter_context(nc.psum_tensor("p3a_ps_sc", [128, 1024], F32))
        ps_out = es.enter_context(nc.psum_tensor("p3a_ps_out", [128, H], F32))
        for c in range(NCH):
            C.op(DM, lambda e, c=c: e.dma_start(out=vprev.ap(c * H, H),
                                                in_=vhi_hbm[c]),
                 w=[vprev.key(c * H)], dma=True)
        for qi in range(NCH * CH // 512):
            i0 = qi * 512
            for ft in range(KT):
                C.op(DM, lambda e, ft=ft, i0=i0: e.dma_start(
                    out=qhi[:, ft * 512:(ft + 1) * 512],
                    in_=qfm_hi[ft, :, i0:i0 + 512]),
                    w=["qhi"], dma=True)
            for jt in range(NCH):
                slot = jt % 3
                C.op(DM, lambda e, jt=jt, slot=slot: e.dma_start(
                    out=ktile[slot][:],
                    in_=kfm_hi[0:KT, :, jt * 128:(jt + 1) * 128].transpose([1, 0, 2])),
                    w=[ktile[slot]], dma=True)
                C.op(G, lambda e, jt=jt, i0=i0, slot=slot: e.dma_start(
                    out=dtile[slot % 2][:], in_=dprev_in[jt * 128:(jt + 1) * 128, i0:i0 + 512]),
                    w=[dtile[slot % 2]], dma=True)
                sc = ps_sc[:, (jt % 2) * 512:(jt % 2) * 512 + 512]
                for ft in range(KT):
                    C.op(T, lambda e, ft=ft, jt=jt, slot=slot, sc=sc: e.matmul(
                        sc, ktile[slot][:, ft * 128:(ft + 1) * 128],
                        qhi[:, ft * 512:(ft + 1) * 512],
                        start=(ft == 0), stop=(ft == KT - 1)),
                        r=[ktile[slot], "qhi"], w=[f"p3a_sc{jt % 2}"], accum=(ft > 0))
                C.op(V, lambda e, jt=jt, slot=slot, sc=sc: e.tensor_mul(
                    out=atile[:, jt * 512:(jt + 1) * 512], in0=sc, in1=dtile[slot % 2][:]),
                    r=[f"p3a_sc{jt % 2}", dtile[slot % 2]], w=["atile"])
            for it in range(4):
                for jt in range(NCH):
                    for vs in range(4):
                        C.op(T, lambda e, it=it, jt=jt, vs=vs: e.matmul(
                            ps_out[:, vs * 512:(vs + 1) * 512],
                            atile[:, jt * 512 + it * 128:jt * 512 + (it + 1) * 128],
                            vprev.ap(jt * H + vs * 512, 512),
                            start=(jt == 0), stop=(jt == NCH - 1)),
                            r=["atile", vprev.key(jt * H)], w=["p3a_out"], accum=(jt > 0))
                oslot = it % 2
                C.op(S, lambda e, oslot=oslot: e.activation(out=oev[oslot][:],
                                                            in_=ps_out[:], func=AF.Copy),
                     r=["p3a_out"], w=[oev[oslot]])
                C.op(DM, lambda e, qi=qi, it=it, oslot=oslot: e.dma_start(
                    out=oprev_hbm[qi * 512 + it * 128:qi * 512 + (it + 1) * 128, :],
                    in_=oev[oslot][:]),
                    r=[oev[oslot]], w=["oprev_hbm"], dma=True)
        C.barrier()
        C.flush(block)

    for vfh in range(2):
        with ExitStack() as es:
            sb = lambda name, shape, dt: es.enter_context(
                nc.sbuf_tensor(f"p3b{vfh}_" + name, shape, dt))
            vh_hi = Split(nc, es, f"p3b{vfh}_vh_hi", NCH * 1024, 8 * 1024, BF16)
            vh_lo = Split(nc, es, f"p3b{vfh}_vh_lo", NCH * 1024, 8 * 1024, BF16)
            qhi = sb("qhi", [128, KT * 256], BF16)
            qlo = sb("qlo", [128, KT * 256], BF16)
            k_hi = [sb(f"k_hi{s}", [128, KT * 128], BF16) for s in range(3)]
            k_lo = [sb(f"k_lo{s}", [128, KT * 128], BF16) for s in range(3)]
            a_hi = sb("a_hi", [128, NCH * 256], BF16)
            a_lo = sb("a_lo", [128, NCH * 256], BF16)
            sf32 = [sb(f"sf32{s}", [128, 256], F32) for s in range(2)]
            dtile = [sb(f"dtile{s}", [128, 256], F32) for s in range(2)]
            oev = [sb(f"oev{s}", [128, 1024], F32) for s in range(2)]
            ps_sc = es.enter_context(nc.psum_tensor(f"p3b{vfh}_ps_sc", [128, 1024], F32))
            ps_out = es.enter_context(nc.psum_tensor(f"p3b{vfh}_ps_out", [128, 2048], F32))
            for c in range(NCH):
                C.op(DM, lambda e, c=c, vfh=vfh: e.dma_start(
                    out=vh_hi.ap(c * 1024, 1024),
                    in_=vhi_hbm[NCH + c][:, vfh * 1024:(vfh + 1) * 1024]),
                    w=[vh_hi.key(c * 1024)], dma=True)
                C.op(DM, lambda e, c=c, vfh=vfh: e.dma_start(
                    out=vh_lo.ap(c * 1024, 1024),
                    in_=vlo_hbm[c][:, vfh * 1024:(vfh + 1) * 1024]),
                    w=[vh_lo.key(c * 1024)], dma=True)
            for qi in range(NCH * CH // 256):
                i0 = qi * 256
                for ft in range(KT):
                    C.op(DM, lambda e, ft=ft, i0=i0: e.dma_start(
                        out=qhi[:, ft * 256:(ft + 1) * 256],
                        in_=qfm_hi[ft, :, i0:i0 + 256]),
                        w=["qhi"], dma=True)
                    C.op(DM, lambda e, ft=ft, i0=i0: e.dma_start(
                        out=qlo[:, ft * 256:(ft + 1) * 256],
                        in_=qfm_lo[ft, :, i0:i0 + 256]),
                        w=["qlo"], dma=True)
                njt = 2 * qi + 2
                for jt in range(njt):
                    slot = jt % 3
                    C.op(DM, lambda e, jt=jt, slot=slot: e.dma_start(
                        out=k_hi[slot][:],
                        in_=kfm_hi[0:KT, :, NCH * CH + jt * 128:
                                   NCH * CH + (jt + 1) * 128].transpose([1, 0, 2])),
                        w=[k_hi[slot]], dma=True)
                    C.op(G, lambda e, jt=jt, slot=slot: e.dma_start(
                        out=k_lo[slot][:],
                        in_=kfm_lo[0:KT, :, jt * 128:(jt + 1) * 128].transpose([1, 0, 2])),
                        w=[k_lo[slot]], dma=True)
                    C.op(G, lambda e, jt=jt, i0=i0, slot=slot: e.dma_start(
                        out=dtile[slot % 2][:], in_=down_in[jt * 128:(jt + 1) * 128, i0:i0 + 256]),
                        w=[dtile[slot % 2]], dma=True)
                    sc = ps_sc[:, (jt % 2) * 512:(jt % 2) * 512 + 256]
                    for ft in range(KT):
                        C.op(T, lambda e, ft=ft, slot=slot, sc=sc: e.matmul(
                            sc, k_hi[slot][:, ft * 128:(ft + 1) * 128],
                            qhi[:, ft * 256:(ft + 1) * 256],
                            start=(ft == 0), stop=False),
                            r=[k_hi[slot], "qhi"], w=[f"p3b_sc{jt % 2}"], accum=(ft > 0))
                        C.op(T, lambda e, ft=ft, slot=slot, sc=sc: e.matmul(
                            sc, k_hi[slot][:, ft * 128:(ft + 1) * 128],
                            qlo[:, ft * 256:(ft + 1) * 256],
                            start=False, stop=False),
                            r=[k_hi[slot], "qlo"], w=[f"p3b_sc{jt % 2}"], accum=True)
                        C.op(T, lambda e, ft=ft, slot=slot, sc=sc: e.matmul(
                            sc, k_lo[slot][:, ft * 128:(ft + 1) * 128],
                            qhi[:, ft * 256:(ft + 1) * 256],
                            start=False, stop=(ft == KT - 1)),
                            r=[k_lo[slot], "qhi"], w=[f"p3b_sc{jt % 2}"], accum=True)
                    C.op(V, lambda e, slot=slot, sc=sc: e.tensor_mul(
                        out=sf32[slot % 2][:], in0=sc, in1=dtile[slot % 2][:]),
                        r=[f"p3b_sc{jt % 2}", dtile[slot % 2]], w=[sf32[slot % 2]])
                    C.op(S, lambda e, jt=jt, slot=slot: e.activation(
                        out=a_hi[:, jt * 256:(jt + 1) * 256], in_=sf32[slot % 2][:],
                        func=AF.Copy),
                        r=[sf32[slot % 2]], w=["a_hi"])
                    C.op(V, lambda e, jt=jt, slot=slot: e.tensor_sub(
                        out=a_lo[:, jt * 256:(jt + 1) * 256], in0=sf32[slot % 2][:],
                        in1=a_hi[:, jt * 256:(jt + 1) * 256]),
                        r=[sf32[slot % 2], "a_hi"], w=["a_lo"])
                for it in range(2):
                    for jt in range(njt):
                        a_h = a_hi[:, jt * 256 + it * 128:jt * 256 + (it + 1) * 128]
                        a_l = a_lo[:, jt * 256 + it * 128:jt * 256 + (it + 1) * 128]
                        for vs in range(2):
                            vh = vh_hi.ap(jt * 1024 + vs * 512, 512)
                            vl = vh_lo.ap(jt * 1024 + vs * 512, 512)
                            vhk = vh_hi.key(jt * 1024)
                            vlk = vh_lo.key(jt * 1024)
                            po = ps_out[:, (it * 1024) + vs * 512:(it * 1024) + (vs + 1) * 512]
                            C.op(T, lambda e, a_h=a_h, vh=vh, po=po, jt=jt: e.matmul(
                                po, a_h, vh, start=(jt == 0), stop=False),
                                r=["a_hi", vhk], w=[f"p3b_out{it}"], accum=(jt > 0))
                            C.op(T, lambda e, a_h=a_h, vl=vl, po=po: e.matmul(
                                po, a_h, vl, start=False, stop=False),
                                r=["a_hi", vlk], w=[f"p3b_out{it}"], accum=True)
                            C.op(T, lambda e, a_l=a_l, vh=vh, po=po, jt=jt, njt=njt: e.matmul(
                                po, a_l, vh, start=False, stop=(jt == njt - 1)),
                                r=["a_lo", vhk], w=[f"p3b_out{it}"], accum=True)
                    oslot = it % 2
                    C.op(S, lambda e, it=it, oslot=oslot: e.activation(
                        out=oev[oslot][:], in_=ps_out[:, it * 1024:(it + 1) * 1024],
                        func=AF.Copy),
                        r=[f"p3b_out{it}"], w=[oev[oslot]])
                    C.op(DM, lambda e, qi=qi, it=it, vfh=vfh, oslot=oslot: e.dma_start(
                        out=oown_hbm[qi * 256 + it * 128:qi * 256 + (it + 1) * 128,
                                     vfh * 1024:(vfh + 1) * 1024],
                        in_=oev[oslot][:]),
                        r=[oev[oslot]], w=["oown_hbm"], dma=True)
            C.barrier()
            C.flush(block)


def _ffn_post_phase(C, nc, block, wf8, x1_hbm, oprev_hbm, oown_hbm, out,
                    ident, consts, wsr, bir):
    V, S, T, DM, G = "vector", "scalar", "tensor", "sync", "gpsimd"
    with ExitStack() as es:
        sb = lambda name, shape, dt: es.enter_context(nc.sbuf_tensor("p4_" + name, shape, dt))
        wft = Split(nc, es, "p4_wft", KT * H, 8 * H, BF16)
        wstage = [sb(f"wstage{s}", [128, H], I8) for s in range(2)]
        wfbf = sb("wfbf", [128, H], BF16)
        ps_tr = es.enter_context(nc.psum_tensor("p4_ps_tr", [128, 2048], BF16))
        ps_ffn = es.enter_context(nc.psum_tensor("p4_ps_ffn", [128, H], F32))
        for ntile in range(KT):
            slot = ntile % 2
            C.op(DM, lambda e, ntile=ntile, slot=slot: e.dma_start(
                out=wstage[slot][:], in_=wf8[ntile * 128:(ntile + 1) * 128, :]),
                w=[wstage[slot]], dma=True)
            C.op(V, lambda e, slot=slot: e.tensor_copy(out=wfbf[:], in_=wstage[slot][:]),
                 r=[wstage[slot]], w=["wfbf4"])
            _transpose16(C, None, ident,
                         lambda kt, ntile=ntile: wfbf[:, kt * 128:(kt + 1) * 128],
                         "wfbf4", ps_tr,
                         lambda kt, ntile=ntile: wft.ap(kt * H + ntile * 128, 128),
                         "wft4")
        scr = [sb(f"s{i}", [128, H], F32) for i in range(5)]
        i32a = sb("i32a", [128, H], I32)
        i32b = sb("i32b", [128, H], I32)
        qbf = wfbf
        u1b = sb("u1", [128, H], BF16)
        u2b = sb("u2", [128, H], BF16)
        x3c_t = sb("x3c", [128, H], F32)
        x1cs = [sb(f"x1c{s}", [128, H], F32) for s in range(2)]
        ops = [sb("op0", [128, H], F32)] * 2
        oos = [sb("oo0", [128, H], F32)] * 2
        qts = [sb("qt0", [128, H], BF16)] * 2
        stts = []
        for s in range(2):
            stt = {}
            for nm in ("msum", "negm", "vsum", "lnv", "rstd", "dmax",
                       "amax", "am2", "rec", "rec127", "fac", "iscale"):
                stt[nm] = sb(f"st_{nm}{s}", [128, 1], F32)
            stts.append(stt)
        tiles = []
        for s in range(2):
            t = dict(d=scr[0], sq=scr[1], y=scr[2], dd=scr[1], rf=scr[3],
                     tq=scr[2],
                     fa=scr[0], fa2=scr[1], flog=scr[0], fu=scr[1],
                     fuc=scr[0],
                     fum=scr[4], frf=scr[0], fdd=scr[1], fr1=scr[3],
                     fr2=scr[0], f0=scr[1],
                     ri=i32a, i32a=i32a, i32b=i32b,
                     q=qbf, u1=u1b, u2=u2b, fu1=u1b, fu2=u2b,
                     x3c=x3c_t,
                     x1c=x1cs[s], op_=ops[s], oo_=oos[s], qt=qts[s])
            tiles.append((t, stts[s]))

        def p4_load_x(c):
            t, _ = tiles[c % 2]
            C.op(DM, lambda e, t=t, c=c: e.dma_start(
                out=t["x1c"][:],
                in_=x1_hbm[NCH * CH + c * CH:NCH * CH + (c + 1) * CH, :]),
                w=[t["x1c"]], dma=True)

        def p4_load_o(c):
            t, _ = tiles[c % 2]
            C.op(DM, lambda e, t=t, c=c: e.dma_start(
                out=t["op_"][:], in_=oprev_hbm[c * CH:(c + 1) * CH, :]),
                w=[t["op_"]], dma=True)
            C.op(G, lambda e, t=t, c=c: e.dma_start(
                out=t["oo_"][:], in_=oown_hbm[c * CH:(c + 1) * CH, :]),
                w=[t["oo_"]], dma=True)

        def p4_seg1(c):
            t, stt = tiles[c % 2]
            C.op(V, lambda e, t=t: e.tensor_add(out=t["x1c"][:], in0=t["x1c"][:],
                                                in1=t["op_"][:]),
                 r=[t["x1c"], t["op_"]], w=[t["x1c"]])
            C.op(V, lambda e, t=t: e.tensor_add(out=t["x1c"][:], in0=t["x1c"][:],
                                                in1=t["oo_"][:]),
                 r=[t["x1c"], t["oo_"]], w=[t["x1c"]])
            if c + 1 < NCH:
                p4_load_o(c + 1)
            _ln_quant(C, consts, t["x1c"][:], t["x1c"], t, stt, want_h2=False)
            _transpose16(C, t, ident,
                         lambda i, t=t: t["q"][:, i * 128:(i + 1) * 128],
                         t["q"], ps_tr,
                         lambda i, t=t: t["qt"][:, i * 128:(i + 1) * 128],
                         t["qt"])

        def p4_mm(c):
            t, stt = tiles[c % 2]
            for ns in range(4):
                for kt in range(KT):
                    C.op(T, lambda e, t=t, kt=kt, ns=ns: e.matmul(
                        ps_ffn[:, ns * 512:(ns + 1) * 512],
                        t["qt"][:, kt * 128:(kt + 1) * 128],
                        wft.ap(kt * H + ns * 512, 512),
                        start=(kt == 0), stop=(kt == KT - 1)),
                        r=[t["qt"], "wft4"], w=[f"ps_ffn4{ns}"], accum=(kt > 0))
                C.op(S, lambda e, stt=stt, ns=ns: e.activation(
                    out=x3c_t[:, ns * 512:(ns + 1) * 512],
                    in_=ps_ffn[:, ns * 512:(ns + 1) * 512], func=AF.Copy,
                    scale=stt["iscale"][:, 0:1]),
                    r=[f"ps_ffn4{ns}", stt["iscale"]], w=[x3c_t])

        def p4_seg3(c):
            t, stt = tiles[c % 2]
            r0 = c * CH
            C.op(V, lambda e: e.tensor_mul(out=x3c_t[:], in0=x3c_t[:], in1=wsr[:]),
                 r=[x3c_t, "wsr"], w=[x3c_t])
            C.op(V, lambda e: e.tensor_add(out=x3c_t[:], in0=x3c_t[:], in1=bir[:]),
                 r=[x3c_t, "bir"], w=[x3c_t])
            tt = dict(t)
            tt["vv"] = x3c_t
            _f16_round_add(C, consts, tt, "vv", t["x1c"][:], t["x1c"],
                           x3c_t[:], x3c_t)
            C.op(DM, lambda e, r0=r0: e.dma_start(
                out=out[r0:r0 + CH, :], in_=x3c_t[:]),
                r=[x3c_t], w=["out"], dma=True)
            if c + 2 < NCH:
                p4_load_x(c + 2)

        p4_load_x(0)
        p4_load_o(0)
        if NCH > 1:
            p4_load_x(1)
        p4_seg1(0)
        for c in range(NCH):
            p4_mm(c)
            if c + 1 < NCH:
                p4_seg1(c + 1)
            p4_seg3(c)
        C.barrier()
        C.flush(block)



def _prep_inputs(L0_attn_X, L0_attn_W_Q, L0_attn_W_K, L0_attn_W_V,
                 L0_ffn_weight, L0_ffn_w_scale, L0_ffn_bias):
    tb = _host_tables()
    X = np.ascontiguousarray(np.asarray(L0_attn_X, dtype=np.float32))
    wq = np.ascontiguousarray(np.asarray(L0_attn_W_Q, dtype=np.float32))
    wk = np.ascontiguousarray(np.asarray(L0_attn_W_K, dtype=np.float32))
    wv = np.ascontiguousarray(np.asarray(L0_attn_W_V, dtype=np.float32))
    wf8 = np.ascontiguousarray(np.asarray(L0_ffn_weight, dtype=np.int8))
    ws = np.asarray(L0_ffn_w_scale, dtype=np.float32).reshape(-1)
    bias = np.asarray(L0_ffn_bias, dtype=np.float32).reshape(-1)
    wsrow = np.ascontiguousarray(np.broadcast_to(ws[None, :], (128, H)))
    biasrow = np.ascontiguousarray(np.broadcast_to(bias[None, :], (128, H)))
    ident = np.eye(128, dtype=ml_dtypes.bfloat16)
    zeros_slice = np.zeros((PREV, H), dtype=np.float32)
    zeros_tab = np.zeros((PREV, H // 2), dtype=np.float32)
    in_maps = []
    for core in range(8):
        b, q = divmod(core, 4)
        own = X[b, q * TOK:(q + 1) * TOK]
        prev = X[b, (q - 1) * TOK:q * TOK] if q > 0 else zeros_slice
        x0both = np.ascontiguousarray(np.concatenate([prev, own], axis=0))
        r0 = q * TOK
        csq = np.ascontiguousarray(tb["csq"][r0:r0 + TOK])
        ssq = np.ascontiguousarray(tb["ssq"][r0:r0 + TOK])
        if q > 0:
            csk = np.ascontiguousarray(tb["csk"][r0 - PREV:r0 + TOK])
            ssk = np.ascontiguousarray(tb["ssk"][r0 - PREV:r0 + TOK])
            dprev = tb["d_prev_T"]
        else:
            csk = np.ascontiguousarray(
                np.concatenate([zeros_tab, tb["csk"][r0:r0 + TOK]], axis=0))
            ssk = np.ascontiguousarray(
                np.concatenate([zeros_tab, tb["ssk"][r0:r0 + TOK]], axis=0))
            dprev = tb["d_zero_T"]
        in_maps.append({
            "x0both": x0both, "wq": wq, "wk": wk, "wv": wv, "wf8": wf8,
            "wsrow": wsrow, "biasrow": biasrow, "ident": ident,
            "csq": csq, "ssq": ssq, "csk": csk, "ssk": ssk,
            "d_own_T": tb["d_own_T"], "d_prev_T": dprev,
        })
    return in_maps


def kernel(L0_attn_X, L0_attn_W_Q, L0_attn_W_K, L0_attn_W_V,
           L0_ffn_weight, L0_ffn_w_scale, L0_ffn_bias):
    in_maps = _prep_inputs(L0_attn_X, L0_attn_W_Q, L0_attn_W_K, L0_attn_W_V,
                           L0_ffn_weight, L0_ffn_w_scale, L0_ffn_bias)
    nc = build_program()
    res = run_bass_kernel_spmd(nc, in_maps, core_ids=list(range(8)))
    out = np.empty((2, 8192, H), dtype=np.float32)
    for core in range(8):
        b, q = divmod(core, 4)
        out[b, q * TOK:(q + 1) * TOK] = np.asarray(res.results[core]["out"],
                                                   dtype=np.float32)
    return out
